# version 14
# baseline (speedup 1.0000x reference)
from contextlib import ExitStack

import numpy as np
import ml_dtypes
import concourse.bass as bass
import concourse.mybir as mybir
from concourse.bass_utils import run_bass_kernel_spmd
from concourse.bass_core_rust import IndirectOffsetOnAxis

F32 = mybir.dt.float32
BF16 = mybir.dt.bfloat16
I32 = mybir.dt.int32
U32 = mybir.dt.uint32
AF = mybir.ActivationFunctionType
ALU = mybir.AluOpType
AX = mybir.AxisListType

NCORES = 8
TPC = 2048
HALO = 3
D = 1024
DCH = D // 128
DIN = 2048
ICH = DIN // 128
NE = 64
CAP = 512
TOPK = 8
W = TPC + HALO
EPC = 8


class Ctr:
    def __init__(self):
        self.v = 0

    def inc(self, n=1):
        self.v += n
        return self.v


def build_fused():
    nc = bass.Bass("TRN2", target_bir_lowering=False)

    xTh = nc.dram_tensor("xTh", [D, W], F32, kind="ExternalInput").ap()
    w_in = nc.dram_tensor("w_in", [D, 2 * DIN], F32, kind="ExternalInput").ap()
    convw = nc.dram_tensor("convw", [DIN, 4], F32, kind="ExternalInput").ap()
    convb = nc.dram_tensor("convb", [DIN, 1], F32, kind="ExternalInput").ap()
    w_xp = nc.dram_tensor("w_xp", [DIN, 192], F32, kind="ExternalInput").ap()
    w_dt = nc.dram_tensor("w_dt", [64, 128], F32, kind="ExternalInput").ap()
    dt_b = nc.dram_tensor("dt_b", [128, 1], F32, kind="ExternalInput").ap()
    a_log = nc.dram_tensor("a_log", [128, 1], F32, kind="ExternalInput").ap()
    w_gate = nc.dram_tensor("w_gate", [D, NE], F32, kind="ExternalInput").ap()
    w_sgu = nc.dram_tensor("w_sgu", [D, 4096], F32, kind="ExternalInput").ap()
    w_sd = nc.dram_tensor("w_sd", [DIN, D], F32, kind="ExternalInput").ap()
    w_sg = nc.dram_tensor("w_sg", [D, 1], F32, kind="ExternalInput").ap()
    hlidx_in = nc.dram_tensor("hlidx_in", [128, 1], I32, kind="ExternalInput").ap()

    cprojT = nc.dram_tensor("cprojT", [128, DIN], F32, kind="ExternalInput").ap()
    dvec = nc.dram_tensor("dvec", [DIN, 1], F32, kind="ExternalInput").ap()
    wop = nc.dram_tensor("wop", [DIN, D], F32, kind="ExternalInput").ap()
    utri = nc.dram_tensor("utri", [128, 128], F32, kind="ExternalInput").ap()
    iota64 = nc.dram_tensor("iota64", [128, NE], F32, kind="ExternalInput").ap()
    tokid = nc.dram_tensor("tokid", [128, 1], F32, kind="ExternalInput").ap()
    prefvec = nc.dram_tensor("prefvec", [8, 1], F32, kind="ExternalInput").ap()
    oddcol = nc.dram_tensor("oddcol", [128, 1], F32, kind="ExternalInput").ap()

    xfull = nc.dram_tensor("xfull", [16384, D], F32, kind="ExternalInput").ap()
    egu8 = nc.dram_tensor("egu8", [EPC, D, 1024], F32, kind="ExternalInput").ap()
    edw8 = nc.dram_tensor("edw8", [EPC, 512, 1024], F32, kind="ExternalInput").ap()
    ident = nc.dram_tensor("ident", [128, 128], F32, kind="ExternalInput").ap()
    ptidx = nc.dram_tensor("ptidx", [8, 8], I32, kind="ExternalInput").ap()

    xTv = nc.dram_tensor("xTv", [D, TPC], F32, kind="ExternalInput").ap()
    xsidx = nc.dram_tensor("xsidx", [128, DCH], I32, kind="ExternalInput").ap()

    wq = nc.dram_tensor("wq", [D, 1024], F32, kind="ExternalInput").ap()
    wkv = nc.dram_tensor("wkv", [D, 256], F32, kind="ExternalInput").ap()
    wku = nc.dram_tensor("wku", [256, 1024], F32, kind="ExternalInput").ap()
    wvu = nc.dram_tensor("wvu", [256, 1024], F32, kind="ExternalInput").ap()
    wout = nc.dram_tensor("wout", [1024, D], F32, kind="ExternalInput").ap()
    wffn = nc.dram_tensor("wffn", [D, D], F32, kind="ExternalInput").ap()
    cmask = nc.dram_tensor("cmask", [128, 4, 512], F32, kind="ExternalInput").ap()
    evenfl_in = nc.dram_tensor("evenfl_in", [128, 1], F32, kind="ExternalInput").ap()
    oddfl_in = nc.dram_tensor("oddfl_in", [128, 1], F32, kind="ExternalInput").ap()
    killc_in = nc.dram_tensor("killc_in", [128, 1], F32, kind="ExternalInput").ap()

    fin_o = nc.dram_tensor("fin_o", [DCH, 128, TPC], F32, kind="ExternalOutput").ap()

    xc_s = nc.dram_tensor("xc_s", [ICH, 128, TPC], BF16)
    z_s = nc.dram_tensor("z_s", [ICH, 128, TPC], BF16)
    hs_s = nc.dram_tensor("hs_s", [128, TPC], F32)
    acum_s = nc.dram_tensor("acum_s", [128, TPC], F32)
    mask_s = nc.dram_tensor("mask_s", [TPC, NE], F32)
    idx8_s = nc.dram_tensor("idx8_s", [TPC, TOPK], U32)
    w8_s = nc.dram_tensor("w8_s", [TPC, TOPK], F32)
    ssh_s = nc.dram_tensor("ssh_s", [DCH, 128, TPC], F32)
    cnt_i = nc.dram_tensor("cnt_i", [1, NE], F32)
    cnt_all = nc.dram_tensor("cnt_all", [8, NE], F32)
    mo_s = nc.dram_tensor("mo_s", [DCH, 128, TPC], F32)
    pt_my = nc.dram_tensor("pt_my", [NE, 513], F32)
    pt_all = nc.dram_tensor("pt_all", [8 * NE, 513], F32)
    gi_s = nc.dram_tensor("gi_s", [TPC, TOPK], F32)
    wk_s = nc.dram_tensor("wk_s", [TPC, TOPK], F32)
    oe_my = nc.dram_tensor("oe_my", [EPC * 513, 1024], BF16)
    oe_all = nc.dram_tensor("oe_all", [NE * 513, 1024], BF16)
    ym_s = nc.dram_tensor("ym_s", [TPC, 1024], F32)
    x1_s = nc.dram_tensor("x1_s", [DCH, 128, TPC], F32)
    xn1_mine = nc.dram_tensor("xn1_mine", [DCH, 128, TPC], BF16)
    wsgn_scr = nc.dram_tensor("wsgn_scr", [D, D], BF16)
    bar_i = nc.dram_tensor("bar_i", [1, 4], F32)
    bar_o = nc.dram_tensor("bar_o", [8, 4], F32)

    hl_sh = nc.dram_tensor("hl_sh", [256, 1], F32, addr_space="Shared")
    xn1_sh = nc.dram_tensor("xn1_sh", [2048, TPC], BF16, addr_space="Shared")

    wsel = w_in.rearrange("(c p) f -> c p f", p=128)
    wsgu = w_sgu.rearrange("(c p) f -> c p f", p=128)
    wsdr = w_sd.rearrange("(c p) f -> c p f", p=128)
    pt_flat = pt_my.ap().rearrange("a (b c) -> (a b) c", c=1)

    with ExitStack() as top:
        ldx = top.enter_context(nc.semaphore("ldx"))
        ldw = top.enter_context(nc.semaphore("ldw"))
        mm = top.enter_context(nc.semaphore("mm"))
        vx = top.enter_context(nc.semaphore("vx"))
        sx = top.enter_context(nc.semaphore("sx"))
        st = top.enter_context(nc.semaphore("st"))
        cc0 = top.enter_context(nc.semaphore("cc0"))
        cc1 = top.enter_context(nc.semaphore("cc1"))
        cc2 = top.enter_context(nc.semaphore("cc2"))
        cc3 = top.enter_context(nc.semaphore("cc3"))
        block = top.enter_context(nc.Block())
        cx, cw_, cm, cv, cs, ct = Ctr(), Ctr(), Ctr(), Ctr(), Ctr(), Ctr()

        def barrier():
            vals = [(ldx, cx), (ldw, cw_), (mm, cm), (vx, cv), (sx, cs), (st, ct)]
            for sec in (block.sync, block.tensor, block.vector, block.scalar, block.gpsimd):
                @sec
                def _(e):
                    for s_, c_ in vals:
                        e.wait_ge(s_, c_.v)

        with ExitStack() as _ctx:
            xs = _ctx.enter_context(nc.sbuf_tensor("xs", [128, DCH, 512], F32))
            sqf = _ctx.enter_context(nc.sbuf_tensor("sqf", [128, DCH, 512], F32))
            xn = _ctx.enter_context(nc.sbuf_tensor("xn", [128, DCH, W], BF16))
            rows = _ctx.enter_context(nc.sbuf_tensor("rows", [1, 8, 512], F32))
            u3 = _ctx.enter_context(nc.sbuf_tensor("u3", [128, 4, 512], F32))
            bR = u3[:, 0, :]
            bM = u3[:, 1, :]
            ones = _ctx.enter_context(nc.sbuf_tensor("ones", [128, 1], F32))
            epst = _ctx.enter_context(nc.sbuf_tensor("epst", [1, 1], F32))
            onerow = _ctx.enter_context(nc.sbuf_tensor("onerow", [1, 128], F32))
            xin = _ctx.enter_context(nc.sbuf_tensor("xin", [128, ICH, W], BF16))
            xcf = _ctx.enter_context(nc.sbuf_tensor("xcf", [128, 2, 512], F32))
            xcb = _ctx.enter_context(nc.sbuf_tensor("xcb", [128, ICH, 512], BF16))
            zb = _ctx.enter_context(nc.sbuf_tensor("zb", [128, 2, 512], BF16))
            wflat = _ctx.enter_context(nc.sbuf_tensor("wflat", [128, 4096], BF16))
            wbuf = wflat[:].rearrange("p (a b c) -> p a b c", a=DCH, b=4, c=128)
            wdbuf = wflat[:].rearrange("p (a b c) -> p a b c", a=ICH, b=2, c=128)
            wxpb = _ctx.enter_context(nc.sbuf_tensor("wxpb", [128, ICH, 192], BF16))
            wdtb = _ctx.enter_context(nc.sbuf_tensor("wdtb", [64, 128], BF16))
            cwb = _ctx.enter_context(nc.sbuf_tensor("cwb", [128, ICH, 4], F32))
            cbb = _ctx.enter_context(nc.sbuf_tensor("cbb", [128, ICH], F32))
            dtbt = _ctx.enter_context(nc.sbuf_tensor("dtbt", [128, 1], F32))
            acol = _ctx.enter_context(nc.sbuf_tensor("acol", [128, 1], F32))
            ralog = _ctx.enter_context(nc.sbuf_tensor("ralog", [128, 1], F32))
            recipa = _ctx.enter_context(nc.sbuf_tensor("recipa", [128, 1], F32))
            sc1 = _ctx.enter_context(nc.sbuf_tensor("sc1", [128, 1], F32))
            sc2 = _ctx.enter_context(nc.sbuf_tensor("sc2", [128, 1], F32))
            sc3 = _ctx.enter_context(nc.sbuf_tensor("sc3", [128, 1], F32))
            hlix = _ctx.enter_context(nc.sbuf_tensor("hlix", [128, 1], I32))
            dt_bf = _ctx.enter_context(nc.sbuf_tensor("dt_bf", [64, 512], BF16))
            bmf = _ctx.enter_context(nc.sbuf_tensor("bmf", [128, 512], F32))
            t1 = _ctx.enter_context(nc.sbuf_tensor("t1", [128, 512], F32))
            t2 = _ctx.enter_context(nc.sbuf_tensor("t2", [128, 512], F32))
            t3 = _ctx.enter_context(nc.sbuf_tensor("t3", [128, 512], F32))
            Ab = _ctx.enter_context(nc.sbuf_tensor("Ab", [128, 4, 512], F32))
            Bb = u3
            hsb = xs[:].rearrange("p a b -> p (a b)")[:, 0:TPC]
            acb = sqf[:].rearrange("p a b -> p (a b)")
            wgb = _ctx.enter_context(nc.sbuf_tensor("wgb", [128, DCH, NE], BF16))
            expt = t2[:, 0:64]
            mrt = t2[:, 64:128]
            maskt = t2[:, 128:192]
            m8 = t2[:, 192:200]
            m8b = t2[:, 200:208]
            i8 = _ctx.enter_context(nc.sbuf_tensor("i8", [128, 8], U32))
            rsum = t2[:, 216:217]
            rrec = t2[:, 217:218]
            w8t = t2[:, 208:216]
            nmax = t2[:, 218:219]
            wsgb = _ctx.enter_context(nc.sbuf_tensor("wsgb", [128, DCH], BF16))
            sgb = zb[:, 0, :]
            sgate = u3[:, 0, :]
            sshb = u3[:, 1:3, :]
            pA = _ctx.enter_context(nc.psum_tensor("pA", [128, 512], F32))
            pB = _ctx.enter_context(nc.psum_tensor("pB", [128, 512], F32))
            pC = _ctx.enter_context(nc.psum_tensor("pC", [128, 512], F32))
            pD = _ctx.enter_context(nc.psum_tensor("pD", [128, 512], F32))
            pR1 = _ctx.enter_context(nc.psum_tensor("pR1", [128, 512], F32))
            pR2 = _ctx.enter_context(nc.psum_tensor("pR2", [128, 512], F32))
            pBC = _ctx.enter_context(nc.psum_tensor("pBC", [128, 512], F32))
            pS = _ctx.enter_context(nc.psum_tensor("pS", [128, 512], F32))

            @block.gpsimd
            def _(g):
                for kc in range(ICH):
                    g.dma_start(out=wxpb[:, kc, :], in_=w_xp[kc * 128:(kc + 1) * 128, :]).then_inc(ldw, 16)
                    g.dma_start(out=cwb[:, kc, :], in_=convw[kc * 128:(kc + 1) * 128, :]).then_inc(ldw, 16)
                    g.dma_start(out=cbb[:, kc:kc + 1], in_=convb[kc * 128:(kc + 1) * 128, :]).then_inc(ldw, 16)
                for kc in range(DCH):
                    g.dma_start(out=wgb[:, kc, :], in_=w_gate[kc * 128:(kc + 1) * 128, :]).then_inc(ldw, 16)
                    g.dma_start(out=wsgb[:, kc:kc + 1], in_=w_sg[kc * 128:(kc + 1) * 128, :]).then_inc(ldw, 16)
                g.dma_start(out=wdtb[:], in_=w_dt).then_inc(ldw, 16)
                g.dma_start(out=dtbt[:], in_=dt_b).then_inc(ldw, 16)
                g.dma_start(out=ralog[:], in_=a_log).then_inc(ldw, 16)
                g.dma_start(out=hlix[:], in_=hlidx_in).then_inc(ldw, 16)
            cw_.inc((ICH * 3 + DCH * 2 + 4) * 16)

            @block.vector
            def _(v):
                v.memset(ones[:], 1.0).then_inc(vx, 1)
                v.memset(onerow[:], 1.0).then_inc(vx, 1)
                v.memset(epst[:], 1e-5).then_inc(vx, 1)
            cv.inc(3)

            @block.scalar
            def _(s):
                s.wait_ge(ldw, cw_.v)
                s.activation(sc1[:], ralog[:], AF.Exp, scale=-1.0).then_inc(sx, 1)
                s.activation(sc2[:], ralog[:], AF.Exp).then_inc(sx, 1)
            cs.inc(2)

            @block.vector
            def _(v):
                v.wait_ge(sx, cs.v)
                v.wait_ge(vx, cv.v)
                v.tensor_scalar_mul(acol[:], sc2[:], -1.0).then_inc(vx, 1)
                v.tensor_single_scalar(sc3[:], sc2[:], 1e-8, ALU.is_gt).then_inc(vx, 1)
                v.wait_ge(vx, cv.v + 2)
                v.tensor_scalar_mul(sc2[:], sc1[:], -1.0).then_inc(vx, 1)
                v.wait_ge(vx, cv.v + 3)
                v.tensor_mul(sc1[:], sc2[:], sc3[:]).then_inc(vx, 1)
                v.tensor_sub(sc3[:], ones[:], sc3[:]).then_inc(vx, 1)
                v.wait_ge(vx, cv.v + 5)
                v.tensor_add(recipa[:], sc1[:], sc3[:]).then_inc(vx, 1)
            cv.inc(6)

            groups = [(k * 512, 512) for k in range(4)] + [(2048, HALO)]
            for g0, gw in groups:
                @block.sync
                def _(e, g0=g0, gw=gw):
                    e.wait_ge(vx, cv.v)
                    for kc in range(DCH):
                        e.dma_start(out=xs[:, kc, 0:gw], in_=xTh[kc * 128:(kc + 1) * 128, g0:g0 + gw]).then_inc(ldx, 16)
                cx.inc(DCH * 16)

                @block.tensor
                def _(t, gw=gw):
                    t.wait_ge(ldx, cx.v)
                    for kc in range(DCH):
                        mi = t.matmul(pR1[0:1, 0:gw], ones[:], xs[:, kc, 0:gw], start=(kc == 0), stop=(kc == DCH - 1))
                    mi.then_inc(mm, 1)
                cm.inc()

                @block.scalar
                def _(s, gw=gw):
                    s.wait_ge(ldx, cx.v)
                    s.wait_ge(vx, cv.v)
                    for kc in range(DCH):
                        s.activation(sqf[:, kc, 0:gw], xs[:, kc, 0:gw], AF.Square).then_inc(sx, 1)
                cs.inc(DCH)

                @block.tensor
                def _(t, gw=gw):
                    t.wait_ge(sx, cs.v)
                    for kc in range(DCH):
                        mi = t.matmul(pR2[0:1, 0:gw], ones[:], sqf[:, kc, 0:gw], start=(kc == 0), stop=(kc == DCH - 1))
                    mi.then_inc(mm, 1)
                cm.inc()

                @block.vector
                def _(v, g0=g0, gw=gw):
                    v.wait_ge(mm, cm.v)
                    v.tensor_scalar_mul(rows[0:1, 0, 0:gw], pR1[0:1, 0:gw], 1.0 / D).then_inc(vx, 1)
                    v.tensor_scalar_mul(rows[0:1, 1, 0:gw], pR2[0:1, 0:gw], 1.0 / D).then_inc(vx, 1)
                    v.wait_ge(vx, cv.v + 2)
                    v.tensor_mul(rows[0:1, 5, 0:gw], rows[0:1, 0, 0:gw],
                                 rows[0:1, 0, 0:gw]).then_inc(vx, 1)
                    v.wait_ge(vx, cv.v + 3)
                    v.tensor_sub(rows[0:1, 1, 0:gw], rows[0:1, 1, 0:gw],
                                 rows[0:1, 5, 0:gw]).then_inc(vx, 1)
                cv.inc(4)

                @block.scalar
                def _(s, g0=g0, gw=gw):
                    s.wait_ge(vx, cv.v)
                    s.activation(rows[0:1, 6, 0:gw], rows[0:1, 1, 0:gw], AF.Ln, bias=epst[:]).then_inc(sx, 1)
                    s.wait_ge(sx, cs.v + 1)
                    s.activation(rows[0:1, 2, 0:gw], rows[0:1, 6, 0:gw], AF.Exp, scale=-0.5).then_inc(sx, 1)
                cs.inc(2)

                @block.vector
                def _(v, g0=g0, gw=gw):
                    v.wait_ge(sx, cs.v)
                    v.tensor_mul(rows[0:1, 3, 0:gw], rows[0:1, 0, 0:gw],
                                 rows[0:1, 2, 0:gw]).then_inc(vx, 1)
                cv.inc()

                @block.tensor
                def _(t, g0=g0, gw=gw):
                    t.wait_ge(vx, cv.v)
                    mi = t.matmul(pBC[:, 0:gw], onerow[:], rows[0:1, 2, 0:gw], start=True, stop=True)
                    mi = t.matmul(pS[:, 0:gw], onerow[:], rows[0:1, 3, 0:gw], start=True, stop=True)
                    mi.then_inc(mm, 1)
                cm.inc()

                @block.vector
                def _(v, g0=g0, gw=gw):
                    v.wait_ge(mm, cm.v)
                    v.tensor_copy(bR[:, 0:gw], pBC[:, 0:gw]).then_inc(vx, 1)
                    v.tensor_copy(bM[:, 0:gw], pS[:, 0:gw]).then_inc(vx, 1)
                    v.wait_ge(vx, cv.v + 2)
                    for kc in range(DCH):
                        v.tensor_mul(sqf[:, kc, 0:gw], xs[:, kc, 0:gw], bR[:, 0:gw]).then_inc(vx, 1)
                    v.wait_ge(vx, cv.v + 2 + DCH)
                    for kc in range(DCH):
                        v.tensor_sub(xn[:, kc, g0:g0 + gw], sqf[:, kc, 0:gw], bM[:, 0:gw]).then_inc(vx, 1)
                cv.inc(2 + 2 * DCH)

            def stream_w(g, src_chunks, slot, nchunks=DCH):
                for kc in range(nchunks):
                    g.dma_start(out=wbuf[:, kc, slot, :], in_=src_chunks(kc)).then_inc(ldw, 16)

            mm_at_wslot = {0: cm.v, 1: cm.v, 2: cm.v, 3: cm.v}
            banksA = [None, None, None, None]
            evacA = {0: cv.v, 1: cv.v, 2: cv.v, 3: cv.v}
            gctr = [0]
            for fi in range(16):
                slot = fi % 2

                @block.gpsimd
                def _(g, fi=fi, slot=slot):
                    g.wait_ge(mm, mm_at_wslot[slot])
                    stream_w(g, lambda kc, fi=fi: wsel[kc, :, fi * 128:(fi + 1) * 128], slot)
                cw_.inc(DCH * 16)

                for gi, (g0, gw) in enumerate(groups):
                    bank = gctr[0] % 4
                    gctr[0] += 1
                    ps = (pA, pB, pC, pD)[bank]

                    @block.tensor
                    def _(t, fi=fi, slot=slot, g0=g0, gw=gw, ps=ps, first=(gi == 0), w_ev=evacA[bank]):
                        if first:
                            t.wait_ge(ldw, cw_.v)
                        t.wait_ge(vx, w_ev)
                        for kc in range(DCH):
                            mi = t.matmul(ps[:, 0:gw], wbuf[:, kc, slot, :], xn[:, kc, g0:g0 + gw],
                                     start=(kc == 0), stop=(kc == DCH - 1))
                        mi.then_inc(mm, 1)
                    cm.inc()
                    mm_at_wslot[slot] = cm.v

                    @block.vector
                    def _(v, fi=fi, g0=g0, gw=gw, ps=ps):
                        v.wait_ge(mm, cm.v)
                        v.tensor_copy(xin[:, fi, g0:g0 + gw], ps[:, 0:gw]).then_inc(vx, 1)
                    cv.inc()
                    evacA[bank] = cv.v

            zgroups = [(HALO + k * 512, 512) for k in range(4)]
            for fi in range(16, 32):
                slot = fi % 2

                @block.gpsimd
                def _(g, fi=fi, slot=slot):
                    g.wait_ge(mm, mm_at_wslot[slot])
                    stream_w(g, lambda kc, fi=fi: wsel[kc, :, fi * 128:(fi + 1) * 128], slot)
                cw_.inc(DCH * 16)

                for gi, (g0, gw) in enumerate(zgroups):
                    bank = gctr[0] % 4
                    gctr[0] += 1
                    ps = (pA, pB, pC, pD)[bank]
                    zslot = gi % 2

                    @block.tensor
                    def _(t, slot=slot, g0=g0, gw=gw, ps=ps, first=(gi == 0), w_ev=evacA[bank]):
                        if first:
                            t.wait_ge(ldw, cw_.v)
                        t.wait_ge(st, ct.v)
                        t.wait_ge(vx, w_ev)
                        for kc in range(DCH):
                            mi = t.matmul(ps[:, 0:gw], wbuf[:, kc, slot, :], xn[:, kc, g0:g0 + gw],
                                     start=(kc == 0), stop=(kc == DCH - 1))
                        mi.then_inc(mm, 1)
                    cm.inc()
                    mm_at_wslot[slot] = cm.v

                    @block.vector
                    def _(v, g0=g0, gw=gw, ps=ps, zslot=zslot):
                        v.wait_ge(mm, cm.v)
                        v.tensor_copy(zb[:, zslot, :], ps[:, 0:gw]).then_inc(vx, 1)
                    cv.inc()
                    evacA[bank] = cv.v

                    @block.scalar
                    def _(s, fi=fi, g0=g0, zslot=zslot):
                        s.wait_ge(vx, cv.v)
                        s.dma_start(out=z_s.ap()[fi - 16, :, g0 - HALO:g0 - HALO + 512],
                                    in_=zb[:, zslot, :]).then_inc(st, 16)
                        s.nop().then_inc(sx, 1)
                    cs.inc()
                    ct.inc(16)

            for blk in range(4):
                base = blk * 512

                for cc in range(ICH):
                    pp = cc % 2

                    @block.vector
                    def _(v, cc=cc, pp=pp, base=base, blk=blk):
                        if blk > 0 and cc == 0:
                            v.wait_ge(mm, cm.v)
                            v.wait_ge(st, ct.v)
                        v.tensor_scalar(xcf[:, pp, :], xin[:, cc, base + 3:base + 515],
                                        cwb[:, cc, 3:4], cbb[:, cc:cc + 1], ALU.mult, ALU.add).then_inc(vx, 1)
                        v.wait_ge(vx, cv.v + 1)
                        v.scalar_tensor_tensor(xcf[:, pp, :], xin[:, cc, base + 2:base + 514],
                                               cwb[:, cc, 2:3], xcf[:, pp, :], ALU.mult, ALU.add).then_inc(vx, 1)
                        v.wait_ge(vx, cv.v + 2)
                        v.scalar_tensor_tensor(xcf[:, pp, :], xin[:, cc, base + 1:base + 513],
                                               cwb[:, cc, 1:2], xcf[:, pp, :], ALU.mult, ALU.add).then_inc(vx, 1)
                        v.wait_ge(vx, cv.v + 3)
                        v.scalar_tensor_tensor(xcf[:, pp, :], xin[:, cc, base:base + 512],
                                               cwb[:, cc, 0:1], xcf[:, pp, :], ALU.mult, ALU.add).then_inc(vx, 1)
                        v.wait_ge(vx, cv.v + 4)
                        v.tensor_copy(xcb[:, cc, :], xcf[:, pp, :]).then_inc(vx, 1)
                    cv.inc(5)

                @block.scalar
                def _(s, blk=blk):
                    s.wait_ge(vx, cv.v)
                    for cc in range(ICH):
                        s.dma_start(out=xc_s.ap()[cc, :, blk * 512:blk * 512 + 512], in_=xcb[:, cc, :]).then_inc(st, 16)
                    s.nop().then_inc(sx, 1)
                cs.inc()
                ct.inc(ICH * 16)

                @block.tensor
                def _(t, blk=blk):
                    t.wait_ge(vx, cv.v)
                    for kc in range(ICH):
                        mi = t.matmul(pC[0:64, 0:512], wxpb[:, kc, 0:64], xcb[:, kc, :],
                                 start=(kc == 0), stop=(kc == ICH - 1))
                    mi.then_inc(mm, 1)
                    for kc in range(ICH):
                        mi = t.matmul(pD[:, 0:512], wxpb[:, kc, 64:192], xcb[:, kc, :],
                                 start=(kc == 0), stop=(kc == ICH - 1))
                    mi.then_inc(mm, 1)
                cm.inc(2)

                @block.vector
                def _(v, blk=blk):
                    v.wait_ge(mm, cm.v)
                    v.tensor_copy(dt_bf[:, :], pC[0:64, 0:512]).then_inc(vx, 1)
                    v.tensor_copy(bmf[:, :], pD[:, 0:512]).then_inc(vx, 1)
                cv.inc(2)

                @block.tensor
                def _(t, blk=blk):
                    t.wait_ge(vx, cv.v)
                    mi = t.matmul(pC[:, 0:512], wdtb[:], dt_bf[:, :], start=True, stop=True)
                    mi.then_inc(mm, 1)
                cm.inc()

                @block.scalar
                def _(s, blk=blk):
                    s.wait_ge(mm, cm.v)
                    s.activation(t1[:, :], pC[:, 0:512], AF.Identity, bias=dtbt[:]).then_inc(sx, 1)
                    s.wait_ge(sx, cs.v + 1)
                    s.activation(t2[:, :], t1[:, :], AF.Abs).then_inc(sx, 1)
                    s.wait_ge(sx, cs.v + 2)
                    s.activation(t3[:, :], t2[:, :], AF.Exp, scale=-1.0).then_inc(sx, 1)
                    s.wait_ge(sx, cs.v + 3)
                    s.activation(t2[:, :], t3[:, :], AF.Ln, bias=1.0).then_inc(sx, 1)
                cs.inc(4)

                @block.vector
                def _(v, blk=blk):
                    v.wait_ge(sx, cs.v)
                    v.tensor_scalar_max(t3[:, :], t1[:, :], 0.0).then_inc(vx, 1)
                    v.wait_ge(vx, cv.v + 1)
                    v.tensor_add(t1[:, :], t2[:, :], t3[:, :]).then_inc(vx, 1)
                    v.wait_ge(vx, cv.v + 2)
                    v.tensor_scalar(t1[:, :], t1[:, :], 0.001, 0.1, ALU.max, ALU.min).then_inc(vx, 1)
                    v.wait_ge(vx, cv.v + 3)
                    v.tensor_scalar(t2[:, :], t1[:, :], acol[:], None, ALU.mult).then_inc(vx, 1)
                cv.inc(4)

                @block.scalar
                def _(s, blk=blk):
                    s.wait_ge(vx, cv.v)
                    s.activation(Ab[:, blk, :], t2[:, :], AF.Exp).then_inc(sx, 1)
                cs.inc()

                @block.vector
                def _(v, blk=blk):
                    v.wait_ge(sx, cs.v)
                    v.tensor_scalar(t2[:, :], Ab[:, blk, :], 1.0, recipa[:], ALU.subtract, ALU.mult).then_inc(vx, 1)
                    v.wait_ge(vx, cv.v + 1)
                    v.tensor_mul(Bb[:, blk, :], t2[:, :], bmf[:, :]).then_inc(vx, 1)
                cv.inc(2)

            @block.vector
            def _(v):
                v.wait_ge(vx, cv.v)
                v.memset(t1[:], 0.0).then_inc(vx, 1)
                v.wait_ge(vx, cv.v + 1)
                v.tensor_tensor_scan(hsb, Ab[:].rearrange("p a b -> p (a b)"),
                                     Bb[:].rearrange("p a b -> p (a b)"), 0.0, ALU.mult, ALU.add).then_inc(vx, 1)
                for blkz in range(4):
                    v.wait_ge(vx, cv.v + 2 + blkz)
                    v.tensor_tensor_scan(acb[:, blkz * 512:(blkz + 1) * 512], Ab[:, blkz, :],
                                         t1[:] if blkz == 0 else t1[:],
                                         1.0 if blkz == 0 else acb[:, blkz * 512 - 1:blkz * 512],
                                         ALU.mult, ALU.add).then_inc(vx, 1)
            cv.inc(6)

            @block.scalar
            def _(s):
                s.wait_ge(vx, cv.v)
                s.dma_start(out=hs_s.ap(), in_=hsb).then_inc(st, 16)
                s.dma_start(out=acum_s.ap(), in_=acb[:, 0:TPC]).then_inc(st, 16)
            ct.inc(32)

            @block.gpsimd
            def _(g):
                g.wait_ge(vx, cv.v)
                g.indirect_dma_start(
                    out=hl_sh.ap(), out_offset=IndirectOffsetOnAxis(hlix[:, 0:1], 0),
                    in_=hsb[:, TPC - 1:TPC], in_offset=None,
                ).then_inc(st, 16)
            ct.inc(16)

            for ti in range(TPC // 128):
                col = HALO + ti * 128

                @block.tensor
                def _(t, col=col):
                    t.wait_ge(st, ct.v)
                    for kc in range(DCH):
                        mi = t.matmul(pD[:, 0:NE], xn[:, kc, col:col + 128], wgb[:, kc, :],
                                 start=(kc == 0), stop=(kc == DCH - 1))
                    mi.then_inc(mm, 1)
                cm.inc()

                @block.vector
                def _(v):
                    v.wait_ge(mm, cm.v)
                    v.reduce_max(nmax, pD[:, 0:NE], axis=AX.X, negate=True).then_inc(vx, 1)
                cv.inc()

                @block.scalar
                def _(s):
                    s.wait_ge(vx, cv.v)
                    s.activation(expt, pD[:, 0:NE], AF.Exp, bias=nmax).then_inc(sx, 1)
                cs.inc()

                @block.vector
                def _(v):
                    v.wait_ge(sx, cs.v)
                    v.max(m8, expt).then_inc(vx, 1)
                cv.inc()

                @block.scalar
                def _(s):
                    s.wait_ge(vx, cv.v)
                    s.copy(m8b, m8).then_inc(sx, 1)
                cs.inc()

                @block.vector
                def _(v):
                    v.wait_ge(sx, cs.v)
                    v.match_replace(mrt, m8b, expt, 0.0).then_inc(vx, 1)
                    v.max_index(i8[:], m8b, expt).then_inc(vx, 1)
                    v.reduce_sum(rsum, m8, axis=AX.X).then_inc(vx, 1)
                    v.wait_ge(vx, cv.v + 3)
                    v.tensor_tensor(maskt, expt, mrt, ALU.not_equal).then_inc(vx, 1)
                    v.reciprocal(rrec, rsum).then_inc(vx, 1)
                    v.wait_ge(vx, cv.v + 5)
                    v.tensor_scalar(w8t, m8, rrec, None, ALU.mult).then_inc(vx, 1)
                cv.inc(6)

                @block.tensor
                def _(t, ti=ti):
                    t.wait_ge(vx, cv.v)
                    mi = t.matmul(pR1[0:1, 0:NE], ones[:], maskt, start=(ti == 0), stop=(ti == 15))
                    mi.then_inc(mm, 1)
                cm.inc()

                @block.scalar
                def _(s, ti=ti):
                    s.wait_ge(vx, cv.v)
                    s.dma_start(out=mask_s.ap()[ti * 128:(ti + 1) * 128, :], in_=maskt).then_inc(st, 16)
                    s.dma_start(out=idx8_s.ap()[ti * 128:(ti + 1) * 128, :], in_=i8[:]).then_inc(st, 16)
                    s.dma_start(out=w8_s.ap()[ti * 128:(ti + 1) * 128, :], in_=w8t).then_inc(st, 16)
                    s.nop().then_inc(sx, 1)
                cs.inc()
                ct.inc(48)

            @block.vector
            def _(v):
                v.wait_ge(mm, cm.v)
                v.tensor_copy(rows[0:1, 0, 0:NE], pR1[0:1, 0:NE]).then_inc(vx, 1)
            cv.inc()

            @block.scalar
            def _(s):
                s.wait_ge(vx, cv.v)
                s.dma_start(out=cnt_i.ap(), in_=rows[0:1, 0, 0:NE]).then_inc(st, 16)
            ct.inc(16)

            act = xin
            wslot_mm = {0: cm.v, 1: cm.v}
            for fi in range(16):
                sl = (fi % 2) * 2

                @block.gpsimd
                def _(g, fi=fi, sl=sl):
                    g.wait_ge(mm, wslot_mm[fi % 2])
                    for kc in range(DCH):
                        g.dma_start(out=wbuf[:, kc, sl, :],
                                    in_=wsgu[kc, :, fi * 128:(fi + 1) * 128]).then_inc(ldw, 16)
                        g.dma_start(out=wbuf[:, kc, sl + 1, :],
                                    in_=wsgu[kc, :, 2048 + fi * 128:2048 + (fi + 1) * 128]).then_inc(ldw, 16)
                cw_.inc(2 * DCH * 16)

                for blk in range(4):
                    col = HALO + blk * 512

                    @block.tensor
                    def _(t, fi=fi, sl=sl, col=col, first=(blk == 0)):
                        if first:
                            t.wait_ge(ldw, cw_.v)
                        t.wait_ge(vx, cv.v)
                        for kc in range(DCH):
                            mi = t.matmul(pA[:, 0:512], wbuf[:, kc, sl, :], xn[:, kc, col:col + 512],
                                     start=(kc == 0), stop=(kc == DCH - 1))
                        mi.then_inc(mm, 1)
                        for kc in range(DCH):
                            mi = t.matmul(pB[:, 0:512], wbuf[:, kc, sl + 1, :], xn[:, kc, col:col + 512],
                                     start=(kc == 0), stop=(kc == DCH - 1))
                        mi.then_inc(mm, 1)
                    cm.inc(2)

                    @block.scalar
                    def _(s):
                        s.wait_ge(mm, cm.v)
                        s.activation(sgb, pA[:, 0:512], AF.Silu).then_inc(sx, 1)
                    cs.inc()

                    @block.vector
                    def _(v, fi=fi, blk=blk):
                        v.wait_ge(sx, cs.v)
                        v.tensor_mul(act[:, fi, blk * 512:(blk + 1) * 512], sgb, pB[:, 0:512]).then_inc(vx, 1)
                    cv.inc()
                wslot_mm[fi % 2] = cm.v

            @block.tensor
            def _(t):
                t.wait_ge(vx, cv.v)
                for blk in range(4):
                    col = HALO + blk * 512
                    for kc in range(DCH):
                        mi = t.matmul(pC[0:1, 0:512], wsgb[:, kc:kc + 1], xn[:, kc, col:col + 512],
                                 start=(kc == 0), stop=(kc == DCH - 1))
                    mi.then_inc(mm, 1)
                    t.wait_ge(sx, cs.v + blk + 1)
            cm.inc(4)

            for blk in range(4):
                @block.scalar
                def _(s, blk=blk):
                    s.wait_ge(mm, cm.v - 4 + blk + 1)
                    s.activation(rows[0:1, 4 + blk, 0:512], pC[0:1, 0:512], AF.Sigmoid).then_inc(sx, 1)
                cs.inc()

            wd_mm = {0: cm.v, 1: cm.v}
            for f2 in range(DCH):
                sl = f2 % 2

                @block.gpsimd
                def _(g, f2=f2, sl=sl):
                    g.wait_ge(mm, wd_mm[sl])
                    for kc in range(ICH):
                        g.dma_start(out=wdbuf[:, kc, sl, :],
                                    in_=wsdr[kc, :, f2 * 128:(f2 + 1) * 128]).then_inc(ldw, 16)
                cw_.inc(ICH * 16)

                for blk in range(4):
                    ps = pA if blk % 2 == 0 else pB
                    shs = blk % 2

                    @block.tensor
                    def _(t, f2=f2, sl=sl, blk=blk, ps=ps, first=(blk == 0)):
                        if first:
                            t.wait_ge(ldw, cw_.v)
                        t.wait_ge(vx, cv.v)
                        for kc in range(ICH):
                            mi = t.matmul(ps[:, 0:512], wdbuf[:, kc, sl, :], act[:, kc, blk * 512:(blk + 1) * 512],
                                     start=(kc == 0), stop=(kc == ICH - 1))
                        mi.then_inc(mm, 1)
                    cm.inc()
                    wd_mm[sl] = cm.v

                    @block.tensor
                    def _(t, blk=blk):
                        t.wait_ge(sx, cs.v)
                        mi = t.matmul(pBC[:, 0:512], onerow[:], rows[0:1, 4 + blk, 0:512],
                                 start=True, stop=True)
                        mi.then_inc(mm, 1)
                    cm.inc()

                    @block.vector
                    def _(v, blk=blk, ps=ps, shs=shs):
                        v.wait_ge(mm, cm.v)
                        v.wait_ge(st, ct.v)
                        v.tensor_copy(sgate, pBC[:, 0:512]).then_inc(vx, 1)
                        v.wait_ge(vx, cv.v + 1)
                        v.tensor_mul(sshb[:, shs, :], ps[:, 0:512], sgate).then_inc(vx, 1)
                    cv.inc(2)

                    @block.scalar
                    def _(s, f2=f2, blk=blk, shs=shs):
                        s.wait_ge(vx, cv.v)
                        s.dma_start(out=ssh_s.ap()[f2, :, blk * 512:(blk + 1) * 512],
                                    in_=sshb[:, shs, :]).then_inc(st, 16)
                        s.nop().then_inc(sx, 1)
                    cs.inc()
                    ct.inc(16)

        @block.gpsimd
        def _(g):
            g.wait_ge(st, ct.v)
            g.collective_compute(
                "AllGather", ALU.bypass, replica_groups=[list(range(8))],
                ins=[cnt_i.ap().opt()], outs=[cnt_all.ap().opt()],
            ).then_inc(cc0)
        barrier()

        with ExitStack() as _ctx:
            hsf = _ctx.enter_context(nc.sbuf_tensor("hsf", [128, TPC], F32))
            hsbf = _ctx.enter_context(nc.sbuf_tensor("hsbf", [128, TPC], BF16))
            acu = _ctx.enter_context(nc.sbuf_tensor("acu", [128, TPC], F32))
            hmid = _ctx.enter_context(nc.sbuf_tensor("hmid", [128, 1], F32))
            oddsb = _ctx.enter_context(nc.sbuf_tensor("oddsb", [128, 1], F32))
            cp_sb = _ctx.enter_context(nc.sbuf_tensor("cp_sb", [128, ICH, 128], BF16))
            wop_sb = _ctx.enter_context(nc.sbuf_tensor("wop_sb", [128, ICH, D], BF16))
            dsb = _ctx.enter_context(nc.sbuf_tensor("dsb", [128, ICH], F32))
            xcs = _ctx.enter_context(nc.sbuf_tensor("xcs", [128, ICH, 512], BF16))
            zsil = _ctx.enter_context(nc.sbuf_tensor("zsil", [128, ICH, 512], BF16))
            zld = _ctx.enter_context(nc.sbuf_tensor("zld", [128, ICH, 512], BF16))
            ymbf = _ctx.enter_context(nc.sbuf_tensor("ymbf", [128, ICH, 512], BF16))
            yf = _ctx.enter_context(nc.sbuf_tensor("yf", [128, 2, 512], F32))
            mosb = _ctx.enter_context(nc.sbuf_tensor("mosb", [128, 2, 512], F32))
            ones = _ctx.enter_context(nc.sbuf_tensor("onesB", [128, 1], F32))
            onerow = _ctx.enter_context(nc.sbuf_tensor("onerowB", [1, 128], F32))
            utri_sb = _ctx.enter_context(nc.sbuf_tensor("utri_sb", [128, 128], F32))
            io64 = _ctx.enter_context(nc.sbuf_tensor("io64", [128, NE], F32))
            tok0 = _ctx.enter_context(nc.sbuf_tensor("tok0", [128, 1], F32))
            tokc = _ctx.enter_context(nc.sbuf_tensor("tokc", [128, 1], F32))
            cntall_sb = _ctx.enter_context(nc.sbuf_tensor("cntall_sb", [8, NE], F32))
            pref_sb = _ctx.enter_context(nc.sbuf_tensor("pref_sb", [8, 1], F32))
            mown = _ctx.enter_context(nc.sbuf_tensor("mown", [128, NE], F32))
            offs = _ctx.enter_context(nc.sbuf_tensor("offs", [1, NE], F32))
            pos_sb = _ctx.enter_context(nc.sbuf_tensor("pos_sb", [128, NE], F32))
            oh8 = _ctx.enter_context(nc.sbuf_tensor("oh8", [128, TOPK, NE], F32))
            tm8 = _ctx.enter_context(nc.sbuf_tensor("tm8", [128, TOPK, NE], F32))
            idx8u = _ctx.enter_context(nc.sbuf_tensor("idx8u", [128, TOPK], U32))
            idx8f = _ctx.enter_context(nc.sbuf_tensor("idx8f", [128, TOPK], F32))
            w8s = _ctx.enter_context(nc.sbuf_tensor("w8s", [128, TOPK], F32))
            pos8 = _ctx.enter_context(nc.sbuf_tensor("pos8", [128, TOPK], F32))
            keep8 = _ctx.enter_context(nc.sbuf_tensor("keep8", [128, TOPK], F32))
            wk8 = _ctx.enter_context(nc.sbuf_tensor("wk8", [128, TOPK], F32))
            slot8 = _ctx.enter_context(nc.sbuf_tensor("slot8", [128, TOPK], F32))
            gidx = _ctx.enter_context(nc.sbuf_tensor("gidx", [128, TOPK], F32))
            gidxi = _ctx.enter_context(nc.sbuf_tensor("gidxi", [128, TOPK], I32))
            binit = _ctx.enter_context(nc.sbuf_tensor("binit", [NE, 513], F32))
            pA = _ctx.enter_context(nc.psum_tensor("pAb", [128, 512], F32))
            pB = _ctx.enter_context(nc.psum_tensor("pBb", [128, 512], F32))
            pOFF = _ctx.enter_context(nc.psum_tensor("pOFF", [1, NE], F32))
            pPOS = _ctx.enter_context(nc.psum_tensor("pPOS", [128, NE], F32))
            pCS = _ctx.enter_context(nc.psum_tensor("pCS", [1, NE], F32))
            cv0B = cv.v

            @block.gpsimd
            def _(g):
                g.dma_start(out=cp_sb[:].rearrange("p a b -> p (a b)"), in_=cprojT).then_inc(ldw, 16)
                for kc in range(ICH):
                    g.dma_start(out=wop_sb[:, kc, :], in_=wop[kc * 128:(kc + 1) * 128, :]).then_inc(ldw, 16)
                    g.dma_start(out=dsb[:, kc:kc + 1], in_=dvec[kc * 128:(kc + 1) * 128, :]).then_inc(ldw, 16)
            cw_.inc((1 + 2 * ICH) * 16)

            @block.sync
            def _(e):
                e.wait_ge(cc0, 1)
                e.dma_start(out=hsf[:], in_=hs_s.ap()).then_inc(ldx, 16)
                e.dma_start(out=acu[:], in_=acum_s.ap()).then_inc(ldx, 16)
                e.dma_start(out=hmid[:], in_=hl_sh.ap()[0:128, :]).then_inc(ldx, 16)
                e.dma_start(out=oddsb[:], in_=oddcol).then_inc(ldx, 16)
                e.dma_start(out=utri_sb[:], in_=utri).then_inc(ldx, 16)
                e.dma_start(out=io64[:], in_=iota64).then_inc(ldx, 16)
                e.dma_start(out=tok0[:], in_=tokid).then_inc(ldx, 16)
                e.dma_start(out=cntall_sb[0:8, :], in_=cnt_all.ap()).then_inc(ldx, 16)
                e.dma_start(out=pref_sb[0:8, :], in_=prefvec).then_inc(ldx, 16)
            cx.inc(9 * 16)

            @block.vector
            def _(v):
                v.memset(ones[:], 1.0).then_inc(vx, 1)
                v.memset(onerow[:], 1.0).then_inc(vx, 1)
                v.memset(binit[:], 1e9).then_inc(vx, 1)
                v.wait_ge(ldx, cx.v)
                v.tensor_mul(hmid[:], hmid[:], oddsb[:]).then_inc(vx, 1)
                v.wait_ge(vx, cv.v + 4)
                for blk in range(4):
                    v.scalar_tensor_tensor(hsf[:, blk * 512:(blk + 1) * 512], acu[:, blk * 512:(blk + 1) * 512],
                                           hmid[:], hsf[:, blk * 512:(blk + 1) * 512],
                                           ALU.mult, ALU.add).then_inc(vx, 1)
                v.wait_ge(vx, cv.v + 8)
                v.tensor_copy(hsbf[:], hsf[:]).then_inc(vx, 1)
            cv.inc(9)

            @block.gpsimd
            def _(g):
                g.wait_ge(vx, cv0B + 3)
                g.dma_start(out=pt_my.ap(), in_=binit[:]).then_inc(st, 16)
            ct.inc(16)
            init_done = ct.v

            for blk in range(4):
                c0 = blk * 512

                @block.sync
                def _(e, c0=c0):
                    e.wait_ge(mm, cm.v)
                    e.wait_ge(sx, cs.v)
                    for kc in range(ICH):
                        e.dma_start(out=xcs[:, kc, :], in_=xc_s.ap()[kc, :, c0:c0 + 512]).then_inc(ldx, 16)
                        e.dma_start(out=zld[:, kc, :], in_=z_s.ap()[kc, :, c0:c0 + 512]).then_inc(ldx, 16)
                cx.inc(2 * ICH * 16)

                @block.scalar
                def _(s, c0=c0):
                    s.wait_ge(vx, cv.v)
                    s.wait_ge(ldx, cx.v)
                    for kc in range(ICH):
                        s.activation(zsil[:, kc, :], zld[:, kc, :], AF.Silu).then_inc(sx, 1)
                cs.inc(ICH)

                for f in range(ICH):
                    ps = pA if f % 2 == 0 else pB
                    sl = f % 2

                    @block.tensor
                    def _(t, f=f, ps=ps, c0=c0, first=(f == 0)):
                        if first:
                            t.wait_ge(ldw, cw_.v)
                            t.wait_ge(ldx, cx.v)
                        t.wait_ge(vx, cv.v)
                        mi = t.matmul(ps[:, 0:512], cp_sb[:, f, :], hsbf[:, c0:c0 + 512], start=True, stop=True)
                        mi.then_inc(mm, 1)
                    cm.inc()

                    @block.vector
                    def _(v, f=f, ps=ps, sl=sl):
                        v.wait_ge(mm, cm.v)
                        v.wait_ge(sx, cs.v)
                        v.scalar_tensor_tensor(yf[:, sl, :], xcs[:, f, :], dsb[:, f:f + 1],
                                               ps[:, 0:512], ALU.mult, ALU.add).then_inc(vx, 1)
                        v.wait_ge(vx, cv.v + 1)
                        v.tensor_mul(ymbf[:, f, :], yf[:, sl, :], zsil[:, f, :]).then_inc(vx, 1)
                    cv.inc(2)

                for dm in range(DCH):
                    ps = pA if dm % 2 == 0 else pB
                    sl = dm % 2

                    @block.tensor
                    def _(t, dm=dm, ps=ps, first=(dm == 0)):
                        t.wait_ge(vx, cv.v)
                        if dm >= 2:
                            t.wait_ge(st, ct.v)
                        for kc in range(ICH):
                            mi = t.matmul(ps[:, 0:512], wop_sb[:, kc, dm * 128:(dm + 1) * 128],
                                          ymbf[:, kc, :], start=(kc == 0), stop=(kc == ICH - 1))
                        mi.then_inc(mm, 1)
                    cm.inc()

                    @block.vector
                    def _(v, ps=ps, sl=sl):
                        v.wait_ge(mm, cm.v)
                        v.tensor_copy(mosb[:, sl, :], ps[:, 0:512]).then_inc(vx, 1)
                    cv.inc()

                    @block.scalar
                    def _(s, dm=dm, sl=sl, c0=c0):
                        s.wait_ge(vx, cv.v)
                        s.dma_start(out=mo_s.ap()[dm, :, c0:c0 + 512], in_=mosb[:, sl, :]).then_inc(st, 16)
                    ct.inc(16)

            @block.tensor
            def _(t):
                t.wait_ge(ldx, cx.v)
                mi = t.matmul(pOFF[0:1, 0:NE], pref_sb[0:8, 0:1], cntall_sb[0:8, 0:NE], start=True, stop=True)
                mi.then_inc(mm, 1)
            cm.inc()

            @block.vector
            def _(v):
                v.wait_ge(mm, cm.v)
                v.tensor_copy(offs[:], pOFF[0:1, 0:NE]).then_inc(vx, 1)
            cv.inc()

            for i in range(TPC // 128):
                @block.sync
                def _(e, i=i):
                    e.wait_ge(vx, cv.v)
                    e.dma_start(out=mown[:], in_=mask_s.ap()[i * 128:(i + 1) * 128, :]).then_inc(ldx, 16)
                    e.dma_start(out=idx8u[:], in_=idx8_s.ap()[i * 128:(i + 1) * 128, :]).then_inc(ldx, 16)
                    e.dma_start(out=w8s[:], in_=w8_s.ap()[i * 128:(i + 1) * 128, :]).then_inc(ldx, 16)
                cx.inc(48)

                @block.tensor
                def _(t, i=i):
                    t.wait_ge(ldx, cx.v)
                    t.matmul(pPOS[:, 0:NE], utri_sb[:], mown[:], start=True, stop=False)
                    mi = t.matmul(pPOS[:, 0:NE], onerow[:], offs[:], start=False, stop=True)
                    mi.then_inc(mm, 1)
                    mi2 = t.matmul(pCS[0:1, 0:NE], ones[:], mown[:], start=True, stop=True)
                    mi2.then_inc(mm, 1)
                cm.inc(2)

                @block.vector
                def _(v, i=i):
                    v.wait_ge(mm, cm.v)
                    v.wait_ge(st, ct.v)
                    v.tensor_copy(pos_sb[:], pPOS[:, 0:NE]).then_inc(vx, 1)
                    v.tensor_add(offs[:], offs[:], pCS[0:1, 0:NE]).then_inc(vx, 1)
                    v.tensor_copy(idx8f[:], idx8u[:]).then_inc(vx, 1)
                    v.wait_ge(vx, cv.v + 3)
                    for k in range(TOPK):
                        v.tensor_scalar(oh8[:, k, :], io64[:], idx8f[:, k:k + 1], None,
                                        ALU.is_equal).then_inc(vx, 1)
                    v.wait_ge(vx, cv.v + 3 + TOPK)
                    for k in range(TOPK):
                        v.tensor_mul(tm8[:, k, :], oh8[:, k, :], pos_sb[:]).then_inc(vx, 1)
                    v.wait_ge(vx, cv.v + 3 + 2 * TOPK)
                    for k in range(TOPK):
                        v.reduce_sum(pos8[:, k:k + 1], tm8[:, k, :], axis=AX.X).then_inc(vx, 1)
                    v.wait_ge(vx, cv.v + 3 + 3 * TOPK)
                    v.tensor_single_scalar(keep8[:], pos8[:], 512.0, ALU.is_lt).then_inc(vx, 1)
                    v.tensor_scalar_min(slot8[:], pos8[:], 512.0).then_inc(vx, 1)
                    v.tensor_scalar_add(tokc[:], tok0[:], float(i * 128)).then_inc(vx, 1)
                    v.wait_ge(vx, cv.v + 6 + 3 * TOPK)
                    v.tensor_mul(wk8[:], w8s[:], keep8[:]).then_inc(vx, 1)
                    v.scalar_tensor_tensor(gidx[:], idx8f[:], 513.0, slot8[:], ALU.mult, ALU.add).then_inc(vx, 1)
                    v.wait_ge(vx, cv.v + 8 + 3 * TOPK)
                    v.tensor_copy(gidxi[:], gidx[:]).then_inc(vx, 1)
                cv.inc(9 + 3 * TOPK)

                @block.scalar
                def _(s, i=i):
                    s.wait_ge(vx, cv.v)
                    s.dma_start(out=gi_s.ap()[i * 128:(i + 1) * 128, :], in_=gidx[:]).then_inc(st, 16)
                    s.dma_start(out=wk_s.ap()[i * 128:(i + 1) * 128, :], in_=wk8[:]).then_inc(st, 16)
                ct.inc(32)

                @block.gpsimd
                def _(g, i=i):
                    g.wait_ge(vx, cv.v)
                    g.wait_ge(st, init_done)
                    for k in range(TOPK):
                        g.indirect_dma_start(
                            out=pt_flat, out_offset=IndirectOffsetOnAxis(gidxi[:, k:k + 1], 0),
                            in_=tokc[:], in_offset=None,
                        ).then_inc(st, 16)
                ct.inc(TOPK * 16)

        @block.gpsimd
        def _(g):
            g.wait_ge(st, ct.v)
            g.collective_compute(
                "AllGather", ALU.bypass, replica_groups=[list(range(8))],
                ins=[pt_my.ap().opt()], outs=[pt_all.ap().opt()],
            ).then_inc(cc1)
        barrier()

        with ExitStack() as _ctx:
            pts = _ctx.enter_context(nc.sbuf_tensor("pts", [8, NCORES, 513], F32))
            tmin = _ctx.enter_context(nc.sbuf_tensor("tmin", [8, 513], F32))
            ptix = _ctx.enter_context(nc.sbuf_tensor("ptix", [8, 8], I32))
            idxf = _ctx.enter_context(nc.sbuf_tensor("idxf", [128, 4, EPC], F32))
            idxi = _ctx.enter_context(nc.sbuf_tensor("idxi", [128, 4, EPC], I32))
            xrows = _ctx.enter_context(nc.sbuf_tensor("xrows", [128, 2, D], F32))
            sqr = _ctx.enter_context(nc.sbuf_tensor("sqr", [128, 2, D], F32))
            xrn = _ctx.enter_context(nc.sbuf_tensor("xrn", [128, 2, D], BF16))
            r1 = _ctx.enter_context(nc.sbuf_tensor("r1", [128, 2, 8], F32))
            idn = _ctx.enter_context(nc.sbuf_tensor("idn", [128, 128], BF16))
            idnf = _ctx.enter_context(nc.sbuf_tensor("idnf", [128, 128], F32))
            xeT = _ctx.enter_context(nc.sbuf_tensor("xeT", [128, 2, DCH, 512], BF16))
            wgu = _ctx.enter_context(nc.sbuf_tensor("wgu", [128, 2, DCH, 1024], BF16))
            wdw = _ctx.enter_context(nc.sbuf_tensor("wdw", [128, 2, 4, 1024], BF16))
            sg = _ctx.enter_context(nc.sbuf_tensor("sg", [128, 512], BF16))
            actb = _ctx.enter_context(nc.sbuf_tensor("actb", [128, 4, 512], BF16))
            oeb = _ctx.enter_context(nc.sbuf_tensor("oeb", [128, 2, 512], BF16))
            zrow = _ctx.enter_context(nc.sbuf_tensor("zrow", [1, 1024], BF16))
            epsc = _ctx.enter_context(nc.sbuf_tensor("epsc", [128, 1], F32))
            pT = _ctx.enter_context(nc.psum_tensor("pT", [128, 128], BF16))
            pTT = _ctx.enter_context(nc.psum_tensor("pTT", [128, 32], F32))
            pA = _ctx.enter_context(nc.psum_tensor("pAc", [128, 512], F32))
            pB = _ctx.enter_context(nc.psum_tensor("pBc", [128, 512], F32))
            pC = _ctx.enter_context(nc.psum_tensor("pCc", [128, 512], F32))
            pD = _ctx.enter_context(nc.psum_tensor("pDc", [128, 512], F32))
            cm0C = cm.v
            cv0C = cv.v

            @block.sync
            def _(e):
                e.dma_start(out=ptix[0:8, :], in_=ptidx).then_inc(ldx, 16)
                e.dma_start(out=idnf[:], in_=ident).then_inc(ldx, 16)
            cx.inc(32)

            @block.gpsimd
            def _(g):
                g.wait_ge(ldx, cx.v)
                g.wait_ge(cc1, 1)
                for src in range(NCORES):
                    g.indirect_dma_start(
                        out=pts[0:8, src, :], out_offset=None,
                        in_=pt_all.ap(), in_offset=IndirectOffsetOnAxis(ptix[0:8, src:src + 1], 0),
                    ).then_inc(ldx, 16)
                g.dma_start(out=idn[:], in_=ident).then_inc(ldw, 16)
            cx.inc(NCORES * 16)
            cw_.inc(16)

            @block.vector
            def _(v):
                v.wait_ge(ldx, cx.v)
                v.memset(zrow[:], 0.0).then_inc(vx, 1)
                v.memset(epsc[:], 1e-5).then_inc(vx, 1)
                v.tensor_copy(tmin[:], pts[:, 0, :]).then_inc(vx, 1)
                for src in range(1, NCORES):
                    v.wait_ge(vx, cv.v + 2 + src)
                    v.tensor_tensor(tmin[:], tmin[:], pts[:, src, :], ALU.min).then_inc(vx, 1)
            cv.inc(2 + NCORES)

            @block.tensor
            def _(t):
                t.wait_ge(vx, cv.v)
                t.wait_ge(ldx, cx.v)
                for i in range(4):
                    mi = t.transpose(pTT[:, i * EPC:(i + 1) * EPC], tmin[0:8, i * 128:(i + 1) * 128],
                                     idnf[0:8, 0:EPC])
                    mi.then_inc(mm, 1)
            cm.inc(4)

            @block.vector
            def _(v):
                v.wait_ge(mm, cm.v)
                v.tensor_copy(idxf[:].rearrange("p a b -> p (a b)"), pTT[:, 0:32]).then_inc(vx, 1)
                v.wait_ge(vx, cv.v + 1)
                v.tensor_copy(idxi[:], idxf[:]).then_inc(vx, 1)
            cv.inc(2)
            idx_ready = cv.v

            gemm_cm = {}
            ln_w = {}
            for e_ in range(EPC):
                wsl = e_ % 2
                w_prev = gemm_cm.get(e_ - 2, cm0C)

                @block.gpsimd
                def _(g, e_=e_, wsl=wsl, w_prev=w_prev):
                    g.wait_ge(mm, w_prev)
                    for kc in range(DCH):
                        g.dma_start(out=wgu[:, wsl, kc, :], in_=egu8[e_, kc * 128:(kc + 1) * 128, :]).then_inc(ldw, 16)
                    for kc in range(4):
                        g.dma_start(out=wdw[:, wsl, kc, :], in_=edw8[e_, kc * 128:(kc + 1) * 128, :]).then_inc(ldw, 16)
                cw_.inc((DCH + 4) * 16)

                for rb in range(4):
                    @block.gpsimd
                    def _(g, e_=e_, rb=rb, wsl=wsl, w_prev=w_prev):
                        g.wait_ge(vx, idx_ready)
                        g.wait_ge(mm, w_prev)
                        if rb > 0:
                            g.wait_ge(vx, ln_w[(e_, rb - 1)])
                        g.indirect_dma_start(
                            out=xrows[:, wsl, :], out_offset=None,
                            in_=xfull, in_offset=IndirectOffsetOnAxis(idxi[:, rb, e_:e_ + 1], 0),
                            bounds_check=16383, oob_is_err=False,
                        ).then_inc(ldx, 16)
                    cx.inc(16)

                    @block.scalar
                    def _(s, e_=e_, rb=rb, wsl=wsl):
                        s.wait_ge(ldx, cx.v)
                        s.activation(sqr[:, wsl, :], xrows[:, wsl, :], AF.Square).then_inc(sx, 1)
                    cs.inc()

                    @block.vector
                    def _(v, e_=e_, rb=rb, wsl=wsl):
                        v.wait_ge(sx, cs.v)
                        v.reduce_sum(r1[:, wsl, 0:1], xrows[:, wsl, :], axis=AX.X).then_inc(vx, 1)
                        v.reduce_sum(r1[:, wsl, 1:2], sqr[:, wsl, :], axis=AX.X).then_inc(vx, 1)
                        v.wait_ge(vx, cv.v + 2)
                        v.tensor_scalar_mul(r1[:, wsl, 2:3], r1[:, wsl, 0:1], 1.0 / D).then_inc(vx, 1)
                        v.tensor_scalar_mul(r1[:, wsl, 3:4], r1[:, wsl, 1:2], 1.0 / D).then_inc(vx, 1)
                        v.wait_ge(vx, cv.v + 4)
                        v.tensor_mul(r1[:, wsl, 4:5], r1[:, wsl, 2:3], r1[:, wsl, 2:3]).then_inc(vx, 1)
                        v.wait_ge(vx, cv.v + 5)
                        v.tensor_sub(r1[:, wsl, 5:6], r1[:, wsl, 3:4], r1[:, wsl, 4:5]).then_inc(vx, 1)
                    cv.inc(6)

                    @block.scalar
                    def _(s, e_=e_, rb=rb, wsl=wsl):
                        s.wait_ge(vx, cv.v)
                        s.activation(r1[:, wsl, 6:7], r1[:, wsl, 5:6], AF.Ln, bias=epsc[:]).then_inc(sx, 1)
                        s.wait_ge(sx, cs.v + 1)
                        s.activation(r1[:, wsl, 7:8], r1[:, wsl, 6:7], AF.Exp, scale=-0.5).then_inc(sx, 1)
                    cs.inc(2)

                    @block.vector
                    def _(v, e_=e_, rb=rb, wsl=wsl):
                        v.wait_ge(sx, cs.v)
                        v.tensor_scalar(xrn[:, wsl, :], xrows[:, wsl, :], r1[:, wsl, 2:3], r1[:, wsl, 7:8],
                                        ALU.subtract, ALU.mult).then_inc(vx, 1)
                    cv.inc()

                    for kc in range(DCH):
                        @block.tensor
                        def _(t, e_=e_, rb=rb, kc=kc, wsl=wsl):
                            t.wait_ge(vx, cv.v)
                            mi = t.transpose(pT[:, 0:128], xrn[:, wsl, kc * 128:(kc + 1) * 128], idn[:])
                            mi.then_inc(mm, 1)
                        cm.inc()

                        @block.vector
                        def _(v, e_=e_, rb=rb, kc=kc, wsl=wsl):
                            v.wait_ge(mm, cm.v)
                            v.tensor_copy(xeT[:, wsl, kc, rb * 128:(rb + 1) * 128], pT[:, 0:128]).then_inc(vx, 1)
                        cv.inc()
                    ln_w[(e_, rb)] = cv.v

                xeT_ready = cv.v

                for f in range(4):
                    @block.tensor
                    def _(t, e_=e_, wsl=wsl, f=f, first=(f == 0), xeT_ready=xeT_ready):
                        if first:
                            t.wait_ge(ldw, cw_.v)
                            t.wait_ge(vx, xeT_ready)
                        t.wait_ge(sx, cs.v)
                        t.wait_ge(vx, cv.v)
                        for kc in range(DCH):
                            mi = t.matmul(pA[:, 0:512], wgu[:, wsl, kc, f * 128:(f + 1) * 128],
                                          xeT[:, wsl, kc, :], start=(kc == 0), stop=(kc == DCH - 1))
                        mi.then_inc(mm, 1)
                        for kc in range(DCH):
                            mi = t.matmul(pB[:, 0:512], wgu[:, wsl, kc, 512 + f * 128:512 + (f + 1) * 128],
                                          xeT[:, wsl, kc, :], start=(kc == 0), stop=(kc == DCH - 1))
                        mi.then_inc(mm, 1)
                    cm.inc(2)

                    @block.scalar
                    def _(s, f=f):
                        s.wait_ge(mm, cm.v)
                        s.activation(sg[:], pA[:, 0:512], AF.Silu).then_inc(sx, 1)
                    cs.inc()

                    @block.vector
                    def _(v, f=f):
                        v.wait_ge(sx, cs.v)
                        v.tensor_mul(actb[:, f, :], sg[:], pB[:, 0:512]).then_inc(vx, 1)
                    cv.inc()

                for rb in range(4):
                    for dmh in range(2):
                        ps = pC if (rb * 2 + dmh) % 2 == 0 else pD
                        osl = (rb * 2 + dmh) % 2

                        @block.tensor
                        def _(t, e_=e_, wsl=wsl, rb=rb, dmh=dmh, ps=ps):
                            t.wait_ge(vx, cv.v)
                            t.wait_ge(st, ct.v)
                            for ic in range(4):
                                mi = t.matmul(ps[:, 0:512], actb[:, ic, rb * 128:(rb + 1) * 128],
                                              wdw[:, wsl, ic, dmh * 512:(dmh + 1) * 512],
                                              start=(ic == 0), stop=(ic == 3))
                            mi.then_inc(mm, 1)
                        cm.inc()

                        @block.vector
                        def _(v, ps=ps, osl=osl):
                            v.wait_ge(mm, cm.v)
                            v.tensor_copy(oeb[:, osl, :], ps[:, 0:512]).then_inc(vx, 1)
                        cv.inc()

                        @block.scalar
                        def _(s, e_=e_, rb=rb, dmh=dmh, osl=osl):
                            s.wait_ge(vx, cv.v)
                            s.dma_start(out=oe_my.ap()[e_ * 513 + rb * 128:e_ * 513 + (rb + 1) * 128,
                                                       dmh * 512:(dmh + 1) * 512],
                                        in_=oeb[:, osl, :]).then_inc(st, 16)
                        ct.inc(16)

                @block.scalar
                def _(s, e_=e_):
                    s.wait_ge(vx, cv0C + 1)
                    s.dma_start(out=oe_my.ap()[e_ * 513 + 512:e_ * 513 + 513, :], in_=zrow[:]).then_inc(st, 16)
                ct.inc(16)
                gemm_cm[e_] = cm.v

        @block.gpsimd
        def _(g):
            g.wait_ge(st, ct.v)
            g.collective_compute(
                "AllGather", ALU.bypass, replica_groups=[list(range(8))],
                ins=[oe_my.ap().opt()], outs=[oe_all.ap().opt()],
            ).then_inc(cc2)
        barrier()
        with ExitStack() as _ctx:
            gis = _ctx.enter_context(nc.sbuf_tensor("gis", [128, TOPK], F32))
            gii = _ctx.enter_context(nc.sbuf_tensor("gii", [128, TOPK], I32))
            wks = _ctx.enter_context(nc.sbuf_tensor("wks", [128, TOPK], F32))
            grow = _ctx.enter_context(nc.sbuf_tensor("grow", [128, 2, 1024], BF16))
            acc = _ctx.enter_context(nc.sbuf_tensor("acc", [128, 1024], F32))
            xa = _ctx.enter_context(nc.sbuf_tensor("xa", [128, DCH, 512], F32))
            xb2 = _ctx.enter_context(nc.sbuf_tensor("xb2", [128, DCH, 512], F32))
            sqf = _ctx.enter_context(nc.sbuf_tensor("sqfE", [128, DCH, 512], F32))
            ymt = _ctx.enter_context(nc.sbuf_tensor("ymt", [128, 4, 1024], F32))
            rows = _ctx.enter_context(nc.sbuf_tensor("rowsE", [1, 8, 512], F32))
            bR = _ctx.enter_context(nc.sbuf_tensor("bRE", [128, 512], F32))
            bM = _ctx.enter_context(nc.sbuf_tensor("bME", [128, 512], F32))
            ones = _ctx.enter_context(nc.sbuf_tensor("onesE", [128, 1], F32))
            onerow = _ctx.enter_context(nc.sbuf_tensor("onerowE", [1, 128], F32))
            epst = _ctx.enter_context(nc.sbuf_tensor("epstE", [1, 1], F32))
            xnb = _ctx.enter_context(nc.sbuf_tensor("xnbE", [128, DCH, 512], BF16))
            idnE = _ctx.enter_context(nc.sbuf_tensor("idnE", [128, 128], F32))
            xsix = _ctx.enter_context(nc.sbuf_tensor("xsix", [128, DCH], I32))
            pR1 = _ctx.enter_context(nc.psum_tensor("pR1e", [128, 512], F32))
            pR2 = _ctx.enter_context(nc.psum_tensor("pR2e", [128, 512], F32))
            pBC = _ctx.enter_context(nc.psum_tensor("pBCe", [128, 512], F32))
            pS = _ctx.enter_context(nc.psum_tensor("pSe", [128, 512], F32))
            pT1 = _ctx.enter_context(nc.psum_tensor("pT1e", [128, 128], F32))
            pT2 = _ctx.enter_context(nc.psum_tensor("pT2e", [128, 128], F32))

            @block.vector
            def _(v):
                v.memset(ones[:], 1.0).then_inc(vx, 1)
                v.memset(onerow[:], 1.0).then_inc(vx, 1)
                v.memset(epst[:], 1e-5).then_inc(vx, 1)
            cv.inc(3)

            @block.sync
            def _(e):
                e.dma_start(out=idnE[:], in_=ident).then_inc(ldx, 16)
                e.dma_start(out=xsix[:], in_=xsidx).then_inc(ldx, 16)
            cx.inc(32)
            eaux = cx.v

            e1_st = {}
            for blk in range(4):
                c0 = blk * 512

                @block.sync
                def _(e, c0=c0, blk=blk):
                    if blk > 0:
                        e.wait_ge(st, e1_st[blk - 1])
                    for kc in range(DCH):
                        e.dma_start(out=xa[:, kc, :], in_=xTv[kc * 128:(kc + 1) * 128, c0:c0 + 512]).then_inc(ldx, 16)
                        e.dma_start(out=xb2[:, kc, :], in_=mo_s.ap()[kc, :, c0:c0 + 512]).then_inc(ldx, 16)
                cx.inc(2 * DCH * 16)

                @block.vector
                def _(v, c0=c0):
                    v.wait_ge(ldx, cx.v)
                    for kc in range(DCH):
                        v.tensor_add(xa[:, kc, :], xa[:, kc, :], xb2[:, kc, :]).then_inc(vx, 1)
                cv.inc(DCH)

                @block.sync
                def _(e, c0=c0):
                    e.wait_ge(vx, cv.v)
                    for kc in range(DCH):
                        e.dma_start(out=xb2[:, kc, :], in_=ssh_s.ap()[kc, :, c0:c0 + 512]).then_inc(ldx, 16)
                cx.inc(DCH * 16)

                @block.vector
                def _(v, c0=c0):
                    v.wait_ge(ldx, cx.v)
                    for kc in range(DCH):
                        v.tensor_add(xa[:, kc, :], xa[:, kc, :], xb2[:, kc, :]).then_inc(vx, 1)
                cv.inc(DCH)

                @block.scalar
                def _(s, c0=c0):
                    s.wait_ge(vx, cv.v)
                    for kc in range(DCH):
                        s.dma_start(out=x1_s.ap()[kc, :, c0:c0 + 512], in_=xa[:, kc, :]).then_inc(st, 16)
                ct.inc(DCH * 16)
                e1_st[blk] = ct.v

            @block.gpsimd
            def _(g):
                g.wait_ge(cc2, 1)

            d_ct = {}
            for i in range(TPC // 128):
                @block.sync
                def _(e, i=i):
                    e.wait_ge(vx, cv.v)
                    e.dma_start(out=gis[:], in_=gi_s.ap()[i * 128:(i + 1) * 128, :]).then_inc(ldx, 16)
                    e.dma_start(out=wks[:], in_=wk_s.ap()[i * 128:(i + 1) * 128, :]).then_inc(ldx, 16)
                cx.inc(32)

                @block.vector
                def _(v, i=i):
                    v.wait_ge(ldx, cx.v)
                    v.tensor_copy(gii[:], gis[:]).then_inc(vx, 1)
                cv.inc()

                for k in range(TOPK):
                    sl = k % 2

                    @block.gpsimd
                    def _(g, i=i, k=k, sl=sl):
                        g.wait_ge(vx, cv.v)
                        g.indirect_dma_start(
                            out=grow[:, sl, :], out_offset=None,
                            in_=oe_all.ap(), in_offset=IndirectOffsetOnAxis(gii[:, k:k + 1], 0),
                        ).then_inc(ldx, 16)
                    cx.inc(16)

                    @block.vector
                    def _(v, i=i, k=k, sl=sl):
                        v.wait_ge(ldx, cx.v)
                        if k == 0:
                            v.tensor_scalar(acc[:], grow[:, sl, :], wks[:, 0:1], None, ALU.mult).then_inc(vx, 1)
                        else:
                            v.wait_ge(vx, cv.v)
                            v.scalar_tensor_tensor(acc[:], grow[:, sl, :], wks[:, k:k + 1],
                                                   acc[:], ALU.mult, ALU.add).then_inc(vx, 1)
                    cv.inc()

                @block.scalar
                def _(s, i=i):
                    s.wait_ge(vx, cv.v)
                    s.dma_start(out=ym_s.ap()[i * 128:(i + 1) * 128, :], in_=acc[:]).then_inc(st, 16)
                ct.inc(16)
                d_ct[i] = ct.v

                @block.vector
                def _(v):
                    v.wait_ge(st, ct.v)
                    v.engine_nop().then_inc(vx, 1)
                cv.inc()

            e2_st = None
            for blk in range(4):
                c0 = blk * 512

                @block.sync
                def _(e, c0=c0, blk=blk):
                    e.wait_ge(st, e1_st[3])
                    e.wait_ge(st, d_ct[15])
                    if blk > 0:
                        e.wait_ge(st, e2_st)
                        e.wait_ge(sx, cs.v)
                        e.wait_ge(mm, cm.v)
                    for kc in range(DCH):
                        e.dma_start(out=xa[:, kc, :], in_=x1_s.ap()[kc, :, c0:c0 + 512]).then_inc(ldx, 16)
                    for j in range(4):
                        e.dma_start(out=ymt[:, j, :], in_=ym_s.ap()[c0 + j * 128:c0 + (j + 1) * 128, :]).then_inc(ldx, 16)
                cx.inc((DCH + 4) * 16)

                for j in range(4):
                    for kc in range(DCH):
                        ps = pT1 if (j * DCH + kc) % 2 == 0 else pT2

                        @block.tensor
                        def _(t, j=j, kc=kc, ps=ps):
                            t.wait_ge(ldx, cx.v)
                            t.wait_ge(vx, cv.v)
                            mi = t.transpose(ps[:, 0:128], ymt[:, j, kc * 128:(kc + 1) * 128], idnE[:])
                            mi.then_inc(mm, 1)
                        cm.inc()

                        @block.vector
                        def _(v, j=j, kc=kc, ps=ps):
                            v.wait_ge(mm, cm.v)
                            v.tensor_copy(sqf[:, kc, j * 128:(j + 1) * 128], ps[:, 0:128]).then_inc(vx, 1)
                        cv.inc()

                @block.vector
                def _(v, c0=c0):
                    v.wait_ge(ldx, cx.v)
                    v.wait_ge(vx, cv.v)
                    for kc in range(DCH):
                        v.tensor_add(xa[:, kc, :], xa[:, kc, :], sqf[:, kc, :]).then_inc(vx, 1)
                cv.inc(DCH)
                @block.scalar
                def _(s, c0=c0):
                    s.wait_ge(vx, cv.v)
                    for kc in range(DCH):
                        s.dma_start(out=x1_s.ap()[kc, :, c0:c0 + 512], in_=xa[:, kc, :]).then_inc(st, 16)
                    for kc in range(DCH):
                        s.activation(sqf[:, kc, :], xa[:, kc, :], AF.Square).then_inc(sx, 1)
                ct.inc(DCH * 16)
                cs.inc(DCH)

                @block.tensor
                def _(t, c0=c0):
                    t.wait_ge(vx, cv.v)
                    for kc in range(DCH):
                        mi = t.matmul(pR1[0:1, 0:512], ones[:], xa[:, kc, :], start=(kc == 0), stop=(kc == DCH - 1))
                    mi.then_inc(mm, 1)
                    t.wait_ge(sx, cs.v)
                    for kc in range(DCH):
                        mi = t.matmul(pR2[0:1, 0:512], ones[:], sqf[:, kc, :], start=(kc == 0), stop=(kc == DCH - 1))
                    mi.then_inc(mm, 1)
                cm.inc(2)

                @block.vector
                def _(v):
                    v.wait_ge(mm, cm.v)
                    v.tensor_scalar_mul(rows[0:1, 0, :], pR1[0:1, 0:512], 1.0 / D).then_inc(vx, 1)
                    v.tensor_scalar_mul(rows[0:1, 1, :], pR2[0:1, 0:512], 1.0 / D).then_inc(vx, 1)
                    v.wait_ge(vx, cv.v + 2)
                    v.tensor_mul(rows[0:1, 5, :], rows[0:1, 0, :], rows[0:1, 0, :]).then_inc(vx, 1)
                    v.wait_ge(vx, cv.v + 3)
                    v.tensor_sub(rows[0:1, 1, :], rows[0:1, 1, :], rows[0:1, 5, :]).then_inc(vx, 1)
                cv.inc(4)

                @block.scalar
                def _(s):
                    s.wait_ge(vx, cv.v)
                    s.activation(rows[0:1, 6, :], rows[0:1, 1, :], AF.Ln, bias=epst[:]).then_inc(sx, 1)
                    s.wait_ge(sx, cs.v + 1)
                    s.activation(rows[0:1, 2, :], rows[0:1, 6, :], AF.Exp, scale=-0.5).then_inc(sx, 1)
                cs.inc(2)

                @block.vector
                def _(v):
                    v.wait_ge(sx, cs.v)
                    v.tensor_mul(rows[0:1, 3, :], rows[0:1, 0, :], rows[0:1, 2, :]).then_inc(vx, 1)
                cv.inc()

                @block.tensor
                def _(t):
                    t.wait_ge(vx, cv.v)
                    mi = t.matmul(pBC[:, 0:512], onerow[:], rows[0:1, 2, :], start=True, stop=True)
                    mi.then_inc(mm, 1)
                    mi = t.matmul(pS[:, 0:512], onerow[:], rows[0:1, 3, :], start=True, stop=True)
                    mi.then_inc(mm, 1)
                cm.inc(2)

                @block.vector
                def _(v, c0=c0):
                    v.wait_ge(mm, cm.v)
                    v.tensor_copy(bR[:], pBC[:, 0:512]).then_inc(vx, 1)
                    v.tensor_copy(bM[:], pS[:, 0:512]).then_inc(vx, 1)
                    v.wait_ge(vx, cv.v + 2)
                    for kc in range(DCH):
                        v.tensor_mul(sqf[:, kc, :], xa[:, kc, :], bR[:]).then_inc(vx, 1)
                    v.wait_ge(vx, cv.v + 2 + DCH)
                    for kc in range(DCH):
                        v.tensor_sub(xnb[:, kc, :], sqf[:, kc, :], bM[:]).then_inc(vx, 1)
                cv.inc(2 + 2 * DCH)

                @block.scalar
                def _(s, c0=c0):
                    s.wait_ge(vx, cv.v)
                    for kc in range(DCH):
                        s.dma_start(out=xn1_mine.ap()[kc, :, c0:c0 + 512], in_=xnb[:, kc, :]).then_inc(st, 16)
                ct.inc(DCH * 16)

                @block.gpsimd
                def _(g, c0=c0):
                    g.wait_ge(vx, cv.v)
                    g.wait_ge(ldx, eaux)
                    for kc in range(DCH):
                        g.indirect_dma_start(
                            out=xn1_sh.ap(), out_offset=IndirectOffsetOnAxis(xsix[:, kc:kc + 1], 0),
                            in_=xnb[:, kc, :], in_offset=None, element_offset=c0,
                        ).then_inc(st, 16)
                ct.inc(DCH * 16)
                e2_st = ct.v

        @block.gpsimd
        def _(g):
            g.wait_ge(st, ct.v)
            g.collective_compute(
                "AllGather", ALU.bypass, replica_groups=[list(range(8))],
                ins=[bar_i.ap().opt()], outs=[bar_o.ap().opt()],
            ).then_inc(cc3)
        barrier()

        with ExitStack() as _ctx:
            LB = 4096
            kbf = _ctx.enter_context(nc.sbuf_tensor("kbf", [128, 8, LB], BF16))
            vbf = _ctx.enter_context(nc.sbuf_tensor("vbf", [128, 32, 1024], BF16))
            vfl = vbf[:].rearrange("p a b -> p (a b)")
            wkv_sb = vfl[:, 0:2048].rearrange("p (a b) -> p a b", a=DCH)
            wku_sb = vfl[:, 2048:4096].rearrange("p (a b) -> p a b", a=2)
            u1 = _ctx.enter_context(nc.sbuf_tensor("u1", [128, 16384], BF16))
            latr = u1[:].rearrange("p (a b) -> p a b", a=2)
            qvv = u1[:, 12288:16384].rearrange("p (a b) -> p a b", a=8)
            wo_sb = u1[:, 0:8192].rearrange("p (a b) -> p a b", a=DCH)
            wsg_sb = u1[:, 8192:16384].rearrange("p (a b) -> p a b", a=DCH)
            xstr = _ctx.enter_context(nc.sbuf_tensor("xstr", [128, DCH, 512], BF16))
            attn = _ctx.enter_context(nc.sbuf_tensor("attnF", [128, 8, 512], BF16))
            pexp = _ctx.enter_context(nc.sbuf_tensor("pexpF", [128, 4, 512], BF16))
            recb = _ctx.enter_context(nc.sbuf_tensor("recbF", [128, 512], BF16))
            wq_sb = _ctx.enter_context(nc.sbuf_tensor("wq_sb", [128, DCH, 1024], BF16))
            wvu_sb = attn[:].rearrange("p a b -> p (a b)")[:, 0:2048].rearrange("p (a b) -> p a b", a=2)
            cmk = _ctx.enter_context(nc.sbuf_tensor("cmkF", [128, 4, 512], BF16))
            evenfl = _ctx.enter_context(nc.sbuf_tensor("evenfl", [128, 1], F32))
            oddfl = _ctx.enter_context(nc.sbuf_tensor("oddfl", [128, 1], F32))
            killc = _ctx.enter_context(nc.sbuf_tensor("killc", [128, 1], F32))
            x1s = _ctx.enter_context(nc.sbuf_tensor("x1sF", [128, 1, 512], F32))
            dnsb = x1s[:, 0, :]
            fino = _ctx.enter_context(nc.sbuf_tensor("finoF", [128, 1, 512], F32))
            wsgb2 = pexp[:].rearrange("p a b -> p (a b)")
            sexp = u1[:, 8192:10240].rearrange("p (a b) -> p a b", a=4)
            onesall = _ctx.enter_context(nc.sbuf_tensor("onesallF", [128, 64], F32))
            onesbf = _ctx.enter_context(nc.sbuf_tensor("onesbfF", [128, 1], BF16))
            pS1 = _ctx.enter_context(nc.psum_tensor("pS1f", [128, 512], F32))
            pS2 = _ctx.enter_context(nc.psum_tensor("pS2f", [128, 512], F32))
            pAV1 = _ctx.enter_context(nc.psum_tensor("pAV1f", [128, 512], F32))
            pAV2 = _ctx.enter_context(nc.psum_tensor("pAV2f", [128, 512], F32))
            pDN1 = _ctx.enter_context(nc.psum_tensor("pDN1f", [128, 512], F32))
            pDN2 = _ctx.enter_context(nc.psum_tensor("pDN2f", [128, 512], F32))
            pO1 = _ctx.enter_context(nc.psum_tensor("pO1f", [128, 512], F32))
            pO2 = _ctx.enter_context(nc.psum_tensor("pO2f", [128, 512], F32))

            @block.gpsimd
            def _(g):
                for kc in range(DCH):
                    g.dma_start(out=wq_sb[:, kc, :], in_=wq[kc * 128:(kc + 1) * 128, :]).then_inc(ldw, 16)
                    g.dma_start(out=wkv_sb[:, kc, :], in_=wkv[kc * 128:(kc + 1) * 128, :]).then_inc(ldw, 16)
                for kc in range(2):
                    g.dma_start(out=wku_sb[:, kc, :], in_=wku[kc * 128:(kc + 1) * 128, :]).then_inc(ldw, 16)
                    g.dma_start(out=wvu_sb[:, kc, :], in_=wvu[kc * 128:(kc + 1) * 128, :]).then_inc(ldw, 16)
                g.dma_start(out=cmk[:], in_=cmask).then_inc(ldw, 16)
                g.dma_start(out=evenfl[:], in_=evenfl_in).then_inc(ldw, 16)
                g.dma_start(out=oddfl[:], in_=oddfl_in).then_inc(ldw, 16)
                g.dma_start(out=killc[:], in_=killc_in).then_inc(ldw, 16)
            cw_.inc((2 * DCH + 8) * 16)

            @block.vector
            def _(v):
                v.memset(onesall[:], 1.0).then_inc(vx, 1)
                v.memset(onesbf[:], 1.0).then_inc(vx, 1)
            cv.inc(2)

            for kc in range(DCH):
                for hf in range(2):
                    @block.sync
                    def _(e, kc=kc, hf=hf):
                        e.wait_ge(st, ct.v)
                        e.dma_start(out=x1s[:, 0, :],
                                    in_=wffn[kc * 128:(kc + 1) * 128, hf * 512:(hf + 1) * 512]).then_inc(ldx, 16)
                    cx.inc(16)

                    @block.scalar
                    def _(s, kc=kc, hf=hf):
                        s.wait_ge(ldx, cx.v)
                        s.wait_ge(st, ct.v)
                        s.activation(wsgb2[:, 0:512], x1s[:, 0, :], AF.Sign).then_inc(sx, 1)
                        s.wait_ge(sx, cs.v + 1)
                        s.dma_start(out=wsgn_scr.ap()[kc * 128:(kc + 1) * 128, hf * 512:(hf + 1) * 512],
                                    in_=wsgb2[:, 0:512]).then_inc(st, 16)
                    cs.inc(1)
                    ct.inc(16)
            st_wsgn = ct.v

            BKF = (pS1, pS2, pO1, pO2)
            evacF = {0: cv.v, 1: cv.v, 2: cv.v, 3: cv.v}
            fctr = [0]
            for tb in range(8):
                slot = tb // 4
                col0 = (tb % 4) * 512

                @block.sync
                def _(e, tb=tb, slot=slot, col0=col0, first=(tb == 0)):
                    if first:
                        e.wait_ge(cc3, 1)
                    e.wait_ge(mm, cm.v)
                    for kc in range(DCH):
                        e.dma_start(out=xstr[:, kc, :],
                                    in_=xn1_sh.ap()[slot * 1024 + kc * 128:slot * 1024 + (kc + 1) * 128,
                                                    col0:col0 + 512]).then_inc(ldx, 16)
                cx.inc(DCH * 16)

                for fc in range(2):
                    bank = fctr[0] % 4
                    fctr[0] += 1
                    ps = BKF[bank]

                    @block.tensor
                    def _(t, tb=tb, fc=fc, ps=ps, first=(tb == 0 and fc == 0), w_ev=evacF[bank]):
                        t.wait_ge(ldx, cx.v)
                        if first:
                            t.wait_ge(ldw, cw_.v)
                        t.wait_ge(vx, w_ev)
                        for kc in range(DCH):
                            mi = t.matmul(ps[:, 0:512], wkv_sb[:, kc, fc * 128:(fc + 1) * 128],
                                          xstr[:, kc, :], start=(kc == 0), stop=(kc == DCH - 1))
                        mi.then_inc(mm, 1)
                    cm.inc()

                    @block.vector
                    def _(v, tb=tb, fc=fc, ps=ps):
                        v.wait_ge(mm, cm.v)
                        v.tensor_copy(latr[:, fc, tb * 512:(tb + 1) * 512], ps[:, 0:512]).then_inc(vx, 1)
                    cv.inc()
                    evacF[bank] = cv.v

            for h in range(16):
                base, ch = (h % 2) * 64, h // 2
                for tb in range(8):
                    bank = fctr[0] % 4
                    fctr[0] += 1
                    ps = BKF[bank]

                    @block.tensor
                    def _(t, h=h, tb=tb, ps=ps, base=base, w_ev=evacF[bank]):
                        t.wait_ge(vx, w_ev)
                        for kc in range(2):
                            mi = t.matmul(ps[base:base + 64, 0:512],
                                          wku_sb[:, kc, h * 64:(h + 1) * 64],
                                          latr[:, kc, tb * 512:(tb + 1) * 512],
                                          start=(kc == 0), stop=(kc == 1))
                        mi.then_inc(mm, 1)
                    cm.inc()

                    @block.vector
                    def _(v, h=h, tb=tb, ps=ps, base=base, ch=ch):
                        v.wait_ge(mm, cm.v)
                        v.tensor_copy(kbf[base:base + 64, ch, tb * 512:(tb + 1) * 512],
                                      ps[base:base + 64, 0:512]).then_inc(vx, 1)
                    cv.inc()
                    evacF[bank] = cv.v

            for kt in range(32):
                for half in range(2):
                    bank = fctr[0] % 4
                    fctr[0] += 1
                    ps = BKF[bank]

                    @block.tensor
                    def _(t, kt=kt, half=half, ps=ps, w_ev=evacF[bank]):
                        t.wait_ge(vx, w_ev)
                        for kc in range(2):
                            mi = t.matmul(ps[:, 0:512], latr[:, kc, kt * 128:(kt + 1) * 128],
                                          wvu_sb[:, kc, half * 512:(half + 1) * 512],
                                          start=(kc == 0), stop=(kc == 1))
                        mi.then_inc(mm, 1)
                    cm.inc()

                    @block.vector
                    def _(v, kt=kt, half=half, ps=ps):
                        v.wait_ge(mm, cm.v)
                        v.tensor_copy(vbf[:, kt, half * 512:(half + 1) * 512], ps[:, 0:512]).then_inc(vx, 1)
                    cv.inc()
                    evacF[bank] = cv.v

            carry = dict(evac=[cv.v, cv.v], exp=[0, 0, 0, 0], av=[0, 0, 0, 0], norm=cv.v, pOmul=[0, 0])
            for qb in range(4):
                ktn = 20 + 4 * qb

                @block.sync
                def _(e, qb=qb):
                    e.wait_ge(mm, cm.v)
                    for kc in range(DCH):
                        e.dma_start(out=xstr[:, kc, :],
                                    in_=xn1_mine.ap()[kc, :, qb * 512:(qb + 1) * 512]).then_inc(ldx, 16)
                cx.inc(DCH * 16)

                for h in range(16):
                    base, ch = (h % 2) * 64, h // 2
                    ps = pS1 if h % 2 == 0 else pS2

                    @block.tensor
                    def _(t, h=h, ps=ps, base=base):
                        t.wait_ge(ldx, cx.v)
                        t.wait_ge(vx, cv.v)
                        for kc in range(DCH):
                            mi = t.matmul(ps[base:base + 64, 0:512], wq_sb[:, kc, h * 64:(h + 1) * 64],
                                          xstr[:, kc, :], start=(kc == 0), stop=(kc == DCH - 1))
                        mi.then_inc(mm, 1)
                    cm.inc()

                    @block.vector
                    def _(v, ps=ps, base=base, ch=ch):
                        v.wait_ge(mm, cm.v)
                        v.tensor_copy(qvv[base:base + 64, ch, :], ps[base:base + 64, 0:512]).then_inc(vx, 1)
                    cv.inc()
                carry["evac"] = [cv.v, cv.v]

                for hg in range(4):
                    tiles = [(kt, hh) for kt in range(ktn) for hh in range(4)]
                    n_t = len(tiles)
                    qk_cm, av_cm, exp_cs, evac_cv = {}, {}, {}, {}

                    def emit_qk(i, hg=hg):
                        kt, hh = tiles[i]
                        h = hg * 4 + hh
                        base, ch = (h % 2) * 64, h // 2
                        psS = pS1 if i % 2 == 0 else pS2
                        w_ev = evac_cv[i - 2] if i >= 2 else carry["evac"][i % 2]

                        @block.tensor
                        def _(t, kt=kt, base=base, ch=ch, psS=psS, w_ev=w_ev):
                            t.wait_ge(vx, w_ev)
                            mi = t.matmul(psS[:, 0:512],
                                          kbf[base:base + 64, ch, kt * 128:(kt + 1) * 128],
                                          qvv[base:base + 64, ch, :], start=True, stop=True)
                            mi.then_inc(mm, 1)
                        cm.inc()
                        qk_cm[i] = cm.v

                    def emit_evac(i, qb=qb):
                        kt, hh = tiles[i]
                        d = kt - 4 * qb
                        psS = pS1 if i % 2 == 0 else pS2
                        sl = i % 4
                        w_exp = exp_cs[i - 4] if i >= 4 else carry["exp"][i % 4]

                        @block.vector
                        def _(v, d=d, psS=psS, sl=sl, w_qk=qk_cm[i], w_exp=w_exp):
                            v.wait_ge(mm, w_qk)
                            v.wait_ge(sx, w_exp)
                            if 0 <= d < 4:
                                v.scalar_tensor_tensor(sexp[:, sl, :], cmk[:, d, :], evenfl[:, 0:1],
                                                       psS[:, 0:512], ALU.mult, ALU.add).then_inc(vx, 1)
                            elif 16 <= d < 20:
                                v.scalar_tensor_tensor(sexp[:, sl, :], cmk[:, d - 16, :], oddfl[:, 0:1],
                                                       psS[:, 0:512], ALU.mult, ALU.add).then_inc(vx, 1)
                                v.wait_ge(vx, cv.v + 1)
                                v.tensor_scalar(sexp[:, sl, :], sexp[:, sl, :], killc[:, 0:1],
                                                None, ALU.add).then_inc(vx, 1)
                            elif d >= 4:
                                v.tensor_scalar(sexp[:, sl, :], psS[:, 0:512], killc[:, 0:1],
                                                None, ALU.add).then_inc(vx, 1)
                            else:
                                v.tensor_copy(sexp[:, sl, :], psS[:, 0:512]).then_inc(vx, 1)
                        cv.inc(2 if 16 <= d < 20 else 1)
                        evac_cv[i] = cv.v

                    def emit_exp_pair(i):
                        s0_ = 2 * ((i // 2) % 2)
                        w_av = av_cm[i - 4] if i >= 4 else carry["av"][i % 4]

                        @block.scalar
                        def _(s, s0_=s0_, w_ev=evac_cv[i], w_av=w_av):
                            s.wait_ge(vx, w_ev)
                            s.wait_ge(mm, w_av)
                            s.activation(pexp[:, s0_:s0_ + 2, :].rearrange("p a b -> p (a b)"),
                                         sexp[:, s0_:s0_ + 2, :].rearrange("p a b -> p (a b)"), AF.Exp,
                                         scale=0.125).then_inc(sx, 1)
                        cs.inc()
                        exp_cs[i - 1] = cs.v
                        exp_cs[i] = cs.v

                    def emit_av(i, hg=hg):
                        kt, hh = tiles[i]
                        h = hg * 4 + hh
                        psAV = pAV1 if hh < 2 else pAV2
                        psDN = pDN1 if hh < 2 else pDN2
                        gbase = (hh % 2) * 64
                        first = (kt == 0)
                        last = (kt == ktn - 1)
                        sl = i % 4

                        @block.tensor
                        def _(t, kt=kt, h=h, hh=hh, sl=sl, psAV=psAV, psDN=psDN,
                              gbase=gbase, first=first, last=last, w_exp=exp_cs[i],
                              w_norm=carry["norm"]):
                            t.wait_ge(sx, w_exp)
                            if first:
                                t.wait_ge(vx, w_norm)
                            mi = t.matmul(psAV[gbase:gbase + 64, 0:512],
                                          vbf[:, kt, h * 64:(h + 1) * 64], pexp[:, sl, :],
                                          start=(first and hh % 2 == 0), stop=last)
                            mi.then_inc(mm, 1)
                            mi = t.matmul(psDN[gbase:gbase + 1, 0:512],
                                          onesbf[:], pexp[:, sl, :],
                                          start=(first and hh % 2 == 0), stop=last)
                            mi.then_inc(mm, 1)
                        cm.inc(2)
                        av_cm[i] = cm.v

                    for i in range(n_t):
                        emit_qk(i)
                        emit_evac(i)
                        if i % 2 == 1:
                            emit_exp_pair(i)
                        if i >= 3:
                            emit_av(i - 3)
                    emit_av(n_t - 3)
                    emit_av(n_t - 2)
                    emit_av(n_t - 1)
                    carry["evac"] = [evac_cv[n_t - 2], evac_cv[n_t - 1]]
                    carry["exp"] = [exp_cs[n_t - 4 + j] for j in range(4)]
                    carry["av"] = [av_cm[n_t - 4 + j] for j in range(4)]

                    avdone = cm.v
                    dn_cv = {}
                    bcast_cm = {}
                    mul_cv = {}
                    for hh in range(4):
                        if hh < 2:
                            psDN = pDN1
                            w_pre = avdone
                        else:
                            psDN = pDN2
                            w_pre = bcast_cm[hh - 2]
                        gb = (hh % 2) * 64

                        @block.vector
                        def _(v, psDN=psDN, gb=gb, avdone=avdone, w_pre=w_pre):
                            v.wait_ge(mm, avdone)
                            v.wait_ge(mm, w_pre)
                            v.tensor_copy(dnsb[gb:gb + 1, :], psDN[gb:gb + 1, 0:512]).then_inc(vx, 1)
                        cv.inc()
                        dn_cv[hh] = cv.v
                        h = hg * 4 + hh
                        base, ch = (h % 2) * 64, h // 2
                        psAV = pAV1 if hh < 2 else pAV2
                        pO = pO1 if hh % 2 == 0 else pO2
                        w_mul = mul_cv[hh - 2] if hh >= 2 else carry["pOmul"][hh]

                        @block.scalar
                        def _(s, gb=gb, w_dn=dn_cv[hh]):
                            s.wait_ge(vx, w_dn)
                            s.activation(dnsb[gb:gb + 1, :], dnsb[gb:gb + 1, :], AF.Ln).then_inc(sx, 1)
                            s.wait_ge(sx, cs.v + 1)
                            s.activation(dnsb[gb:gb + 1, :], dnsb[gb:gb + 1, :], AF.Exp,
                                         scale=-1.0).then_inc(sx, 1)
                        cs.inc(2)
                        rec_cs = cs.v

                        @block.tensor
                        def _(t, gb=gb, base=base, pO=pO, rec_cs=rec_cs, w_mul=w_mul):
                            t.wait_ge(sx, rec_cs)
                            t.wait_ge(vx, w_mul)
                            mi = t.matmul(pO[base:base + 64, 0:512], onesall[gb:gb + 1, :],
                                          dnsb[gb:gb + 1, :], start=True, stop=True)
                            mi.then_inc(mm, 1)
                        cm.inc()
                        bcast_cm[hh] = cm.v

                        @block.vector
                        def _(v, h=h, hh=hh, base=base, ch=ch, gb=gb, psAV=psAV, pO=pO,
                              w_b=cm.v):
                            v.wait_ge(mm, w_b)
                            v.tensor_copy(recb[base:base + 64, :], pO[base:base + 64, 0:512]).then_inc(vx, 1)
                            v.wait_ge(vx, cv.v + 1)
                            v.tensor_mul(attn[base:base + 64, ch, :], psAV[gb:gb + 64, 0:512],
                                         recb[base:base + 64, :]).then_inc(vx, 1)
                        cv.inc(2)
                        mul_cv[hh] = cv.v
                    carry["norm"] = cv.v
                    carry["pOmul"] = [mul_cv[2], mul_cv[3]]

                @block.gpsimd
                def _(g, qb=qb):
                    g.wait_ge(mm, cm.v)
                    for kc in range(DCH):
                        g.dma_start(out=wo_sb[:, kc, :], in_=wout[kc * 128:(kc + 1) * 128, :]).then_inc(ldw, 16)
                cw_.inc(DCH * 16)

                @block.sync
                def _(e, qb=qb):
                    e.wait_ge(mm, cm.v)
                    e.wait_ge(st, st_wsgn)
                    for kc in range(DCH):
                        e.dma_start(out=wsg_sb[:, kc, :],
                                    in_=wsgn_scr.ap()[kc * 128:(kc + 1) * 128, :]).then_inc(ldx, 16)
                cx.inc(DCH * 16)

                for dm in range(DCH):
                    ps = pO1 if dm % 2 == 0 else pO2
                    sl = dm % 2

                    @block.sync
                    def _(e, qb=qb, dm=dm, sl=sl):
                        e.wait_ge(vx, cv.v)
                        e.dma_start(out=x1s[:, 0, :],
                                    in_=x1_s.ap()[dm, :, qb * 512:(qb + 1) * 512]).then_inc(ldx, 16)
                    cx.inc(16)

                    @block.tensor
                    def _(t, dm=dm, ps=ps, first=(dm == 0)):
                        if first:
                            t.wait_ge(ldw, cw_.v)
                            t.wait_ge(ldx, cx.v)
                            t.wait_ge(vx, cv.v)
                        t.wait_ge(vx, cv.v)
                        for kc in range(DCH):
                            mi = t.matmul(ps[:, 0:512], wo_sb[:, kc, dm * 128:(dm + 1) * 128],
                                          attn[:, kc, :], start=(kc == 0), stop=False)
                        for kc in range(DCH):
                            mi = t.matmul(ps[:, 0:512], wsg_sb[:, kc, dm * 128:(dm + 1) * 128],
                                          xstr[:, kc, :], start=False, stop=(kc == DCH - 1))
                        mi.then_inc(mm, 1)
                    cm.inc()

                    @block.vector
                    def _(v, dm=dm, ps=ps, sl=sl):
                        v.wait_ge(mm, cm.v)
                        v.wait_ge(ldx, cx.v)
                        v.wait_ge(st, ct.v)
                        v.tensor_add(fino[:, 0, :], ps[:, 0:512], x1s[:, 0, :]).then_inc(vx, 1)
                    cv.inc()

                    @block.scalar
                    def _(s, qb=qb, dm=dm, sl=sl):
                        s.wait_ge(vx, cv.v)
                        s.dma_start(out=fin_o[dm, :, qb * 512:(qb + 1) * 512], in_=fino[:, 0, :]).then_inc(st, 16)
                    ct.inc(16)
                carry["pOmul"] = [cv.v, cv.v]

        @block.sync
        def _(e):
            e.wait_ge(st, ct.v)

    return nc



_NC_CACHE = {}


def _get_nc():
    if "fused" not in _NC_CACHE:
        _NC_CACHE["fused"] = build_fused()
    return _NC_CACHE["fused"]


def kernel(**inputs):
    x = np.ascontiguousarray(inputs["L0_attn_x"].astype(np.float32, copy=False))
    B, L, Dm = x.shape

    w_in = np.ascontiguousarray(inputs["L0_attn_in_proj_w"].T)
    convw = np.ascontiguousarray(inputs["L0_attn_conv_w"][:, 0, :])
    convb = np.ascontiguousarray(inputs["L0_attn_conv_b"].reshape(DIN, 1))
    w_xp = np.ascontiguousarray(inputs["L0_attn_x_proj_w"].T[:, 0:192])
    w_dt = np.ascontiguousarray(inputs["L0_attn_dt_proj_w"].T)
    dt_b = np.ascontiguousarray(inputs["L0_attn_dt_proj_b"].reshape(128, 1))
    a_log = np.ascontiguousarray(inputs["L0_attn_A_log"].reshape(128, 1))
    w_gate = np.ascontiguousarray(inputs["L0_ffn_gate_w"])
    w_sgu = np.ascontiguousarray(inputs["L0_ffn_shared_gate_up_w"])
    w_sd = np.ascontiguousarray(inputs["L0_ffn_shared_down_w"])
    w_sg = np.ascontiguousarray(inputs["L0_ffn_shared_gate_w"])
    cprojT = np.ascontiguousarray(inputs["L0_attn_C_proj_w"].T.astype(np.float32, copy=False))
    dvec = np.ascontiguousarray(inputs["L0_attn_D"].reshape(DIN, 1).astype(np.float32, copy=False))
    wop = np.ascontiguousarray(inputs["L0_attn_out_proj_w"].T.astype(np.float32, copy=False))
    utri = np.triu(np.ones((128, 128), np.float32), 1)
    iota64 = np.tile(np.arange(NE, dtype=np.float32), (128, 1))
    xfull = np.ascontiguousarray(x.reshape(B * L, Dm))
    ident = np.eye(128, dtype=np.float32)
    egu = inputs["L0_ffn_expert_gate_up_w"].astype(np.float32, copy=False)
    edw = inputs["L0_ffn_expert_down_w"].astype(np.float32, copy=False)
    wq_h = np.ascontiguousarray(inputs["L1_attn_Wq"].reshape(Dm, 1024).astype(np.float32, copy=False))
    wkv_h = np.ascontiguousarray(inputs["L1_attn_Wkv_down"].astype(np.float32, copy=False))
    wku_h = np.ascontiguousarray(inputs["L1_attn_Wk_up"].reshape(256, 1024).astype(np.float32, copy=False))
    wvu_h = np.ascontiguousarray(inputs["L1_attn_Wv_up"].reshape(256, 1024).astype(np.float32, copy=False))
    wout_h = np.ascontiguousarray(inputs["L1_attn_Wout"].reshape(1024, Dm).astype(np.float32, copy=False))
    wffn_h = np.ascontiguousarray(inputs["L1_ffn_weight"].T.astype(np.float32, copy=False))
    qq = np.arange(512)[None, :]
    kk = np.arange(128)[:, None]
    cmask = np.ascontiguousarray(np.stack(
        [np.where(qq - kk >= 128 * j, 0.0, -1e30).astype(np.float32) for j in range(4)], axis=1))
    p128 = np.arange(128, dtype=np.int32)

    maps = []
    for c in range(NCORES):
        b, h = c // 2, c % 2
        s0 = h * TPC
        my0 = c * TPC
        xb = x[b]
        xTh = np.zeros((Dm, TPC + HALO), np.float32)
        lo = s0 - HALO
        src0 = max(lo, 0)
        xTh[:, HALO - (s0 - src0):] = xb[src0:s0 + TPC].T
        xTv = np.ascontiguousarray(xb[s0:s0 + TPC].T)
        prefv = np.zeros((8, 1), np.float32)
        prefv[:c] = 1.0
        ptid = (64 * np.arange(8, dtype=np.int32)[None, :] + 8 * c
                + np.arange(8, dtype=np.int32)[:, None]).astype(np.int32)
        xsix = (h * 1024 + 128 * np.arange(DCH, dtype=np.int32)[None, :]
                + p128[:, None]).astype(np.int32)
        maps.append(dict(
            xTh=np.ascontiguousarray(xTh), w_in=w_in, convw=convw, convb=convb,
            w_xp=w_xp, w_dt=w_dt, dt_b=dt_b, a_log=a_log, w_gate=w_gate,
            w_sgu=w_sgu, w_sd=w_sd, w_sg=w_sg,
            hlidx_in=(h * 128 + p128).reshape(128, 1).astype(np.int32),
            cprojT=cprojT, dvec=dvec, wop=wop, utri=utri, iota64=iota64,
            tokid=(np.arange(128, dtype=np.float32) + my0).reshape(128, 1),
            prefvec=prefv,
            oddcol=np.full((128, 1), float(h), np.float32),
            xfull=xfull,
            egu8=np.ascontiguousarray(egu[c * 8:(c + 1) * 8]),
            edw8=np.ascontiguousarray(edw[c * 8:(c + 1) * 8]),
            ident=ident, ptidx=ptid,
            xTv=xTv, xsidx=xsix,
            wq=wq_h, wkv=wkv_h, wku=wku_h, wvu=wvu_h, wout=wout_h, wffn=wffn_h,
            cmask=cmask,
            evenfl_in=np.full((128, 1), 1.0 - h, np.float32),
            oddfl_in=np.full((128, 1), float(h), np.float32),
            killc_in=np.full((128, 1), -1e30 if h == 0 else 0.0, np.float32),
        ))

    nc = _get_nc()
    res = run_bass_kernel_spmd(nc, maps, core_ids=list(range(NCORES))).results

    out = np.empty((B, L, Dm), np.float32)
    for c in range(NCORES):
        b, h = c // 2, c % 2
        s0 = h * TPC
        out[b, s0:s0 + TPC, :] = res[c]["fin_o"].reshape(Dm, TPC).T
    return out


# version 15 (final)
# speedup vs baseline: 1.1582x; 1.1582x over previous
from contextlib import ExitStack

import numpy as np
import ml_dtypes
import concourse.bass as bass
import concourse.mybir as mybir
from concourse.bass_utils import run_bass_kernel_spmd
from concourse.bass_core_rust import IndirectOffsetOnAxis

F32 = mybir.dt.float32
BF16 = mybir.dt.bfloat16
I32 = mybir.dt.int32
U32 = mybir.dt.uint32
AF = mybir.ActivationFunctionType
ALU = mybir.AluOpType
AX = mybir.AxisListType

NCORES = 8
TPC = 2048
HALO = 3
D = 1024
DCH = D // 128
DIN = 2048
ICH = DIN // 128
NE = 64
CAP = 512
TOPK = 8
W = TPC + HALO
EPC = 8


class Ctr:
    def __init__(self):
        self.v = 0

    def inc(self, n=1):
        self.v += n
        return self.v


def build_fused():
    nc = bass.Bass("TRN2", target_bir_lowering=False)

    xTh = nc.dram_tensor("xTh", [D, W], F32, kind="ExternalInput").ap()
    w_in = nc.dram_tensor("w_in", [D, 2 * DIN], F32, kind="ExternalInput").ap()
    convw = nc.dram_tensor("convw", [DIN, 4], F32, kind="ExternalInput").ap()
    convb = nc.dram_tensor("convb", [DIN, 1], F32, kind="ExternalInput").ap()
    w_xp = nc.dram_tensor("w_xp", [DIN, 192], F32, kind="ExternalInput").ap()
    w_dt = nc.dram_tensor("w_dt", [64, 128], F32, kind="ExternalInput").ap()
    dt_b = nc.dram_tensor("dt_b", [128, 1], F32, kind="ExternalInput").ap()
    a_log = nc.dram_tensor("a_log", [128, 1], F32, kind="ExternalInput").ap()
    w_gate = nc.dram_tensor("w_gate", [D, NE], F32, kind="ExternalInput").ap()
    w_sgu = nc.dram_tensor("w_sgu", [D, 4096], F32, kind="ExternalInput").ap()
    w_sd = nc.dram_tensor("w_sd", [DIN, D], F32, kind="ExternalInput").ap()
    w_sg = nc.dram_tensor("w_sg", [D, 1], F32, kind="ExternalInput").ap()
    hlidx_in = nc.dram_tensor("hlidx_in", [128, 1], I32, kind="ExternalInput").ap()

    cprojT = nc.dram_tensor("cprojT", [128, DIN], F32, kind="ExternalInput").ap()
    dvec = nc.dram_tensor("dvec", [DIN, 1], F32, kind="ExternalInput").ap()
    wop = nc.dram_tensor("wop", [DIN, D], F32, kind="ExternalInput").ap()
    utri = nc.dram_tensor("utri", [128, 128], F32, kind="ExternalInput").ap()
    iota64 = nc.dram_tensor("iota64", [128, NE], F32, kind="ExternalInput").ap()
    tokid = nc.dram_tensor("tokid", [128, 1], F32, kind="ExternalInput").ap()
    prefvec = nc.dram_tensor("prefvec", [8, 1], F32, kind="ExternalInput").ap()
    oddcol = nc.dram_tensor("oddcol", [128, 1], F32, kind="ExternalInput").ap()

    xfull = nc.dram_tensor("xfull", [16384, D], F32, kind="ExternalInput").ap()
    egu8 = nc.dram_tensor("egu8", [EPC, D, 1024], F32, kind="ExternalInput").ap()
    edw8 = nc.dram_tensor("edw8", [EPC, 512, 1024], F32, kind="ExternalInput").ap()
    ident = nc.dram_tensor("ident", [128, 128], F32, kind="ExternalInput").ap()
    ptidx = nc.dram_tensor("ptidx", [8, 8], I32, kind="ExternalInput").ap()

    xTv = nc.dram_tensor("xTv", [D, TPC], F32, kind="ExternalInput").ap()
    xsidx = nc.dram_tensor("xsidx", [128, DCH], I32, kind="ExternalInput").ap()

    wq = nc.dram_tensor("wq", [D, 1024], F32, kind="ExternalInput").ap()
    wkv = nc.dram_tensor("wkv", [D, 256], F32, kind="ExternalInput").ap()
    wku = nc.dram_tensor("wku", [256, 1024], F32, kind="ExternalInput").ap()
    wvu = nc.dram_tensor("wvu", [256, 1024], F32, kind="ExternalInput").ap()
    wout = nc.dram_tensor("wout", [1024, D], F32, kind="ExternalInput").ap()
    wffn = nc.dram_tensor("wffn", [D, D], F32, kind="ExternalInput").ap()
    cmask = nc.dram_tensor("cmask", [128, 4, 512], F32, kind="ExternalInput").ap()
    evenfl_in = nc.dram_tensor("evenfl_in", [128, 1], F32, kind="ExternalInput").ap()
    oddfl_in = nc.dram_tensor("oddfl_in", [128, 1], F32, kind="ExternalInput").ap()
    killc_in = nc.dram_tensor("killc_in", [128, 1], F32, kind="ExternalInput").ap()

    fin_o = nc.dram_tensor("fin_o", [DCH, 128, TPC], F32, kind="ExternalOutput").ap()

    xc_s = nc.dram_tensor("xc_s", [ICH, 128, TPC], BF16)
    z_s = nc.dram_tensor("z_s", [ICH, 128, TPC], BF16)
    hs_s = nc.dram_tensor("hs_s", [128, TPC], F32)
    acum_s = nc.dram_tensor("acum_s", [128, TPC], F32)
    mask_s = nc.dram_tensor("mask_s", [TPC, NE], F32)
    idx8_s = nc.dram_tensor("idx8_s", [TPC, TOPK], U32)
    w8_s = nc.dram_tensor("w8_s", [TPC, TOPK], F32)
    ssh_s = nc.dram_tensor("ssh_s", [DCH, 128, TPC], F32)
    cnt_i = nc.dram_tensor("cnt_i", [1, NE], F32)
    cnt_all = nc.dram_tensor("cnt_all", [8, NE], F32)
    mo_s = nc.dram_tensor("mo_s", [DCH, 128, TPC], F32)
    pt_my = nc.dram_tensor("pt_my", [NE, 513], F32)
    pt_all = nc.dram_tensor("pt_all", [8 * NE, 513], F32)
    gi_s = nc.dram_tensor("gi_s", [TPC, TOPK], F32)
    wk_s = nc.dram_tensor("wk_s", [TPC, TOPK], F32)
    oe_my = nc.dram_tensor("oe_my", [EPC * 513, 1024], BF16)
    oe_all = nc.dram_tensor("oe_all", [NE * 513, 1024], BF16)
    ym_s = nc.dram_tensor("ym_s", [TPC, 1024], F32)
    x1_s = nc.dram_tensor("x1_s", [DCH, 128, TPC], F32)
    xn1_mine = nc.dram_tensor("xn1_mine", [DCH, 128, TPC], BF16)
    wsgn_scr = nc.dram_tensor("wsgn_scr", [D, D], BF16)
    bar_i = nc.dram_tensor("bar_i", [1, 4], F32)
    bar_o = nc.dram_tensor("bar_o", [8, 4], F32)

    hl_sh = nc.dram_tensor("hl_sh", [256, 1], F32, addr_space="Shared")
    xn1_sh = nc.dram_tensor("xn1_sh", [2048, TPC], BF16, addr_space="Shared")

    wsel = w_in.rearrange("(c p) f -> c p f", p=128)
    wsgu = w_sgu.rearrange("(c p) f -> c p f", p=128)
    wsdr = w_sd.rearrange("(c p) f -> c p f", p=128)
    pt_flat = pt_my.ap().rearrange("a (b c) -> (a b) c", c=1)

    with ExitStack() as top:
        ldx = top.enter_context(nc.semaphore("ldx"))
        ldw = top.enter_context(nc.semaphore("ldw"))
        mm = top.enter_context(nc.semaphore("mm"))
        vx = top.enter_context(nc.semaphore("vx"))
        sx = top.enter_context(nc.semaphore("sx"))
        st = top.enter_context(nc.semaphore("st"))
        cc0 = top.enter_context(nc.semaphore("cc0"))
        cc1 = top.enter_context(nc.semaphore("cc1"))
        cc2 = top.enter_context(nc.semaphore("cc2"))
        cc3 = top.enter_context(nc.semaphore("cc3"))
        block = top.enter_context(nc.Block())
        cx, cw_, cm, cv, cs, ct = Ctr(), Ctr(), Ctr(), Ctr(), Ctr(), Ctr()

        def barrier():
            vals = [(ldx, cx), (ldw, cw_), (mm, cm), (vx, cv), (sx, cs), (st, ct)]
            for sec in (block.sync, block.tensor, block.vector, block.scalar, block.gpsimd):
                @sec
                def _(e):
                    for s_, c_ in vals:
                        e.wait_ge(s_, c_.v)

        with ExitStack() as _ctx:
            xs = _ctx.enter_context(nc.sbuf_tensor("xs", [128, DCH, 512], F32))
            sqf = _ctx.enter_context(nc.sbuf_tensor("sqf", [128, DCH, 512], F32))
            xn = _ctx.enter_context(nc.sbuf_tensor("xn", [128, DCH, W], BF16))
            rows = _ctx.enter_context(nc.sbuf_tensor("rows", [1, 8, 512], F32))
            u3 = _ctx.enter_context(nc.sbuf_tensor("u3", [128, 4, 512], F32))
            bR = u3[:, 0, :]
            bM = u3[:, 1, :]
            ones = _ctx.enter_context(nc.sbuf_tensor("ones", [128, 1], F32))
            epst = _ctx.enter_context(nc.sbuf_tensor("epst", [1, 1], F32))
            onerow = _ctx.enter_context(nc.sbuf_tensor("onerow", [1, 128], F32))
            xin = _ctx.enter_context(nc.sbuf_tensor("xin", [128, ICH, W], BF16))
            xcf = _ctx.enter_context(nc.sbuf_tensor("xcf", [128, 2, 512], F32))
            xcb = _ctx.enter_context(nc.sbuf_tensor("xcb", [128, ICH, 512], BF16))
            zb = _ctx.enter_context(nc.sbuf_tensor("zb", [128, 2, 512], BF16))
            wflat = _ctx.enter_context(nc.sbuf_tensor("wflat", [128, 4096], BF16))
            wbuf = wflat[:].rearrange("p (a b c) -> p a b c", a=DCH, b=4, c=128)
            wdbuf = wflat[:].rearrange("p (a b c) -> p a b c", a=ICH, b=2, c=128)
            wxpb = _ctx.enter_context(nc.sbuf_tensor("wxpb", [128, ICH, 192], BF16))
            wdtb = _ctx.enter_context(nc.sbuf_tensor("wdtb", [64, 128], BF16))
            cwb = _ctx.enter_context(nc.sbuf_tensor("cwb", [128, ICH, 4], F32))
            cbb = _ctx.enter_context(nc.sbuf_tensor("cbb", [128, ICH], F32))
            dtbt = _ctx.enter_context(nc.sbuf_tensor("dtbt", [128, 1], F32))
            acol = _ctx.enter_context(nc.sbuf_tensor("acol", [128, 1], F32))
            ralog = _ctx.enter_context(nc.sbuf_tensor("ralog", [128, 1], F32))
            recipa = _ctx.enter_context(nc.sbuf_tensor("recipa", [128, 1], F32))
            sc1 = _ctx.enter_context(nc.sbuf_tensor("sc1", [128, 1], F32))
            sc2 = _ctx.enter_context(nc.sbuf_tensor("sc2", [128, 1], F32))
            sc3 = _ctx.enter_context(nc.sbuf_tensor("sc3", [128, 1], F32))
            hlix = _ctx.enter_context(nc.sbuf_tensor("hlix", [128, 1], I32))
            dt_bf = _ctx.enter_context(nc.sbuf_tensor("dt_bf", [64, 512], BF16))
            bmf = _ctx.enter_context(nc.sbuf_tensor("bmf", [128, 512], F32))
            t1 = _ctx.enter_context(nc.sbuf_tensor("t1", [128, 512], F32))
            t2 = _ctx.enter_context(nc.sbuf_tensor("t2", [128, 512], F32))
            t3 = _ctx.enter_context(nc.sbuf_tensor("t3", [128, 512], F32))
            Ab = _ctx.enter_context(nc.sbuf_tensor("Ab", [128, 4, 512], F32))
            Bb = u3
            hsb = xs[:].rearrange("p a b -> p (a b)")[:, 0:TPC]
            acb = sqf[:].rearrange("p a b -> p (a b)")
            wgb = _ctx.enter_context(nc.sbuf_tensor("wgb", [128, DCH, NE], BF16))
            expt = t2[:, 0:64]
            mrt = t2[:, 64:128]
            maskt = t2[:, 128:192]
            m8 = t2[:, 192:200]
            m8b = t2[:, 200:208]
            i8 = _ctx.enter_context(nc.sbuf_tensor("i8", [128, 8], U32))
            rsum = t2[:, 216:217]
            rrec = t2[:, 217:218]
            w8t = t2[:, 208:216]
            nmax = t2[:, 218:219]
            wsgb = _ctx.enter_context(nc.sbuf_tensor("wsgb", [128, DCH], BF16))
            sgb = zb[:, 0, :]
            sgate = u3[:, 0, :]
            sshb = u3[:, 1:3, :]
            pA = _ctx.enter_context(nc.psum_tensor("pA", [128, 512], F32))
            pB = _ctx.enter_context(nc.psum_tensor("pB", [128, 512], F32))
            pC = _ctx.enter_context(nc.psum_tensor("pC", [128, 512], F32))
            pD = _ctx.enter_context(nc.psum_tensor("pD", [128, 512], F32))
            pR1 = _ctx.enter_context(nc.psum_tensor("pR1", [128, 512], F32))
            pR2 = _ctx.enter_context(nc.psum_tensor("pR2", [128, 512], F32))
            pBC = _ctx.enter_context(nc.psum_tensor("pBC", [128, 512], F32))
            pS = _ctx.enter_context(nc.psum_tensor("pS", [128, 512], F32))

            @block.gpsimd
            def _(g):
                for kc in range(ICH):
                    g.dma_start(out=wxpb[:, kc, :], in_=w_xp[kc * 128:(kc + 1) * 128, :]).then_inc(ldw, 16)
                    g.dma_start(out=cwb[:, kc, :], in_=convw[kc * 128:(kc + 1) * 128, :]).then_inc(ldw, 16)
                    g.dma_start(out=cbb[:, kc:kc + 1], in_=convb[kc * 128:(kc + 1) * 128, :]).then_inc(ldw, 16)
                for kc in range(DCH):
                    g.dma_start(out=wgb[:, kc, :], in_=w_gate[kc * 128:(kc + 1) * 128, :]).then_inc(ldw, 16)
                    g.dma_start(out=wsgb[:, kc:kc + 1], in_=w_sg[kc * 128:(kc + 1) * 128, :]).then_inc(ldw, 16)
                g.dma_start(out=wdtb[:], in_=w_dt).then_inc(ldw, 16)
                g.dma_start(out=dtbt[:], in_=dt_b).then_inc(ldw, 16)
                g.dma_start(out=ralog[:], in_=a_log).then_inc(ldw, 16)
                g.dma_start(out=hlix[:], in_=hlidx_in).then_inc(ldw, 16)
            cw_.inc((ICH * 3 + DCH * 2 + 4) * 16)

            @block.vector
            def _(v):
                v.memset(ones[:], 1.0).then_inc(vx, 1)
                v.memset(onerow[:], 1.0).then_inc(vx, 1)
                v.memset(epst[:], 1e-5).then_inc(vx, 1)
            cv.inc(3)

            @block.scalar
            def _(s):
                s.wait_ge(ldw, cw_.v)
                s.activation(sc1[:], ralog[:], AF.Exp, scale=-1.0).then_inc(sx, 1)
                s.activation(sc2[:], ralog[:], AF.Exp).then_inc(sx, 1)
            cs.inc(2)

            @block.vector
            def _(v):
                v.wait_ge(sx, cs.v)
                v.wait_ge(vx, cv.v)
                v.tensor_scalar_mul(acol[:], sc2[:], -1.0).then_inc(vx, 1)
                v.tensor_single_scalar(sc3[:], sc2[:], 1e-8, ALU.is_gt).then_inc(vx, 1)
                v.wait_ge(vx, cv.v + 2)
                v.tensor_scalar_mul(sc2[:], sc1[:], -1.0).then_inc(vx, 1)
                v.wait_ge(vx, cv.v + 3)
                v.tensor_mul(sc1[:], sc2[:], sc3[:]).then_inc(vx, 1)
                v.tensor_sub(sc3[:], ones[:], sc3[:]).then_inc(vx, 1)
                v.wait_ge(vx, cv.v + 5)
                v.tensor_add(recipa[:], sc1[:], sc3[:]).then_inc(vx, 1)
            cv.inc(6)

            groups = [(k * 512, 512) for k in range(4)] + [(2048, HALO)]
            for g0, gw in groups:
                @block.sync
                def _(e, g0=g0, gw=gw):
                    e.wait_ge(vx, cv.v)
                    for kc in range(DCH):
                        e.dma_start(out=xs[:, kc, 0:gw], in_=xTh[kc * 128:(kc + 1) * 128, g0:g0 + gw]).then_inc(ldx, 16)
                cx.inc(DCH * 16)

                @block.tensor
                def _(t, gw=gw):
                    t.wait_ge(ldx, cx.v)
                    for kc in range(DCH):
                        mi = t.matmul(pR1[0:1, 0:gw], ones[:], xs[:, kc, 0:gw], start=(kc == 0), stop=(kc == DCH - 1))
                    mi.then_inc(mm, 1)
                cm.inc()

                @block.scalar
                def _(s, gw=gw):
                    s.wait_ge(ldx, cx.v)
                    s.wait_ge(vx, cv.v)
                    for kc in range(DCH):
                        s.activation(sqf[:, kc, 0:gw], xs[:, kc, 0:gw], AF.Square).then_inc(sx, 1)
                cs.inc(DCH)

                @block.tensor
                def _(t, gw=gw):
                    t.wait_ge(sx, cs.v)
                    for kc in range(DCH):
                        mi = t.matmul(pR2[0:1, 0:gw], ones[:], sqf[:, kc, 0:gw], start=(kc == 0), stop=(kc == DCH - 1))
                    mi.then_inc(mm, 1)
                cm.inc()

                @block.vector
                def _(v, g0=g0, gw=gw):
                    v.wait_ge(mm, cm.v)
                    v.tensor_scalar_mul(rows[0:1, 0, 0:gw], pR1[0:1, 0:gw], 1.0 / D).then_inc(vx, 1)
                    v.tensor_scalar_mul(rows[0:1, 1, 0:gw], pR2[0:1, 0:gw], 1.0 / D).then_inc(vx, 1)
                    v.wait_ge(vx, cv.v + 2)
                    v.tensor_mul(rows[0:1, 5, 0:gw], rows[0:1, 0, 0:gw],
                                 rows[0:1, 0, 0:gw]).then_inc(vx, 1)
                    v.wait_ge(vx, cv.v + 3)
                    v.tensor_sub(rows[0:1, 1, 0:gw], rows[0:1, 1, 0:gw],
                                 rows[0:1, 5, 0:gw]).then_inc(vx, 1)
                cv.inc(4)

                @block.scalar
                def _(s, g0=g0, gw=gw):
                    s.wait_ge(vx, cv.v)
                    s.activation(rows[0:1, 6, 0:gw], rows[0:1, 1, 0:gw], AF.Ln, bias=epst[:]).then_inc(sx, 1)
                    s.wait_ge(sx, cs.v + 1)
                    s.activation(rows[0:1, 2, 0:gw], rows[0:1, 6, 0:gw], AF.Exp, scale=-0.5).then_inc(sx, 1)
                cs.inc(2)

                @block.vector
                def _(v, g0=g0, gw=gw):
                    v.wait_ge(sx, cs.v)
                    v.tensor_mul(rows[0:1, 3, 0:gw], rows[0:1, 0, 0:gw],
                                 rows[0:1, 2, 0:gw]).then_inc(vx, 1)
                cv.inc()

                @block.tensor
                def _(t, g0=g0, gw=gw):
                    t.wait_ge(vx, cv.v)
                    mi = t.matmul(pBC[:, 0:gw], onerow[:], rows[0:1, 2, 0:gw], start=True, stop=True)
                    mi = t.matmul(pS[:, 0:gw], onerow[:], rows[0:1, 3, 0:gw], start=True, stop=True)
                    mi.then_inc(mm, 1)
                cm.inc()

                @block.vector
                def _(v, g0=g0, gw=gw):
                    v.wait_ge(mm, cm.v)
                    v.tensor_copy(bR[:, 0:gw], pBC[:, 0:gw]).then_inc(vx, 1)
                    v.tensor_copy(bM[:, 0:gw], pS[:, 0:gw]).then_inc(vx, 1)
                    v.wait_ge(vx, cv.v + 2)
                    for kc in range(DCH):
                        v.tensor_mul(sqf[:, kc, 0:gw], xs[:, kc, 0:gw], bR[:, 0:gw]).then_inc(vx, 1)
                    v.wait_ge(vx, cv.v + 2 + DCH)
                    for kc in range(DCH):
                        v.tensor_sub(xn[:, kc, g0:g0 + gw], sqf[:, kc, 0:gw], bM[:, 0:gw]).then_inc(vx, 1)
                cv.inc(2 + 2 * DCH)

            def stream_w(g, src_chunks, slot, nchunks=DCH):
                for kc in range(nchunks):
                    g.dma_start(out=wbuf[:, kc, slot, :], in_=src_chunks(kc)).then_inc(ldw, 16)

            mm_at_wslot = {0: cm.v, 1: cm.v, 2: cm.v, 3: cm.v}
            banksA = [None, None, None, None]
            evacA = {0: cv.v, 1: cv.v, 2: cv.v, 3: cv.v}
            gctr = [0]
            for fi in range(16):
                slot = fi % 2

                @block.gpsimd
                def _(g, fi=fi, slot=slot):
                    g.wait_ge(mm, mm_at_wslot[slot])
                    stream_w(g, lambda kc, fi=fi: wsel[kc, :, fi * 128:(fi + 1) * 128], slot)
                cw_.inc(DCH * 16)

                for gi, (g0, gw) in enumerate(groups):
                    bank = gctr[0] % 4
                    gctr[0] += 1
                    ps = (pA, pB, pC, pD)[bank]

                    @block.tensor
                    def _(t, fi=fi, slot=slot, g0=g0, gw=gw, ps=ps, first=(gi == 0), w_ev=evacA[bank]):
                        if first:
                            t.wait_ge(ldw, cw_.v)
                        t.wait_ge(vx, w_ev)
                        for kc in range(DCH):
                            mi = t.matmul(ps[:, 0:gw], wbuf[:, kc, slot, :], xn[:, kc, g0:g0 + gw],
                                     start=(kc == 0), stop=(kc == DCH - 1))
                        mi.then_inc(mm, 1)
                    cm.inc()
                    mm_at_wslot[slot] = cm.v

                    @block.vector
                    def _(v, fi=fi, g0=g0, gw=gw, ps=ps):
                        v.wait_ge(mm, cm.v)
                        v.tensor_copy(xin[:, fi, g0:g0 + gw], ps[:, 0:gw]).then_inc(vx, 1)
                    cv.inc()
                    evacA[bank] = cv.v

            zgroups = [(HALO + k * 512, 512) for k in range(4)]
            for fi in range(16, 32):
                slot = fi % 2

                @block.gpsimd
                def _(g, fi=fi, slot=slot):
                    g.wait_ge(mm, mm_at_wslot[slot])
                    stream_w(g, lambda kc, fi=fi: wsel[kc, :, fi * 128:(fi + 1) * 128], slot)
                cw_.inc(DCH * 16)

                for gi, (g0, gw) in enumerate(zgroups):
                    bank = gctr[0] % 4
                    gctr[0] += 1
                    ps = (pA, pB, pC, pD)[bank]
                    zslot = gi % 2

                    @block.tensor
                    def _(t, slot=slot, g0=g0, gw=gw, ps=ps, first=(gi == 0), w_ev=evacA[bank]):
                        if first:
                            t.wait_ge(ldw, cw_.v)
                        t.wait_ge(st, ct.v)
                        t.wait_ge(vx, w_ev)
                        for kc in range(DCH):
                            mi = t.matmul(ps[:, 0:gw], wbuf[:, kc, slot, :], xn[:, kc, g0:g0 + gw],
                                     start=(kc == 0), stop=(kc == DCH - 1))
                        mi.then_inc(mm, 1)
                    cm.inc()
                    mm_at_wslot[slot] = cm.v

                    @block.vector
                    def _(v, g0=g0, gw=gw, ps=ps, zslot=zslot):
                        v.wait_ge(mm, cm.v)
                        v.tensor_copy(zb[:, zslot, :], ps[:, 0:gw]).then_inc(vx, 1)
                    cv.inc()
                    evacA[bank] = cv.v

                    @block.scalar
                    def _(s, fi=fi, g0=g0, zslot=zslot):
                        s.wait_ge(vx, cv.v)
                        s.dma_start(out=z_s.ap()[fi - 16, :, g0 - HALO:g0 - HALO + 512],
                                    in_=zb[:, zslot, :]).then_inc(st, 16)
                        s.nop().then_inc(sx, 1)
                    cs.inc()
                    ct.inc(16)

            for blk in range(4):
                base = blk * 512

                for cc in range(ICH):
                    pp = cc % 2

                    @block.vector
                    def _(v, cc=cc, pp=pp, base=base, blk=blk):
                        if blk > 0 and cc == 0:
                            v.wait_ge(mm, cm.v)
                            v.wait_ge(st, ct.v)
                        v.tensor_scalar(xcf[:, pp, :], xin[:, cc, base + 3:base + 515],
                                        cwb[:, cc, 3:4], cbb[:, cc:cc + 1], ALU.mult, ALU.add).then_inc(vx, 1)
                        v.wait_ge(vx, cv.v + 1)
                        v.scalar_tensor_tensor(xcf[:, pp, :], xin[:, cc, base + 2:base + 514],
                                               cwb[:, cc, 2:3], xcf[:, pp, :], ALU.mult, ALU.add).then_inc(vx, 1)
                        v.wait_ge(vx, cv.v + 2)
                        v.scalar_tensor_tensor(xcf[:, pp, :], xin[:, cc, base + 1:base + 513],
                                               cwb[:, cc, 1:2], xcf[:, pp, :], ALU.mult, ALU.add).then_inc(vx, 1)
                        v.wait_ge(vx, cv.v + 3)
                        v.scalar_tensor_tensor(xcf[:, pp, :], xin[:, cc, base:base + 512],
                                               cwb[:, cc, 0:1], xcf[:, pp, :], ALU.mult, ALU.add).then_inc(vx, 1)
                        v.wait_ge(vx, cv.v + 4)
                        v.tensor_copy(xcb[:, cc, :], xcf[:, pp, :]).then_inc(vx, 1)
                    cv.inc(5)

                @block.scalar
                def _(s, blk=blk):
                    s.wait_ge(vx, cv.v)
                    for cc in range(ICH):
                        s.dma_start(out=xc_s.ap()[cc, :, blk * 512:blk * 512 + 512], in_=xcb[:, cc, :]).then_inc(st, 16)
                    s.nop().then_inc(sx, 1)
                cs.inc()
                ct.inc(ICH * 16)

                @block.tensor
                def _(t, blk=blk):
                    t.wait_ge(vx, cv.v)
                    for kc in range(ICH):
                        mi = t.matmul(pC[0:64, 0:512], wxpb[:, kc, 0:64], xcb[:, kc, :],
                                 start=(kc == 0), stop=(kc == ICH - 1))
                    mi.then_inc(mm, 1)
                    for kc in range(ICH):
                        mi = t.matmul(pD[:, 0:512], wxpb[:, kc, 64:192], xcb[:, kc, :],
                                 start=(kc == 0), stop=(kc == ICH - 1))
                    mi.then_inc(mm, 1)
                cm.inc(2)

                @block.vector
                def _(v, blk=blk):
                    v.wait_ge(mm, cm.v)
                    v.tensor_copy(dt_bf[:, :], pC[0:64, 0:512]).then_inc(vx, 1)
                    v.tensor_copy(bmf[:, :], pD[:, 0:512]).then_inc(vx, 1)
                cv.inc(2)

                @block.tensor
                def _(t, blk=blk):
                    t.wait_ge(vx, cv.v)
                    mi = t.matmul(pC[:, 0:512], wdtb[:], dt_bf[:, :], start=True, stop=True)
                    mi.then_inc(mm, 1)
                cm.inc()

                @block.scalar
                def _(s, blk=blk):
                    s.wait_ge(mm, cm.v)
                    s.activation(t1[:, :], pC[:, 0:512], AF.Identity, bias=dtbt[:]).then_inc(sx, 1)
                    s.wait_ge(sx, cs.v + 1)
                    s.activation(t2[:, :], t1[:, :], AF.Abs).then_inc(sx, 1)
                    s.wait_ge(sx, cs.v + 2)
                    s.activation(t3[:, :], t2[:, :], AF.Exp, scale=-1.0).then_inc(sx, 1)
                    s.wait_ge(sx, cs.v + 3)
                    s.activation(t2[:, :], t3[:, :], AF.Ln, bias=1.0).then_inc(sx, 1)
                cs.inc(4)

                @block.vector
                def _(v, blk=blk):
                    v.wait_ge(sx, cs.v)
                    v.tensor_scalar_max(t3[:, :], t1[:, :], 0.0).then_inc(vx, 1)
                    v.wait_ge(vx, cv.v + 1)
                    v.tensor_add(t1[:, :], t2[:, :], t3[:, :]).then_inc(vx, 1)
                    v.wait_ge(vx, cv.v + 2)
                    v.tensor_scalar(t1[:, :], t1[:, :], 0.001, 0.1, ALU.max, ALU.min).then_inc(vx, 1)
                    v.wait_ge(vx, cv.v + 3)
                    v.tensor_scalar(t2[:, :], t1[:, :], acol[:], None, ALU.mult).then_inc(vx, 1)
                cv.inc(4)

                @block.scalar
                def _(s, blk=blk):
                    s.wait_ge(vx, cv.v)
                    s.activation(Ab[:, blk, :], t2[:, :], AF.Exp).then_inc(sx, 1)
                cs.inc()

                @block.vector
                def _(v, blk=blk):
                    v.wait_ge(sx, cs.v)
                    v.tensor_scalar(t2[:, :], Ab[:, blk, :], 1.0, recipa[:], ALU.subtract, ALU.mult).then_inc(vx, 1)
                    v.wait_ge(vx, cv.v + 1)
                    v.tensor_mul(Bb[:, blk, :], t2[:, :], bmf[:, :]).then_inc(vx, 1)
                cv.inc(2)

            @block.vector
            def _(v):
                v.wait_ge(vx, cv.v)
                v.memset(t1[:], 0.0).then_inc(vx, 1)
                v.wait_ge(vx, cv.v + 1)
                v.tensor_tensor_scan(hsb, Ab[:].rearrange("p a b -> p (a b)"),
                                     Bb[:].rearrange("p a b -> p (a b)"), 0.0, ALU.mult, ALU.add).then_inc(vx, 1)
                for blkz in range(4):
                    v.wait_ge(vx, cv.v + 2 + blkz)
                    v.tensor_tensor_scan(acb[:, blkz * 512:(blkz + 1) * 512], Ab[:, blkz, :],
                                         t1[:] if blkz == 0 else t1[:],
                                         1.0 if blkz == 0 else acb[:, blkz * 512 - 1:blkz * 512],
                                         ALU.mult, ALU.add).then_inc(vx, 1)
            cv.inc(6)

            @block.scalar
            def _(s):
                s.wait_ge(vx, cv.v)
                s.dma_start(out=hs_s.ap(), in_=hsb).then_inc(st, 16)
                s.dma_start(out=acum_s.ap(), in_=acb[:, 0:TPC]).then_inc(st, 16)
            ct.inc(32)

            @block.gpsimd
            def _(g):
                g.wait_ge(vx, cv.v)
                g.indirect_dma_start(
                    out=hl_sh.ap(), out_offset=IndirectOffsetOnAxis(hlix[:, 0:1], 0),
                    in_=hsb[:, TPC - 1:TPC], in_offset=None,
                ).then_inc(st, 16)
            ct.inc(16)

            for ti in range(TPC // 128):
                col = HALO + ti * 128

                @block.tensor
                def _(t, col=col):
                    t.wait_ge(st, ct.v)
                    for kc in range(DCH):
                        mi = t.matmul(pD[:, 0:NE], xn[:, kc, col:col + 128], wgb[:, kc, :],
                                 start=(kc == 0), stop=(kc == DCH - 1))
                    mi.then_inc(mm, 1)
                cm.inc()

                @block.vector
                def _(v):
                    v.wait_ge(mm, cm.v)
                    v.reduce_max(nmax, pD[:, 0:NE], axis=AX.X, negate=True).then_inc(vx, 1)
                cv.inc()

                @block.scalar
                def _(s):
                    s.wait_ge(vx, cv.v)
                    s.activation(expt, pD[:, 0:NE], AF.Exp, bias=nmax).then_inc(sx, 1)
                cs.inc()

                @block.vector
                def _(v):
                    v.wait_ge(sx, cs.v)
                    v.max(m8, expt).then_inc(vx, 1)
                cv.inc()

                @block.scalar
                def _(s):
                    s.wait_ge(vx, cv.v)
                    s.copy(m8b, m8).then_inc(sx, 1)
                cs.inc()

                @block.vector
                def _(v):
                    v.wait_ge(sx, cs.v)
                    v.match_replace(mrt, m8b, expt, 0.0).then_inc(vx, 1)
                    v.max_index(i8[:], m8b, expt).then_inc(vx, 1)
                    v.reduce_sum(rsum, m8, axis=AX.X).then_inc(vx, 1)
                    v.wait_ge(vx, cv.v + 3)
                    v.tensor_tensor(maskt, expt, mrt, ALU.not_equal).then_inc(vx, 1)
                    v.reciprocal(rrec, rsum).then_inc(vx, 1)
                    v.wait_ge(vx, cv.v + 5)
                    v.tensor_scalar(w8t, m8, rrec, None, ALU.mult).then_inc(vx, 1)
                cv.inc(6)

                @block.tensor
                def _(t, ti=ti):
                    t.wait_ge(vx, cv.v)
                    mi = t.matmul(pR1[0:1, 0:NE], ones[:], maskt, start=(ti == 0), stop=(ti == 15))
                    mi.then_inc(mm, 1)
                cm.inc()

                @block.scalar
                def _(s, ti=ti):
                    s.wait_ge(vx, cv.v)
                    s.dma_start(out=mask_s.ap()[ti * 128:(ti + 1) * 128, :], in_=maskt).then_inc(st, 16)
                    s.dma_start(out=idx8_s.ap()[ti * 128:(ti + 1) * 128, :], in_=i8[:]).then_inc(st, 16)
                    s.dma_start(out=w8_s.ap()[ti * 128:(ti + 1) * 128, :], in_=w8t).then_inc(st, 16)
                    s.nop().then_inc(sx, 1)
                cs.inc()
                ct.inc(48)

            @block.vector
            def _(v):
                v.wait_ge(mm, cm.v)
                v.tensor_copy(rows[0:1, 0, 0:NE], pR1[0:1, 0:NE]).then_inc(vx, 1)
            cv.inc()

            @block.scalar
            def _(s):
                s.wait_ge(vx, cv.v)
                s.dma_start(out=cnt_i.ap(), in_=rows[0:1, 0, 0:NE]).then_inc(st, 16)
            ct.inc(16)

            act = xin
            wslot_mm = {0: cm.v, 1: cm.v}
            for fi in range(16):
                sl = (fi % 2) * 2

                @block.gpsimd
                def _(g, fi=fi, sl=sl):
                    g.wait_ge(mm, wslot_mm[fi % 2])
                    for kc in range(DCH):
                        g.dma_start(out=wbuf[:, kc, sl, :],
                                    in_=wsgu[kc, :, fi * 128:(fi + 1) * 128]).then_inc(ldw, 16)
                        g.dma_start(out=wbuf[:, kc, sl + 1, :],
                                    in_=wsgu[kc, :, 2048 + fi * 128:2048 + (fi + 1) * 128]).then_inc(ldw, 16)
                cw_.inc(2 * DCH * 16)

                for blk in range(4):
                    col = HALO + blk * 512

                    @block.tensor
                    def _(t, fi=fi, sl=sl, col=col, first=(blk == 0)):
                        if first:
                            t.wait_ge(ldw, cw_.v)
                        t.wait_ge(vx, cv.v)
                        for kc in range(DCH):
                            mi = t.matmul(pA[:, 0:512], wbuf[:, kc, sl, :], xn[:, kc, col:col + 512],
                                     start=(kc == 0), stop=(kc == DCH - 1))
                        mi.then_inc(mm, 1)
                        for kc in range(DCH):
                            mi = t.matmul(pB[:, 0:512], wbuf[:, kc, sl + 1, :], xn[:, kc, col:col + 512],
                                     start=(kc == 0), stop=(kc == DCH - 1))
                        mi.then_inc(mm, 1)
                    cm.inc(2)

                    @block.scalar
                    def _(s):
                        s.wait_ge(mm, cm.v)
                        s.activation(sgb, pA[:, 0:512], AF.Silu).then_inc(sx, 1)
                    cs.inc()

                    @block.vector
                    def _(v, fi=fi, blk=blk):
                        v.wait_ge(sx, cs.v)
                        v.tensor_mul(act[:, fi, blk * 512:(blk + 1) * 512], sgb, pB[:, 0:512]).then_inc(vx, 1)
                    cv.inc()
                wslot_mm[fi % 2] = cm.v

            @block.tensor
            def _(t):
                t.wait_ge(vx, cv.v)
                for blk in range(4):
                    col = HALO + blk * 512
                    for kc in range(DCH):
                        mi = t.matmul(pC[0:1, 0:512], wsgb[:, kc:kc + 1], xn[:, kc, col:col + 512],
                                 start=(kc == 0), stop=(kc == DCH - 1))
                    mi.then_inc(mm, 1)
                    t.wait_ge(sx, cs.v + blk + 1)
            cm.inc(4)

            for blk in range(4):
                @block.scalar
                def _(s, blk=blk):
                    s.wait_ge(mm, cm.v - 4 + blk + 1)
                    s.activation(rows[0:1, 4 + blk, 0:512], pC[0:1, 0:512], AF.Sigmoid).then_inc(sx, 1)
                cs.inc()

            wd_mm = {0: cm.v, 1: cm.v}
            for f2 in range(DCH):
                sl = f2 % 2

                @block.gpsimd
                def _(g, f2=f2, sl=sl):
                    g.wait_ge(mm, wd_mm[sl])
                    for kc in range(ICH):
                        g.dma_start(out=wdbuf[:, kc, sl, :],
                                    in_=wsdr[kc, :, f2 * 128:(f2 + 1) * 128]).then_inc(ldw, 16)
                cw_.inc(ICH * 16)

                for blk in range(4):
                    ps = pA if blk % 2 == 0 else pB
                    shs = blk % 2

                    @block.tensor
                    def _(t, f2=f2, sl=sl, blk=blk, ps=ps, first=(blk == 0)):
                        if first:
                            t.wait_ge(ldw, cw_.v)
                        t.wait_ge(vx, cv.v)
                        for kc in range(ICH):
                            mi = t.matmul(ps[:, 0:512], wdbuf[:, kc, sl, :], act[:, kc, blk * 512:(blk + 1) * 512],
                                     start=(kc == 0), stop=(kc == ICH - 1))
                        mi.then_inc(mm, 1)
                    cm.inc()
                    wd_mm[sl] = cm.v

                    @block.tensor
                    def _(t, blk=blk):
                        t.wait_ge(sx, cs.v)
                        mi = t.matmul(pBC[:, 0:512], onerow[:], rows[0:1, 4 + blk, 0:512],
                                 start=True, stop=True)
                        mi.then_inc(mm, 1)
                    cm.inc()

                    @block.vector
                    def _(v, blk=blk, ps=ps, shs=shs):
                        v.wait_ge(mm, cm.v)
                        v.wait_ge(st, ct.v)
                        v.tensor_copy(sgate, pBC[:, 0:512]).then_inc(vx, 1)
                        v.wait_ge(vx, cv.v + 1)
                        v.tensor_mul(sshb[:, shs, :], ps[:, 0:512], sgate).then_inc(vx, 1)
                    cv.inc(2)

                    @block.scalar
                    def _(s, f2=f2, blk=blk, shs=shs):
                        s.wait_ge(vx, cv.v)
                        s.dma_start(out=ssh_s.ap()[f2, :, blk * 512:(blk + 1) * 512],
                                    in_=sshb[:, shs, :]).then_inc(st, 16)
                        s.nop().then_inc(sx, 1)
                    cs.inc()
                    ct.inc(16)

        @block.gpsimd
        def _(g):
            g.wait_ge(st, ct.v)
            g.collective_compute(
                "AllGather", ALU.bypass, replica_groups=[list(range(8))],
                ins=[cnt_i.ap().opt()], outs=[cnt_all.ap().opt()],
            ).then_inc(cc0)
        barrier()

        with ExitStack() as _ctx:
            hsbf = _ctx.enter_context(nc.sbuf_tensor("hsbf", [128, TPC], BF16))
            hmid = _ctx.enter_context(nc.sbuf_tensor("hmid", [128, 1], F32))
            oddsb = _ctx.enter_context(nc.sbuf_tensor("oddsb", [128, 1], F32))
            cp_sb = _ctx.enter_context(nc.sbuf_tensor("cp_sb", [128, ICH, 128], BF16))
            wop_sb = _ctx.enter_context(nc.sbuf_tensor("wop_sb", [128, ICH, D], BF16))
            dsb = _ctx.enter_context(nc.sbuf_tensor("dsb", [128, ICH], F32))
            xcs = _ctx.enter_context(nc.sbuf_tensor("xcs", [128, ICH, 512], BF16))
            zsil = _ctx.enter_context(nc.sbuf_tensor("zsil", [128, ICH, 512], BF16))
            zld = _ctx.enter_context(nc.sbuf_tensor("zld", [128, ICH, 512], BF16))
            ymbf = _ctx.enter_context(nc.sbuf_tensor("ymbf", [128, ICH, 512], BF16))
            yf = _ctx.enter_context(nc.sbuf_tensor("yf", [128, 2, 512], F32))
            mosb = _ctx.enter_context(nc.sbuf_tensor("mosb", [128, 2, 512], F32))
            ones = _ctx.enter_context(nc.sbuf_tensor("onesB", [128, 1], F32))
            onerow = _ctx.enter_context(nc.sbuf_tensor("onerowB", [1, 128], F32))
            tok0 = _ctx.enter_context(nc.sbuf_tensor("tok0", [128, 1], F32))
            tokc = _ctx.enter_context(nc.sbuf_tensor("tokc", [128, 1], F32))
            cntall_sb = _ctx.enter_context(nc.sbuf_tensor("cntall_sb", [8, NE], F32))
            pref_sb = _ctx.enter_context(nc.sbuf_tensor("pref_sb", [8, 1], F32))
            offs = _ctx.enter_context(nc.sbuf_tensor("offs", [1, NE], F32))
            idx8u = _ctx.enter_context(nc.sbuf_tensor("idx8u", [128, TOPK], U32))
            idx8f = _ctx.enter_context(nc.sbuf_tensor("idx8f", [128, TOPK], F32))
            w8s = _ctx.enter_context(nc.sbuf_tensor("w8s", [128, TOPK], F32))
            pos8 = _ctx.enter_context(nc.sbuf_tensor("pos8", [128, TOPK], F32))
            keep8 = _ctx.enter_context(nc.sbuf_tensor("keep8", [128, TOPK], F32))
            wk8 = _ctx.enter_context(nc.sbuf_tensor("wk8", [128, TOPK], F32))
            slot8 = _ctx.enter_context(nc.sbuf_tensor("slot8", [128, TOPK], F32))
            gidx = _ctx.enter_context(nc.sbuf_tensor("gidx", [128, TOPK], F32))
            gidxi = _ctx.enter_context(nc.sbuf_tensor("gidxi", [128, TOPK], I32))
            ptix = _ctx.enter_context(nc.sbuf_tensor("ptix", [8, 8], I32))
            idxf = _ctx.enter_context(nc.sbuf_tensor("idxf", [128, 4, EPC], F32))
            idxi = _ctx.enter_context(nc.sbuf_tensor("idxi", [128, 4, EPC], I32))
            xrows = _ctx.enter_context(nc.sbuf_tensor("xrows", [128, 2, D], F32))
            sqr = _ctx.enter_context(nc.sbuf_tensor("sqr", [128, D], F32))
            xrn = _ctx.enter_context(nc.sbuf_tensor("xrn", [128, 2, D], BF16))
            r1 = _ctx.enter_context(nc.sbuf_tensor("r1", [128, 2, 8], F32))
            idn = _ctx.enter_context(nc.sbuf_tensor("idn", [128, 128], BF16))
            xeT = _ctx.enter_context(nc.sbuf_tensor("xeT", [128, 2, DCH, 512], BF16))
            zrow = _ctx.enter_context(nc.sbuf_tensor("zrow", [1, 1024], BF16))
            epsc = _ctx.enter_context(nc.sbuf_tensor("epsc", [128, 1], F32))
            _bctx = ExitStack()
            hsf = _bctx.enter_context(nc.sbuf_tensor("hsf", [128, TPC], F32))
            acu = _bctx.enter_context(nc.sbuf_tensor("acu", [128, TPC], F32))
            utri_sb = _bctx.enter_context(nc.sbuf_tensor("utri_sb", [128, 128], F32))
            io64 = _bctx.enter_context(nc.sbuf_tensor("io64", [128, NE], F32))
            mown = _bctx.enter_context(nc.sbuf_tensor("mown", [128, NE], F32))
            pos_sb = _bctx.enter_context(nc.sbuf_tensor("pos_sb", [128, NE], F32))
            oh8 = _bctx.enter_context(nc.sbuf_tensor("oh8", [128, TOPK, NE], F32))
            tm8 = _bctx.enter_context(nc.sbuf_tensor("tm8", [128, TOPK, NE], F32))
            binit = _bctx.enter_context(nc.sbuf_tensor("binit", [NE, 513], F32))
            pts = _bctx.enter_context(nc.sbuf_tensor("pts", [8, NCORES, 513], F32))
            tmin = _bctx.enter_context(nc.sbuf_tensor("tmin", [8, 513], F32))
            idnf = _bctx.enter_context(nc.sbuf_tensor("idnf", [128, 128], F32))
            pA = _ctx.enter_context(nc.psum_tensor("pAb", [128, 512], F32))
            pB = _ctx.enter_context(nc.psum_tensor("pBb", [128, 512], F32))
            _posctx = ExitStack()
            pOFF = _posctx.enter_context(nc.psum_tensor("pOFF", [1, NE], F32))
            pPOS = _posctx.enter_context(nc.psum_tensor("pPOS", [128, NE], F32))
            pCS = _posctx.enter_context(nc.psum_tensor("pCS", [1, NE], F32))
            cv0B = cv.v

            @block.gpsimd
            def _(g):
                g.dma_start(out=cp_sb[:].rearrange("p a b -> p (a b)"), in_=cprojT).then_inc(ldw, 16)
                for kc in range(ICH):
                    g.dma_start(out=wop_sb[:, kc, :], in_=wop[kc * 128:(kc + 1) * 128, :]).then_inc(ldw, 16)
                    g.dma_start(out=dsb[:, kc:kc + 1], in_=dvec[kc * 128:(kc + 1) * 128, :]).then_inc(ldw, 16)
            cw_.inc((1 + 2 * ICH) * 16)

            @block.sync
            def _(e):
                e.wait_ge(cc0, 1)
                e.dma_start(out=hsf[:], in_=hs_s.ap()).then_inc(ldx, 16)
                e.dma_start(out=acu[:], in_=acum_s.ap()).then_inc(ldx, 16)
                e.dma_start(out=hmid[:], in_=hl_sh.ap()[0:128, :]).then_inc(ldx, 16)
                e.dma_start(out=oddsb[:], in_=oddcol).then_inc(ldx, 16)
                e.dma_start(out=utri_sb[:], in_=utri).then_inc(ldx, 16)
                e.dma_start(out=io64[:], in_=iota64).then_inc(ldx, 16)
                e.dma_start(out=tok0[:], in_=tokid).then_inc(ldx, 16)
                e.dma_start(out=cntall_sb[0:8, :], in_=cnt_all.ap()).then_inc(ldx, 16)
                e.dma_start(out=pref_sb[0:8, :], in_=prefvec).then_inc(ldx, 16)
            cx.inc(9 * 16)

            @block.vector
            def _(v):
                v.memset(ones[:], 1.0).then_inc(vx, 1)
                v.memset(onerow[:], 1.0).then_inc(vx, 1)
                v.memset(binit[:], 1e9).then_inc(vx, 1)
                v.wait_ge(ldx, cx.v)
                v.tensor_mul(hmid[:], hmid[:], oddsb[:]).then_inc(vx, 1)
                v.wait_ge(vx, cv.v + 4)
                for blk in range(4):
                    v.scalar_tensor_tensor(hsf[:, blk * 512:(blk + 1) * 512], acu[:, blk * 512:(blk + 1) * 512],
                                           hmid[:], hsf[:, blk * 512:(blk + 1) * 512],
                                           ALU.mult, ALU.add).then_inc(vx, 1)
                v.wait_ge(vx, cv.v + 8)
                v.tensor_copy(hsbf[:], hsf[:]).then_inc(vx, 1)
            cv.inc(9)

            @block.gpsimd
            def _(g):
                g.wait_ge(vx, cv0B + 3)
                g.dma_start(out=pt_my.ap(), in_=binit[:]).then_inc(st, 16)
            ct.inc(16)
            init_done = ct.v

            @block.tensor
            def _(t):
                t.wait_ge(ldx, cx.v)
                mi = t.matmul(pOFF[0:1, 0:NE], pref_sb[0:8, 0:1], cntall_sb[0:8, 0:NE], start=True, stop=True)
                mi.then_inc(mm, 1)
            cm.inc()

            @block.vector
            def _(v):
                v.wait_ge(mm, cm.v)
                v.tensor_copy(offs[:], pOFF[0:1, 0:NE]).then_inc(vx, 1)
            cv.inc()

            for i in range(TPC // 128):
                @block.sync
                def _(e, i=i):
                    e.wait_ge(vx, cv.v)
                    e.dma_start(out=mown[:], in_=mask_s.ap()[i * 128:(i + 1) * 128, :]).then_inc(ldx, 16)
                    e.dma_start(out=idx8u[:], in_=idx8_s.ap()[i * 128:(i + 1) * 128, :]).then_inc(ldx, 16)
                    e.dma_start(out=w8s[:], in_=w8_s.ap()[i * 128:(i + 1) * 128, :]).then_inc(ldx, 16)
                cx.inc(48)

                @block.tensor
                def _(t, i=i):
                    t.wait_ge(ldx, cx.v)
                    t.matmul(pPOS[:, 0:NE], utri_sb[:], mown[:], start=True, stop=False)
                    mi = t.matmul(pPOS[:, 0:NE], onerow[:], offs[:], start=False, stop=True)
                    mi.then_inc(mm, 1)
                    mi2 = t.matmul(pCS[0:1, 0:NE], ones[:], mown[:], start=True, stop=True)
                    mi2.then_inc(mm, 1)
                cm.inc(2)

                @block.vector
                def _(v, i=i):
                    v.wait_ge(mm, cm.v)
                    v.wait_ge(st, ct.v)
                    v.tensor_copy(pos_sb[:], pPOS[:, 0:NE]).then_inc(vx, 1)
                    v.tensor_add(offs[:], offs[:], pCS[0:1, 0:NE]).then_inc(vx, 1)
                    v.tensor_copy(idx8f[:], idx8u[:]).then_inc(vx, 1)
                    v.wait_ge(vx, cv.v + 3)
                    for k in range(TOPK):
                        v.tensor_scalar(oh8[:, k, :], io64[:], idx8f[:, k:k + 1], None,
                                        ALU.is_equal).then_inc(vx, 1)
                    v.wait_ge(vx, cv.v + 3 + TOPK)
                    for k in range(TOPK):
                        v.tensor_mul(tm8[:, k, :], oh8[:, k, :], pos_sb[:]).then_inc(vx, 1)
                    v.wait_ge(vx, cv.v + 3 + 2 * TOPK)
                    for k in range(TOPK):
                        v.reduce_sum(pos8[:, k:k + 1], tm8[:, k, :], axis=AX.X).then_inc(vx, 1)
                    v.wait_ge(vx, cv.v + 3 + 3 * TOPK)
                    v.tensor_single_scalar(keep8[:], pos8[:], 512.0, ALU.is_lt).then_inc(vx, 1)
                    v.tensor_scalar_min(slot8[:], pos8[:], 512.0).then_inc(vx, 1)
                    v.tensor_scalar_add(tokc[:], tok0[:], float(i * 128)).then_inc(vx, 1)
                    v.wait_ge(vx, cv.v + 6 + 3 * TOPK)
                    v.tensor_mul(wk8[:], w8s[:], keep8[:]).then_inc(vx, 1)
                    v.scalar_tensor_tensor(gidx[:], idx8f[:], 513.0, slot8[:], ALU.mult, ALU.add).then_inc(vx, 1)
                    v.wait_ge(vx, cv.v + 8 + 3 * TOPK)
                    v.tensor_copy(gidxi[:], gidx[:]).then_inc(vx, 1)
                cv.inc(9 + 3 * TOPK)

                @block.scalar
                def _(s, i=i):
                    s.wait_ge(vx, cv.v)
                    s.dma_start(out=gi_s.ap()[i * 128:(i + 1) * 128, :], in_=gidx[:]).then_inc(st, 16)
                    s.dma_start(out=wk_s.ap()[i * 128:(i + 1) * 128, :], in_=wk8[:]).then_inc(st, 16)
                ct.inc(32)

                @block.gpsimd
                def _(g, i=i):
                    g.wait_ge(vx, cv.v)
                    g.wait_ge(st, init_done)
                    for k in range(TOPK):
                        g.indirect_dma_start(
                            out=pt_flat, out_offset=IndirectOffsetOnAxis(gidxi[:, k:k + 1], 0),
                            in_=tokc[:], in_offset=None,
                        ).then_inc(st, 16)
                ct.inc(TOPK * 16)

            _posctx.close()
            cm0C = cm.v

            @block.gpsimd
            def _(g):
                g.wait_ge(st, ct.v)
                g.collective_compute(
                    "AllGather", ALU.bypass, replica_groups=[list(range(8))],
                    ins=[pt_my.ap().opt()], outs=[pt_all.ap().opt()],
                ).then_inc(cc1)

            for blk in range(4):
                c0 = blk * 512

                @block.sync
                def _(e, c0=c0):
                    e.wait_ge(mm, cm.v)
                    e.wait_ge(sx, cs.v)
                    for kc in range(ICH):
                        e.dma_start(out=xcs[:, kc, :], in_=xc_s.ap()[kc, :, c0:c0 + 512]).then_inc(ldx, 16)
                        e.dma_start(out=zld[:, kc, :], in_=z_s.ap()[kc, :, c0:c0 + 512]).then_inc(ldx, 16)
                cx.inc(2 * ICH * 16)

                @block.scalar
                def _(s, c0=c0):
                    s.wait_ge(vx, cv.v)
                    s.wait_ge(ldx, cx.v)
                    for kc in range(ICH):
                        s.activation(zsil[:, kc, :], zld[:, kc, :], AF.Silu).then_inc(sx, 1)
                cs.inc(ICH)

                for f in range(ICH):
                    ps = pA if f % 2 == 0 else pB
                    sl = f % 2

                    @block.tensor
                    def _(t, f=f, ps=ps, c0=c0, first=(f == 0)):
                        if first:
                            t.wait_ge(ldw, cw_.v)
                            t.wait_ge(ldx, cx.v)
                        t.wait_ge(vx, cv.v)
                        mi = t.matmul(ps[:, 0:512], cp_sb[:, f, :], hsbf[:, c0:c0 + 512], start=True, stop=True)
                        mi.then_inc(mm, 1)
                    cm.inc()

                    @block.vector
                    def _(v, f=f, ps=ps, sl=sl):
                        v.wait_ge(mm, cm.v)
                        v.wait_ge(sx, cs.v)
                        v.scalar_tensor_tensor(yf[:, sl, :], xcs[:, f, :], dsb[:, f:f + 1],
                                               ps[:, 0:512], ALU.mult, ALU.add).then_inc(vx, 1)
                        v.wait_ge(vx, cv.v + 1)
                        v.tensor_mul(ymbf[:, f, :], yf[:, sl, :], zsil[:, f, :]).then_inc(vx, 1)
                    cv.inc(2)

                for dm in range(DCH):
                    ps = pA if dm % 2 == 0 else pB
                    sl = dm % 2

                    @block.tensor
                    def _(t, dm=dm, ps=ps, first=(dm == 0)):
                        t.wait_ge(vx, cv.v)
                        if dm >= 2:
                            t.wait_ge(st, ct.v)
                        for kc in range(ICH):
                            mi = t.matmul(ps[:, 0:512], wop_sb[:, kc, dm * 128:(dm + 1) * 128],
                                          ymbf[:, kc, :], start=(kc == 0), stop=(kc == ICH - 1))
                        mi.then_inc(mm, 1)
                    cm.inc()

                    @block.vector
                    def _(v, ps=ps, sl=sl):
                        v.wait_ge(mm, cm.v)
                        v.tensor_copy(mosb[:, sl, :], ps[:, 0:512]).then_inc(vx, 1)
                    cv.inc()

                    @block.scalar
                    def _(s, dm=dm, sl=sl, c0=c0):
                        s.wait_ge(vx, cv.v)
                        s.dma_start(out=mo_s.ap()[dm, :, c0:c0 + 512], in_=mosb[:, sl, :]).then_inc(st, 16)
                    ct.inc(16)

            pT = _ctx.enter_context(nc.psum_tensor("pT", [128, 128], BF16))
            pTT = _ctx.enter_context(nc.psum_tensor("pTT", [128, 32], F32))
            pA = _ctx.enter_context(nc.psum_tensor("pAc", [128, 512], F32))
            pB = _ctx.enter_context(nc.psum_tensor("pBc", [128, 512], F32))
            pC = _ctx.enter_context(nc.psum_tensor("pCc", [128, 512], F32))
            pD = _ctx.enter_context(nc.psum_tensor("pDc", [128, 512], F32))
            cv0C = cv.v

            @block.sync
            def _(e):
                e.dma_start(out=ptix[0:8, :], in_=ptidx).then_inc(ldx, 16)
                e.dma_start(out=idnf[:], in_=ident).then_inc(ldx, 16)
            cx.inc(32)

            @block.gpsimd
            def _(g):
                g.wait_ge(ldx, cx.v)
                g.wait_ge(cc1, 1)
                for src in range(NCORES):
                    g.indirect_dma_start(
                        out=pts[0:8, src, :], out_offset=None,
                        in_=pt_all.ap(), in_offset=IndirectOffsetOnAxis(ptix[0:8, src:src + 1], 0),
                    ).then_inc(ldx, 16)
                g.dma_start(out=idn[:], in_=ident).then_inc(ldw, 16)
            cx.inc(NCORES * 16)
            cw_.inc(16)

            @block.vector
            def _(v):
                v.wait_ge(ldx, cx.v)
                v.memset(zrow[:], 0.0).then_inc(vx, 1)
                v.memset(epsc[:], 1e-5).then_inc(vx, 1)
                v.tensor_copy(tmin[:], pts[:, 0, :]).then_inc(vx, 1)
                for src in range(1, NCORES):
                    v.wait_ge(vx, cv.v + 2 + src)
                    v.tensor_tensor(tmin[:], tmin[:], pts[:, src, :], ALU.min).then_inc(vx, 1)
            cv.inc(2 + NCORES)

            @block.tensor
            def _(t):
                t.wait_ge(vx, cv.v)
                t.wait_ge(ldx, cx.v)
                for i in range(4):
                    mi = t.transpose(pTT[:, i * EPC:(i + 1) * EPC], tmin[0:8, i * 128:(i + 1) * 128],
                                     idnf[0:8, 0:EPC])
                    mi.then_inc(mm, 1)
            cm.inc(4)

            @block.vector
            def _(v):
                v.wait_ge(mm, cm.v)
                v.tensor_copy(idxf[:].rearrange("p a b -> p (a b)"), pTT[:, 0:32]).then_inc(vx, 1)
                v.wait_ge(vx, cv.v + 1)
                v.tensor_copy(idxi[:], idxf[:]).then_inc(vx, 1)
            cv.inc(2)
            idx_ready = cv.v
            _bctx.close()
            wgu = _ctx.enter_context(nc.sbuf_tensor("wgu", [128, 2, DCH, 1024], BF16))
            wdw = _ctx.enter_context(nc.sbuf_tensor("wdw", [128, 2, 4, 1024], BF16))
            sg = _ctx.enter_context(nc.sbuf_tensor("sg", [128, 512], BF16))
            actb = _ctx.enter_context(nc.sbuf_tensor("actb", [128, 4, 512], BF16))
            oeb = _ctx.enter_context(nc.sbuf_tensor("oeb", [128, 2, 512], BF16))

            gemm_cm = {}
            ln_w = {}
            for e_ in range(EPC):
                wsl = e_ % 2
                w_prev = gemm_cm.get(e_ - 2, cm0C)

                @block.gpsimd
                def _(g, e_=e_, wsl=wsl, w_prev=w_prev):
                    g.wait_ge(mm, w_prev)
                    g.wait_ge(vx, idx_ready)
                    for kc in range(DCH):
                        g.dma_start(out=wgu[:, wsl, kc, :], in_=egu8[e_, kc * 128:(kc + 1) * 128, :]).then_inc(ldw, 16)
                    for kc in range(4):
                        g.dma_start(out=wdw[:, wsl, kc, :], in_=edw8[e_, kc * 128:(kc + 1) * 128, :]).then_inc(ldw, 16)
                cw_.inc((DCH + 4) * 16)

                for rb in range(4):
                    @block.gpsimd
                    def _(g, e_=e_, rb=rb, wsl=wsl, w_prev=w_prev):
                        g.wait_ge(vx, idx_ready)
                        g.wait_ge(mm, w_prev)
                        if rb > 0:
                            g.wait_ge(vx, ln_w[(e_, rb - 1)])
                        g.indirect_dma_start(
                            out=xrows[:, wsl, :], out_offset=None,
                            in_=xfull, in_offset=IndirectOffsetOnAxis(idxi[:, rb, e_:e_ + 1], 0),
                            bounds_check=16383, oob_is_err=False,
                        ).then_inc(ldx, 16)
                    cx.inc(16)

                    @block.scalar
                    def _(s, e_=e_, rb=rb, wsl=wsl):
                        s.wait_ge(ldx, cx.v)
                        s.activation(sqr[:], xrows[:, wsl, :], AF.Square).then_inc(sx, 1)
                    cs.inc()

                    @block.vector
                    def _(v, e_=e_, rb=rb, wsl=wsl):
                        v.wait_ge(sx, cs.v)
                        v.reduce_sum(r1[:, wsl, 0:1], xrows[:, wsl, :], axis=AX.X).then_inc(vx, 1)
                        v.reduce_sum(r1[:, wsl, 1:2], sqr[:], axis=AX.X).then_inc(vx, 1)
                        v.wait_ge(vx, cv.v + 2)
                        v.tensor_scalar_mul(r1[:, wsl, 2:3], r1[:, wsl, 0:1], 1.0 / D).then_inc(vx, 1)
                        v.tensor_scalar_mul(r1[:, wsl, 3:4], r1[:, wsl, 1:2], 1.0 / D).then_inc(vx, 1)
                        v.wait_ge(vx, cv.v + 4)
                        v.tensor_mul(r1[:, wsl, 4:5], r1[:, wsl, 2:3], r1[:, wsl, 2:3]).then_inc(vx, 1)
                        v.wait_ge(vx, cv.v + 5)
                        v.tensor_sub(r1[:, wsl, 5:6], r1[:, wsl, 3:4], r1[:, wsl, 4:5]).then_inc(vx, 1)
                    cv.inc(6)

                    @block.scalar
                    def _(s, e_=e_, rb=rb, wsl=wsl):
                        s.wait_ge(vx, cv.v)
                        s.activation(r1[:, wsl, 6:7], r1[:, wsl, 5:6], AF.Ln, bias=epsc[:]).then_inc(sx, 1)
                        s.wait_ge(sx, cs.v + 1)
                        s.activation(r1[:, wsl, 7:8], r1[:, wsl, 6:7], AF.Exp, scale=-0.5).then_inc(sx, 1)
                    cs.inc(2)

                    @block.vector
                    def _(v, e_=e_, rb=rb, wsl=wsl):
                        v.wait_ge(sx, cs.v)
                        v.tensor_scalar(xrn[:, wsl, :], xrows[:, wsl, :], r1[:, wsl, 2:3], r1[:, wsl, 7:8],
                                        ALU.subtract, ALU.mult).then_inc(vx, 1)
                    cv.inc()

                    for kc in range(DCH):
                        @block.tensor
                        def _(t, e_=e_, rb=rb, kc=kc, wsl=wsl):
                            t.wait_ge(vx, cv.v)
                            mi = t.transpose(pT[:, 0:128], xrn[:, wsl, kc * 128:(kc + 1) * 128], idn[:])
                            mi.then_inc(mm, 1)
                        cm.inc()

                        @block.vector
                        def _(v, e_=e_, rb=rb, kc=kc, wsl=wsl):
                            v.wait_ge(mm, cm.v)
                            v.tensor_copy(xeT[:, wsl, kc, rb * 128:(rb + 1) * 128], pT[:, 0:128]).then_inc(vx, 1)
                        cv.inc()
                    ln_w[(e_, rb)] = cv.v

                xeT_ready = cv.v

                for f in range(4):
                    @block.tensor
                    def _(t, e_=e_, wsl=wsl, f=f, first=(f == 0), xeT_ready=xeT_ready):
                        if first:
                            t.wait_ge(ldw, cw_.v)
                            t.wait_ge(vx, xeT_ready)
                        t.wait_ge(sx, cs.v)
                        t.wait_ge(vx, cv.v)
                        for kc in range(DCH):
                            mi = t.matmul(pA[:, 0:512], wgu[:, wsl, kc, f * 128:(f + 1) * 128],
                                          xeT[:, wsl, kc, :], start=(kc == 0), stop=(kc == DCH - 1))
                        mi.then_inc(mm, 1)
                        for kc in range(DCH):
                            mi = t.matmul(pB[:, 0:512], wgu[:, wsl, kc, 512 + f * 128:512 + (f + 1) * 128],
                                          xeT[:, wsl, kc, :], start=(kc == 0), stop=(kc == DCH - 1))
                        mi.then_inc(mm, 1)
                    cm.inc(2)

                    @block.scalar
                    def _(s, f=f):
                        s.wait_ge(mm, cm.v)
                        s.activation(sg[:], pA[:, 0:512], AF.Silu).then_inc(sx, 1)
                    cs.inc()

                    @block.vector
                    def _(v, f=f):
                        v.wait_ge(sx, cs.v)
                        v.tensor_mul(actb[:, f, :], sg[:], pB[:, 0:512]).then_inc(vx, 1)
                    cv.inc()

                for rb in range(4):
                    for dmh in range(2):
                        ps = pC if (rb * 2 + dmh) % 2 == 0 else pD
                        osl = (rb * 2 + dmh) % 2

                        @block.tensor
                        def _(t, e_=e_, wsl=wsl, rb=rb, dmh=dmh, ps=ps):
                            t.wait_ge(vx, cv.v)
                            t.wait_ge(st, ct.v)
                            for ic in range(4):
                                mi = t.matmul(ps[:, 0:512], actb[:, ic, rb * 128:(rb + 1) * 128],
                                              wdw[:, wsl, ic, dmh * 512:(dmh + 1) * 512],
                                              start=(ic == 0), stop=(ic == 3))
                            mi.then_inc(mm, 1)
                        cm.inc()

                        @block.vector
                        def _(v, ps=ps, osl=osl):
                            v.wait_ge(mm, cm.v)
                            v.tensor_copy(oeb[:, osl, :], ps[:, 0:512]).then_inc(vx, 1)
                        cv.inc()

                        @block.scalar
                        def _(s, e_=e_, rb=rb, dmh=dmh, osl=osl):
                            s.wait_ge(vx, cv.v)
                            s.dma_start(out=oe_my.ap()[e_ * 513 + rb * 128:e_ * 513 + (rb + 1) * 128,
                                                       dmh * 512:(dmh + 1) * 512],
                                        in_=oeb[:, osl, :]).then_inc(st, 16)
                        ct.inc(16)

                @block.scalar
                def _(s, e_=e_):
                    s.wait_ge(vx, cv0C + 1)
                    s.dma_start(out=oe_my.ap()[e_ * 513 + 512:e_ * 513 + 513, :], in_=zrow[:]).then_inc(st, 16)
                ct.inc(16)
                gemm_cm[e_] = cm.v

        @block.gpsimd
        def _(g):
            g.wait_ge(st, ct.v)
            g.collective_compute(
                "AllGather", ALU.bypass, replica_groups=[list(range(8))],
                ins=[oe_my.ap().opt()], outs=[oe_all.ap().opt()],
            ).then_inc(cc2)
        barrier()
        with ExitStack() as _ctx:
            gis = _ctx.enter_context(nc.sbuf_tensor("gis", [128, TOPK], F32))
            gii = _ctx.enter_context(nc.sbuf_tensor("gii", [128, TOPK], I32))
            wks = _ctx.enter_context(nc.sbuf_tensor("wks", [128, TOPK], F32))
            grow = _ctx.enter_context(nc.sbuf_tensor("grow", [128, 2, 1024], BF16))
            acc = _ctx.enter_context(nc.sbuf_tensor("acc", [128, 1024], F32))
            xa = _ctx.enter_context(nc.sbuf_tensor("xa", [128, DCH, 512], F32))
            xb2 = _ctx.enter_context(nc.sbuf_tensor("xb2", [128, DCH, 512], F32))
            sqf = _ctx.enter_context(nc.sbuf_tensor("sqfE", [128, DCH, 512], F32))
            ymt = _ctx.enter_context(nc.sbuf_tensor("ymt", [128, 4, 1024], F32))
            rows = _ctx.enter_context(nc.sbuf_tensor("rowsE", [1, 8, 512], F32))
            bR = _ctx.enter_context(nc.sbuf_tensor("bRE", [128, 512], F32))
            bM = _ctx.enter_context(nc.sbuf_tensor("bME", [128, 512], F32))
            ones = _ctx.enter_context(nc.sbuf_tensor("onesE", [128, 1], F32))
            onerow = _ctx.enter_context(nc.sbuf_tensor("onerowE", [1, 128], F32))
            epst = _ctx.enter_context(nc.sbuf_tensor("epstE", [1, 1], F32))
            xnb = _ctx.enter_context(nc.sbuf_tensor("xnbE", [128, DCH, 512], BF16))
            idnE = _ctx.enter_context(nc.sbuf_tensor("idnE", [128, 128], F32))
            xsix = _ctx.enter_context(nc.sbuf_tensor("xsix", [128, DCH], I32))
            pR1 = _ctx.enter_context(nc.psum_tensor("pR1e", [128, 512], F32))
            pR2 = _ctx.enter_context(nc.psum_tensor("pR2e", [128, 512], F32))
            pBC = _ctx.enter_context(nc.psum_tensor("pBCe", [128, 512], F32))
            pS = _ctx.enter_context(nc.psum_tensor("pSe", [128, 512], F32))
            pT1 = _ctx.enter_context(nc.psum_tensor("pT1e", [128, 128], F32))
            pT2 = _ctx.enter_context(nc.psum_tensor("pT2e", [128, 128], F32))

            @block.vector
            def _(v):
                v.memset(ones[:], 1.0).then_inc(vx, 1)
                v.memset(onerow[:], 1.0).then_inc(vx, 1)
                v.memset(epst[:], 1e-5).then_inc(vx, 1)
            cv.inc(3)

            @block.sync
            def _(e):
                e.dma_start(out=idnE[:], in_=ident).then_inc(ldx, 16)
                e.dma_start(out=xsix[:], in_=xsidx).then_inc(ldx, 16)
            cx.inc(32)
            eaux = cx.v

            e1_st = {}
            for blk in range(4):
                c0 = blk * 512

                @block.sync
                def _(e, c0=c0, blk=blk):
                    if blk > 0:
                        e.wait_ge(st, e1_st[blk - 1])
                    for kc in range(DCH):
                        e.dma_start(out=xa[:, kc, :], in_=xTv[kc * 128:(kc + 1) * 128, c0:c0 + 512]).then_inc(ldx, 16)
                        e.dma_start(out=xb2[:, kc, :], in_=mo_s.ap()[kc, :, c0:c0 + 512]).then_inc(ldx, 16)
                cx.inc(2 * DCH * 16)

                @block.vector
                def _(v, c0=c0):
                    v.wait_ge(ldx, cx.v)
                    for kc in range(DCH):
                        v.tensor_add(xa[:, kc, :], xa[:, kc, :], xb2[:, kc, :]).then_inc(vx, 1)
                cv.inc(DCH)

                @block.sync
                def _(e, c0=c0):
                    e.wait_ge(vx, cv.v)
                    for kc in range(DCH):
                        e.dma_start(out=xb2[:, kc, :], in_=ssh_s.ap()[kc, :, c0:c0 + 512]).then_inc(ldx, 16)
                cx.inc(DCH * 16)

                @block.vector
                def _(v, c0=c0):
                    v.wait_ge(ldx, cx.v)
                    for kc in range(DCH):
                        v.tensor_add(xa[:, kc, :], xa[:, kc, :], xb2[:, kc, :]).then_inc(vx, 1)
                cv.inc(DCH)

                @block.scalar
                def _(s, c0=c0):
                    s.wait_ge(vx, cv.v)
                    for kc in range(DCH):
                        s.dma_start(out=x1_s.ap()[kc, :, c0:c0 + 512], in_=xa[:, kc, :]).then_inc(st, 16)
                ct.inc(DCH * 16)
                e1_st[blk] = ct.v

            @block.gpsimd
            def _(g):
                g.wait_ge(cc2, 1)

            d_ct = {}
            for i in range(TPC // 128):
                @block.sync
                def _(e, i=i):
                    e.wait_ge(vx, cv.v)
                    e.dma_start(out=gis[:], in_=gi_s.ap()[i * 128:(i + 1) * 128, :]).then_inc(ldx, 16)
                    e.dma_start(out=wks[:], in_=wk_s.ap()[i * 128:(i + 1) * 128, :]).then_inc(ldx, 16)
                cx.inc(32)

                @block.vector
                def _(v, i=i):
                    v.wait_ge(ldx, cx.v)
                    v.tensor_copy(gii[:], gis[:]).then_inc(vx, 1)
                cv.inc()

                for k in range(TOPK):
                    sl = k % 2

                    @block.gpsimd
                    def _(g, i=i, k=k, sl=sl):
                        g.wait_ge(vx, cv.v)
                        g.indirect_dma_start(
                            out=grow[:, sl, :], out_offset=None,
                            in_=oe_all.ap(), in_offset=IndirectOffsetOnAxis(gii[:, k:k + 1], 0),
                        ).then_inc(ldx, 16)
                    cx.inc(16)

                    @block.vector
                    def _(v, i=i, k=k, sl=sl):
                        v.wait_ge(ldx, cx.v)
                        if k == 0:
                            v.tensor_scalar(acc[:], grow[:, sl, :], wks[:, 0:1], None, ALU.mult).then_inc(vx, 1)
                        else:
                            v.wait_ge(vx, cv.v)
                            v.scalar_tensor_tensor(acc[:], grow[:, sl, :], wks[:, k:k + 1],
                                                   acc[:], ALU.mult, ALU.add).then_inc(vx, 1)
                    cv.inc()

                @block.scalar
                def _(s, i=i):
                    s.wait_ge(vx, cv.v)
                    s.dma_start(out=ym_s.ap()[i * 128:(i + 1) * 128, :], in_=acc[:]).then_inc(st, 16)
                ct.inc(16)
                d_ct[i] = ct.v

                @block.vector
                def _(v):
                    v.wait_ge(st, ct.v)
                    v.engine_nop().then_inc(vx, 1)
                cv.inc()

            e2_st = None
            for blk in range(4):
                c0 = blk * 512

                @block.sync
                def _(e, c0=c0, blk=blk):
                    e.wait_ge(st, e1_st[3])
                    e.wait_ge(st, d_ct[15])
                    if blk > 0:
                        e.wait_ge(st, e2_st)
                        e.wait_ge(sx, cs.v)
                        e.wait_ge(mm, cm.v)
                    for kc in range(DCH):
                        e.dma_start(out=xa[:, kc, :], in_=x1_s.ap()[kc, :, c0:c0 + 512]).then_inc(ldx, 16)
                    for j in range(4):
                        e.dma_start(out=ymt[:, j, :], in_=ym_s.ap()[c0 + j * 128:c0 + (j + 1) * 128, :]).then_inc(ldx, 16)
                cx.inc((DCH + 4) * 16)

                for j in range(4):
                    for kc in range(DCH):
                        ps = pT1 if (j * DCH + kc) % 2 == 0 else pT2

                        @block.tensor
                        def _(t, j=j, kc=kc, ps=ps):
                            t.wait_ge(ldx, cx.v)
                            t.wait_ge(vx, cv.v)
                            mi = t.transpose(ps[:, 0:128], ymt[:, j, kc * 128:(kc + 1) * 128], idnE[:])
                            mi.then_inc(mm, 1)
                        cm.inc()

                        @block.vector
                        def _(v, j=j, kc=kc, ps=ps):
                            v.wait_ge(mm, cm.v)
                            v.tensor_copy(sqf[:, kc, j * 128:(j + 1) * 128], ps[:, 0:128]).then_inc(vx, 1)
                        cv.inc()

                @block.vector
                def _(v, c0=c0):
                    v.wait_ge(ldx, cx.v)
                    v.wait_ge(vx, cv.v)
                    for kc in range(DCH):
                        v.tensor_add(xa[:, kc, :], xa[:, kc, :], sqf[:, kc, :]).then_inc(vx, 1)
                cv.inc(DCH)
                @block.scalar
                def _(s, c0=c0):
                    s.wait_ge(vx, cv.v)
                    for kc in range(DCH):
                        s.dma_start(out=x1_s.ap()[kc, :, c0:c0 + 512], in_=xa[:, kc, :]).then_inc(st, 16)
                    for kc in range(DCH):
                        s.activation(sqf[:, kc, :], xa[:, kc, :], AF.Square).then_inc(sx, 1)
                ct.inc(DCH * 16)
                cs.inc(DCH)

                @block.tensor
                def _(t, c0=c0):
                    t.wait_ge(vx, cv.v)
                    for kc in range(DCH):
                        mi = t.matmul(pR1[0:1, 0:512], ones[:], xa[:, kc, :], start=(kc == 0), stop=(kc == DCH - 1))
                    mi.then_inc(mm, 1)
                    t.wait_ge(sx, cs.v)
                    for kc in range(DCH):
                        mi = t.matmul(pR2[0:1, 0:512], ones[:], sqf[:, kc, :], start=(kc == 0), stop=(kc == DCH - 1))
                    mi.then_inc(mm, 1)
                cm.inc(2)

                @block.vector
                def _(v):
                    v.wait_ge(mm, cm.v)
                    v.tensor_scalar_mul(rows[0:1, 0, :], pR1[0:1, 0:512], 1.0 / D).then_inc(vx, 1)
                    v.tensor_scalar_mul(rows[0:1, 1, :], pR2[0:1, 0:512], 1.0 / D).then_inc(vx, 1)
                    v.wait_ge(vx, cv.v + 2)
                    v.tensor_mul(rows[0:1, 5, :], rows[0:1, 0, :], rows[0:1, 0, :]).then_inc(vx, 1)
                    v.wait_ge(vx, cv.v + 3)
                    v.tensor_sub(rows[0:1, 1, :], rows[0:1, 1, :], rows[0:1, 5, :]).then_inc(vx, 1)
                cv.inc(4)

                @block.scalar
                def _(s):
                    s.wait_ge(vx, cv.v)
                    s.activation(rows[0:1, 6, :], rows[0:1, 1, :], AF.Ln, bias=epst[:]).then_inc(sx, 1)
                    s.wait_ge(sx, cs.v + 1)
                    s.activation(rows[0:1, 2, :], rows[0:1, 6, :], AF.Exp, scale=-0.5).then_inc(sx, 1)
                cs.inc(2)

                @block.vector
                def _(v):
                    v.wait_ge(sx, cs.v)
                    v.tensor_mul(rows[0:1, 3, :], rows[0:1, 0, :], rows[0:1, 2, :]).then_inc(vx, 1)
                cv.inc()

                @block.tensor
                def _(t):
                    t.wait_ge(vx, cv.v)
                    mi = t.matmul(pBC[:, 0:512], onerow[:], rows[0:1, 2, :], start=True, stop=True)
                    mi.then_inc(mm, 1)
                    mi = t.matmul(pS[:, 0:512], onerow[:], rows[0:1, 3, :], start=True, stop=True)
                    mi.then_inc(mm, 1)
                cm.inc(2)

                @block.vector
                def _(v, c0=c0):
                    v.wait_ge(mm, cm.v)
                    v.tensor_copy(bR[:], pBC[:, 0:512]).then_inc(vx, 1)
                    v.tensor_copy(bM[:], pS[:, 0:512]).then_inc(vx, 1)
                    v.wait_ge(vx, cv.v + 2)
                    for kc in range(DCH):
                        v.tensor_mul(sqf[:, kc, :], xa[:, kc, :], bR[:]).then_inc(vx, 1)
                    v.wait_ge(vx, cv.v + 2 + DCH)
                    for kc in range(DCH):
                        v.tensor_sub(xnb[:, kc, :], sqf[:, kc, :], bM[:]).then_inc(vx, 1)
                cv.inc(2 + 2 * DCH)

                @block.scalar
                def _(s, c0=c0):
                    s.wait_ge(vx, cv.v)
                    for kc in range(DCH):
                        s.dma_start(out=xn1_mine.ap()[kc, :, c0:c0 + 512], in_=xnb[:, kc, :]).then_inc(st, 16)
                ct.inc(DCH * 16)

                @block.gpsimd
                def _(g, c0=c0):
                    g.wait_ge(vx, cv.v)
                    g.wait_ge(ldx, eaux)
                    for kc in range(DCH):
                        g.indirect_dma_start(
                            out=xn1_sh.ap(), out_offset=IndirectOffsetOnAxis(xsix[:, kc:kc + 1], 0),
                            in_=xnb[:, kc, :], in_offset=None, element_offset=c0,
                        ).then_inc(st, 16)
                ct.inc(DCH * 16)
                e2_st = ct.v

        @block.gpsimd
        def _(g):
            g.wait_ge(st, ct.v)
            g.collective_compute(
                "AllGather", ALU.bypass, replica_groups=[list(range(8))],
                ins=[bar_i.ap().opt()], outs=[bar_o.ap().opt()],
            ).then_inc(cc3)
        barrier()

        with ExitStack() as _ctx:
            LB = 4096
            kbf = _ctx.enter_context(nc.sbuf_tensor("kbf", [128, 8, LB], BF16))
            vbf = _ctx.enter_context(nc.sbuf_tensor("vbf", [128, 32, 1024], BF16))
            vfl = vbf[:].rearrange("p a b -> p (a b)")
            wkv_sb = vfl[:, 0:2048].rearrange("p (a b) -> p a b", a=DCH)
            wku_sb = vfl[:, 2048:4096].rearrange("p (a b) -> p a b", a=2)
            u1 = _ctx.enter_context(nc.sbuf_tensor("u1", [128, 16384], BF16))
            latr = u1[:].rearrange("p (a b) -> p a b", a=2)
            qvv = u1[:, 12288:16384].rearrange("p (a b) -> p a b", a=8)
            wo_sb = u1[:, 0:8192].rearrange("p (a b) -> p a b", a=DCH)
            wsg_sb = u1[:, 8192:16384].rearrange("p (a b) -> p a b", a=DCH)
            xstr = _ctx.enter_context(nc.sbuf_tensor("xstr", [128, DCH, 512], BF16))
            attn = _ctx.enter_context(nc.sbuf_tensor("attnF", [128, 8, 512], BF16))
            pexp = _ctx.enter_context(nc.sbuf_tensor("pexpF", [128, 4, 512], BF16))
            recb = _ctx.enter_context(nc.sbuf_tensor("recbF", [128, 512], BF16))
            wq_sb = _ctx.enter_context(nc.sbuf_tensor("wq_sb", [128, DCH, 1024], BF16))
            wvu_sb = attn[:].rearrange("p a b -> p (a b)")[:, 0:2048].rearrange("p (a b) -> p a b", a=2)
            cmk = _ctx.enter_context(nc.sbuf_tensor("cmkF", [128, 4, 512], BF16))
            evenfl = _ctx.enter_context(nc.sbuf_tensor("evenfl", [128, 1], F32))
            oddfl = _ctx.enter_context(nc.sbuf_tensor("oddfl", [128, 1], F32))
            killc = _ctx.enter_context(nc.sbuf_tensor("killc", [128, 1], F32))
            x1s = _ctx.enter_context(nc.sbuf_tensor("x1sF", [128, 1, 512], F32))
            dnsb = x1s[:, 0, :]
            fino = _ctx.enter_context(nc.sbuf_tensor("finoF", [128, 1, 512], F32))
            wsgb2 = pexp[:].rearrange("p a b -> p (a b)")
            sexp = u1[:, 8192:10240].rearrange("p (a b) -> p a b", a=4)
            onesall = _ctx.enter_context(nc.sbuf_tensor("onesallF", [128, 64], F32))
            onesbf = _ctx.enter_context(nc.sbuf_tensor("onesbfF", [128, 1], BF16))
            pS1 = _ctx.enter_context(nc.psum_tensor("pS1f", [128, 512], F32))
            pS2 = _ctx.enter_context(nc.psum_tensor("pS2f", [128, 512], F32))
            pAV1 = _ctx.enter_context(nc.psum_tensor("pAV1f", [128, 512], F32))
            pAV2 = _ctx.enter_context(nc.psum_tensor("pAV2f", [128, 512], F32))
            pDN1 = _ctx.enter_context(nc.psum_tensor("pDN1f", [128, 512], F32))
            pDN2 = _ctx.enter_context(nc.psum_tensor("pDN2f", [128, 512], F32))
            pO1 = _ctx.enter_context(nc.psum_tensor("pO1f", [128, 512], F32))
            pO2 = _ctx.enter_context(nc.psum_tensor("pO2f", [128, 512], F32))

            @block.gpsimd
            def _(g):
                for kc in range(DCH):
                    g.dma_start(out=wq_sb[:, kc, :], in_=wq[kc * 128:(kc + 1) * 128, :]).then_inc(ldw, 16)
                    g.dma_start(out=wkv_sb[:, kc, :], in_=wkv[kc * 128:(kc + 1) * 128, :]).then_inc(ldw, 16)
                for kc in range(2):
                    g.dma_start(out=wku_sb[:, kc, :], in_=wku[kc * 128:(kc + 1) * 128, :]).then_inc(ldw, 16)
                    g.dma_start(out=wvu_sb[:, kc, :], in_=wvu[kc * 128:(kc + 1) * 128, :]).then_inc(ldw, 16)
                g.dma_start(out=cmk[:], in_=cmask).then_inc(ldw, 16)
                g.dma_start(out=evenfl[:], in_=evenfl_in).then_inc(ldw, 16)
                g.dma_start(out=oddfl[:], in_=oddfl_in).then_inc(ldw, 16)
                g.dma_start(out=killc[:], in_=killc_in).then_inc(ldw, 16)
            cw_.inc((2 * DCH + 8) * 16)

            @block.vector
            def _(v):
                v.memset(onesall[:], 1.0).then_inc(vx, 1)
                v.memset(onesbf[:], 1.0).then_inc(vx, 1)
            cv.inc(2)

            for kc in range(DCH):
                for hf in range(2):
                    @block.sync
                    def _(e, kc=kc, hf=hf):
                        e.wait_ge(st, ct.v)
                        e.dma_start(out=x1s[:, 0, :],
                                    in_=wffn[kc * 128:(kc + 1) * 128, hf * 512:(hf + 1) * 512]).then_inc(ldx, 16)
                    cx.inc(16)

                    @block.scalar
                    def _(s, kc=kc, hf=hf):
                        s.wait_ge(ldx, cx.v)
                        s.wait_ge(st, ct.v)
                        s.activation(wsgb2[:, 0:512], x1s[:, 0, :], AF.Sign).then_inc(sx, 1)
                        s.wait_ge(sx, cs.v + 1)
                        s.dma_start(out=wsgn_scr.ap()[kc * 128:(kc + 1) * 128, hf * 512:(hf + 1) * 512],
                                    in_=wsgb2[:, 0:512]).then_inc(st, 16)
                    cs.inc(1)
                    ct.inc(16)
            st_wsgn = ct.v

            BKF = (pS1, pS2, pO1, pO2)
            evacF = {0: cv.v, 1: cv.v, 2: cv.v, 3: cv.v}
            fctr = [0]
            for tb in range(8):
                slot = tb // 4
                col0 = (tb % 4) * 512

                @block.sync
                def _(e, tb=tb, slot=slot, col0=col0, first=(tb == 0)):
                    if first:
                        e.wait_ge(cc3, 1)
                    e.wait_ge(mm, cm.v)
                    for kc in range(DCH):
                        e.dma_start(out=xstr[:, kc, :],
                                    in_=xn1_sh.ap()[slot * 1024 + kc * 128:slot * 1024 + (kc + 1) * 128,
                                                    col0:col0 + 512]).then_inc(ldx, 16)
                cx.inc(DCH * 16)

                for fc in range(2):
                    bank = fctr[0] % 4
                    fctr[0] += 1
                    ps = BKF[bank]

                    @block.tensor
                    def _(t, tb=tb, fc=fc, ps=ps, first=(tb == 0 and fc == 0), w_ev=evacF[bank]):
                        t.wait_ge(ldx, cx.v)
                        if first:
                            t.wait_ge(ldw, cw_.v)
                        t.wait_ge(vx, w_ev)
                        for kc in range(DCH):
                            mi = t.matmul(ps[:, 0:512], wkv_sb[:, kc, fc * 128:(fc + 1) * 128],
                                          xstr[:, kc, :], start=(kc == 0), stop=(kc == DCH - 1))
                        mi.then_inc(mm, 1)
                    cm.inc()

                    @block.vector
                    def _(v, tb=tb, fc=fc, ps=ps):
                        v.wait_ge(mm, cm.v)
                        v.tensor_copy(latr[:, fc, tb * 512:(tb + 1) * 512], ps[:, 0:512]).then_inc(vx, 1)
                    cv.inc()
                    evacF[bank] = cv.v

            for h in range(16):
                base, ch = (h % 2) * 64, h // 2
                for tb in range(8):
                    bank = fctr[0] % 4
                    fctr[0] += 1
                    ps = BKF[bank]

                    @block.tensor
                    def _(t, h=h, tb=tb, ps=ps, base=base, w_ev=evacF[bank]):
                        t.wait_ge(vx, w_ev)
                        for kc in range(2):
                            mi = t.matmul(ps[base:base + 64, 0:512],
                                          wku_sb[:, kc, h * 64:(h + 1) * 64],
                                          latr[:, kc, tb * 512:(tb + 1) * 512],
                                          start=(kc == 0), stop=(kc == 1))
                        mi.then_inc(mm, 1)
                    cm.inc()

                    @block.vector
                    def _(v, h=h, tb=tb, ps=ps, base=base, ch=ch):
                        v.wait_ge(mm, cm.v)
                        v.tensor_copy(kbf[base:base + 64, ch, tb * 512:(tb + 1) * 512],
                                      ps[base:base + 64, 0:512]).then_inc(vx, 1)
                    cv.inc()
                    evacF[bank] = cv.v

            for kt in range(32):
                for half in range(2):
                    bank = fctr[0] % 4
                    fctr[0] += 1
                    ps = BKF[bank]

                    @block.tensor
                    def _(t, kt=kt, half=half, ps=ps, w_ev=evacF[bank]):
                        t.wait_ge(vx, w_ev)
                        for kc in range(2):
                            mi = t.matmul(ps[:, 0:512], latr[:, kc, kt * 128:(kt + 1) * 128],
                                          wvu_sb[:, kc, half * 512:(half + 1) * 512],
                                          start=(kc == 0), stop=(kc == 1))
                        mi.then_inc(mm, 1)
                    cm.inc()

                    @block.vector
                    def _(v, kt=kt, half=half, ps=ps):
                        v.wait_ge(mm, cm.v)
                        v.tensor_copy(vbf[:, kt, half * 512:(half + 1) * 512], ps[:, 0:512]).then_inc(vx, 1)
                    cv.inc()
                    evacF[bank] = cv.v

            carry = dict(evac=[cv.v, cv.v], exp=[0, 0, 0, 0], av=[0, 0, 0, 0], norm=cv.v, pOmul=[0, 0])
            for qb in range(4):
                ktn = 20 + 4 * qb

                @block.sync
                def _(e, qb=qb):
                    e.wait_ge(mm, cm.v)
                    for kc in range(DCH):
                        e.dma_start(out=xstr[:, kc, :],
                                    in_=xn1_mine.ap()[kc, :, qb * 512:(qb + 1) * 512]).then_inc(ldx, 16)
                cx.inc(DCH * 16)

                for h in range(16):
                    base, ch = (h % 2) * 64, h // 2
                    ps = pS1 if h % 2 == 0 else pS2

                    @block.tensor
                    def _(t, h=h, ps=ps, base=base):
                        t.wait_ge(ldx, cx.v)
                        t.wait_ge(vx, cv.v)
                        for kc in range(DCH):
                            mi = t.matmul(ps[base:base + 64, 0:512], wq_sb[:, kc, h * 64:(h + 1) * 64],
                                          xstr[:, kc, :], start=(kc == 0), stop=(kc == DCH - 1))
                        mi.then_inc(mm, 1)
                    cm.inc()

                    @block.vector
                    def _(v, ps=ps, base=base, ch=ch):
                        v.wait_ge(mm, cm.v)
                        v.tensor_copy(qvv[base:base + 64, ch, :], ps[base:base + 64, 0:512]).then_inc(vx, 1)
                    cv.inc()
                carry["evac"] = [cv.v, cv.v]

                for hg in range(4):
                    tiles = [(kt, hh) for kt in range(ktn) for hh in range(4)]
                    n_t = len(tiles)
                    qk_cm, av_cm, exp_cs, evac_cv = {}, {}, {}, {}

                    def emit_qk(i, hg=hg):
                        kt, hh = tiles[i]
                        h = hg * 4 + hh
                        base, ch = (h % 2) * 64, h // 2
                        psS = pS1 if i % 2 == 0 else pS2
                        w_ev = evac_cv[i - 2] if i >= 2 else carry["evac"][i % 2]

                        @block.tensor
                        def _(t, kt=kt, base=base, ch=ch, psS=psS, w_ev=w_ev):
                            t.wait_ge(vx, w_ev)
                            mi = t.matmul(psS[:, 0:512],
                                          kbf[base:base + 64, ch, kt * 128:(kt + 1) * 128],
                                          qvv[base:base + 64, ch, :], start=True, stop=True)
                            mi.then_inc(mm, 1)
                        cm.inc()
                        qk_cm[i] = cm.v

                    def emit_evac(i, qb=qb):
                        kt, hh = tiles[i]
                        d = kt - 4 * qb
                        psS = pS1 if i % 2 == 0 else pS2
                        sl = i % 4
                        w_exp = exp_cs[i - 4] if i >= 4 else carry["exp"][i % 4]

                        @block.vector
                        def _(v, d=d, psS=psS, sl=sl, w_qk=qk_cm[i], w_exp=w_exp):
                            v.wait_ge(mm, w_qk)
                            v.wait_ge(sx, w_exp)
                            if 0 <= d < 4:
                                v.scalar_tensor_tensor(sexp[:, sl, :], cmk[:, d, :], evenfl[:, 0:1],
                                                       psS[:, 0:512], ALU.mult, ALU.add).then_inc(vx, 1)
                            elif 16 <= d < 20:
                                v.scalar_tensor_tensor(sexp[:, sl, :], cmk[:, d - 16, :], oddfl[:, 0:1],
                                                       psS[:, 0:512], ALU.mult, ALU.add).then_inc(vx, 1)
                                v.wait_ge(vx, cv.v + 1)
                                v.tensor_scalar(sexp[:, sl, :], sexp[:, sl, :], killc[:, 0:1],
                                                None, ALU.add).then_inc(vx, 1)
                            elif d >= 4:
                                v.tensor_scalar(sexp[:, sl, :], psS[:, 0:512], killc[:, 0:1],
                                                None, ALU.add).then_inc(vx, 1)
                            else:
                                v.tensor_copy(sexp[:, sl, :], psS[:, 0:512]).then_inc(vx, 1)
                        cv.inc(2 if 16 <= d < 20 else 1)
                        evac_cv[i] = cv.v

                    def emit_exp_pair(i):
                        s0_ = 2 * ((i // 2) % 2)
                        w_av = av_cm[i - 4] if i >= 4 else carry["av"][i % 4]

                        @block.scalar
                        def _(s, s0_=s0_, w_ev=evac_cv[i], w_av=w_av):
                            s.wait_ge(vx, w_ev)
                            s.wait_ge(mm, w_av)
                            s.activation(pexp[:, s0_:s0_ + 2, :].rearrange("p a b -> p (a b)"),
                                         sexp[:, s0_:s0_ + 2, :].rearrange("p a b -> p (a b)"), AF.Exp,
                                         scale=0.125).then_inc(sx, 1)
                        cs.inc()
                        exp_cs[i - 1] = cs.v
                        exp_cs[i] = cs.v

                    def emit_av(i, hg=hg):
                        kt, hh = tiles[i]
                        h = hg * 4 + hh
                        psAV = pAV1 if hh < 2 else pAV2
                        psDN = pDN1 if hh < 2 else pDN2
                        gbase = (hh % 2) * 64
                        first = (kt == 0)
                        last = (kt == ktn - 1)
                        sl = i % 4

                        @block.tensor
                        def _(t, kt=kt, h=h, hh=hh, sl=sl, psAV=psAV, psDN=psDN,
                              gbase=gbase, first=first, last=last, w_exp=exp_cs[i],
                              w_norm=carry["norm"]):
                            t.wait_ge(sx, w_exp)
                            if first:
                                t.wait_ge(vx, w_norm)
                            mi = t.matmul(psAV[gbase:gbase + 64, 0:512],
                                          vbf[:, kt, h * 64:(h + 1) * 64], pexp[:, sl, :],
                                          start=(first and hh % 2 == 0), stop=last)
                            mi.then_inc(mm, 1)
                            mi = t.matmul(psDN[gbase:gbase + 1, 0:512],
                                          onesbf[:], pexp[:, sl, :],
                                          start=(first and hh % 2 == 0), stop=last)
                            mi.then_inc(mm, 1)
                        cm.inc(2)
                        av_cm[i] = cm.v

                    for i in range(n_t):
                        emit_qk(i)
                        emit_evac(i)
                        if i % 2 == 1:
                            emit_exp_pair(i)
                        if i >= 3:
                            emit_av(i - 3)
                    emit_av(n_t - 3)
                    emit_av(n_t - 2)
                    emit_av(n_t - 1)
                    carry["evac"] = [evac_cv[n_t - 2], evac_cv[n_t - 1]]
                    carry["exp"] = [exp_cs[n_t - 4 + j] for j in range(4)]
                    carry["av"] = [av_cm[n_t - 4 + j] for j in range(4)]

                    avdone = cm.v
                    dn_cv = {}
                    bcast_cm = {}
                    mul_cv = {}
                    for hh in range(4):
                        if hh < 2:
                            psDN = pDN1
                            w_pre = avdone
                        else:
                            psDN = pDN2
                            w_pre = bcast_cm[hh - 2]
                        gb = (hh % 2) * 64

                        @block.vector
                        def _(v, psDN=psDN, gb=gb, avdone=avdone, w_pre=w_pre):
                            v.wait_ge(mm, avdone)
                            v.wait_ge(mm, w_pre)
                            v.tensor_copy(dnsb[gb:gb + 1, :], psDN[gb:gb + 1, 0:512]).then_inc(vx, 1)
                        cv.inc()
                        dn_cv[hh] = cv.v
                        h = hg * 4 + hh
                        base, ch = (h % 2) * 64, h // 2
                        psAV = pAV1 if hh < 2 else pAV2
                        pO = pO1 if hh % 2 == 0 else pO2
                        w_mul = mul_cv[hh - 2] if hh >= 2 else carry["pOmul"][hh]

                        @block.scalar
                        def _(s, gb=gb, w_dn=dn_cv[hh]):
                            s.wait_ge(vx, w_dn)
                            s.activation(dnsb[gb:gb + 1, :], dnsb[gb:gb + 1, :], AF.Ln).then_inc(sx, 1)
                            s.wait_ge(sx, cs.v + 1)
                            s.activation(dnsb[gb:gb + 1, :], dnsb[gb:gb + 1, :], AF.Exp,
                                         scale=-1.0).then_inc(sx, 1)
                        cs.inc(2)
                        rec_cs = cs.v

                        @block.tensor
                        def _(t, gb=gb, base=base, pO=pO, rec_cs=rec_cs, w_mul=w_mul):
                            t.wait_ge(sx, rec_cs)
                            t.wait_ge(vx, w_mul)
                            mi = t.matmul(pO[base:base + 64, 0:512], onesall[gb:gb + 1, :],
                                          dnsb[gb:gb + 1, :], start=True, stop=True)
                            mi.then_inc(mm, 1)
                        cm.inc()
                        bcast_cm[hh] = cm.v

                        @block.vector
                        def _(v, h=h, hh=hh, base=base, ch=ch, gb=gb, psAV=psAV, pO=pO,
                              w_b=cm.v):
                            v.wait_ge(mm, w_b)
                            v.tensor_copy(recb[base:base + 64, :], pO[base:base + 64, 0:512]).then_inc(vx, 1)
                            v.wait_ge(vx, cv.v + 1)
                            v.tensor_mul(attn[base:base + 64, ch, :], psAV[gb:gb + 64, 0:512],
                                         recb[base:base + 64, :]).then_inc(vx, 1)
                        cv.inc(2)
                        mul_cv[hh] = cv.v
                    carry["norm"] = cv.v
                    carry["pOmul"] = [mul_cv[2], mul_cv[3]]

                @block.gpsimd
                def _(g, qb=qb):
                    g.wait_ge(mm, cm.v)
                    for kc in range(DCH):
                        g.dma_start(out=wo_sb[:, kc, :], in_=wout[kc * 128:(kc + 1) * 128, :]).then_inc(ldw, 16)
                cw_.inc(DCH * 16)

                @block.sync
                def _(e, qb=qb):
                    e.wait_ge(mm, cm.v)
                    e.wait_ge(st, st_wsgn)
                    for kc in range(DCH):
                        e.dma_start(out=wsg_sb[:, kc, :],
                                    in_=wsgn_scr.ap()[kc * 128:(kc + 1) * 128, :]).then_inc(ldx, 16)
                cx.inc(DCH * 16)

                for dm in range(DCH):
                    ps = pO1 if dm % 2 == 0 else pO2
                    sl = dm % 2

                    @block.sync
                    def _(e, qb=qb, dm=dm, sl=sl):
                        e.wait_ge(vx, cv.v)
                        e.dma_start(out=x1s[:, 0, :],
                                    in_=x1_s.ap()[dm, :, qb * 512:(qb + 1) * 512]).then_inc(ldx, 16)
                    cx.inc(16)

                    @block.tensor
                    def _(t, dm=dm, ps=ps, first=(dm == 0)):
                        if first:
                            t.wait_ge(ldw, cw_.v)
                            t.wait_ge(ldx, cx.v)
                            t.wait_ge(vx, cv.v)
                        t.wait_ge(vx, cv.v)
                        for kc in range(DCH):
                            mi = t.matmul(ps[:, 0:512], wo_sb[:, kc, dm * 128:(dm + 1) * 128],
                                          attn[:, kc, :], start=(kc == 0), stop=False)
                        for kc in range(DCH):
                            mi = t.matmul(ps[:, 0:512], wsg_sb[:, kc, dm * 128:(dm + 1) * 128],
                                          xstr[:, kc, :], start=False, stop=(kc == DCH - 1))
                        mi.then_inc(mm, 1)
                    cm.inc()

                    @block.vector
                    def _(v, dm=dm, ps=ps, sl=sl):
                        v.wait_ge(mm, cm.v)
                        v.wait_ge(ldx, cx.v)
                        v.wait_ge(st, ct.v)
                        v.tensor_add(fino[:, 0, :], ps[:, 0:512], x1s[:, 0, :]).then_inc(vx, 1)
                    cv.inc()

                    @block.scalar
                    def _(s, qb=qb, dm=dm, sl=sl):
                        s.wait_ge(vx, cv.v)
                        s.dma_start(out=fin_o[dm, :, qb * 512:(qb + 1) * 512], in_=fino[:, 0, :]).then_inc(st, 16)
                    ct.inc(16)
                carry["pOmul"] = [cv.v, cv.v]

        @block.sync
        def _(e):
            e.wait_ge(st, ct.v)

    return nc



_NC_CACHE = {}


def _get_nc():
    if "fused" not in _NC_CACHE:
        _NC_CACHE["fused"] = build_fused()
    return _NC_CACHE["fused"]


def kernel(**inputs):
    x = np.ascontiguousarray(inputs["L0_attn_x"].astype(np.float32, copy=False))
    B, L, Dm = x.shape

    w_in = np.ascontiguousarray(inputs["L0_attn_in_proj_w"].T)
    convw = np.ascontiguousarray(inputs["L0_attn_conv_w"][:, 0, :])
    convb = np.ascontiguousarray(inputs["L0_attn_conv_b"].reshape(DIN, 1))
    w_xp = np.ascontiguousarray(inputs["L0_attn_x_proj_w"].T[:, 0:192])
    w_dt = np.ascontiguousarray(inputs["L0_attn_dt_proj_w"].T)
    dt_b = np.ascontiguousarray(inputs["L0_attn_dt_proj_b"].reshape(128, 1))
    a_log = np.ascontiguousarray(inputs["L0_attn_A_log"].reshape(128, 1))
    w_gate = np.ascontiguousarray(inputs["L0_ffn_gate_w"])
    w_sgu = np.ascontiguousarray(inputs["L0_ffn_shared_gate_up_w"])
    w_sd = np.ascontiguousarray(inputs["L0_ffn_shared_down_w"])
    w_sg = np.ascontiguousarray(inputs["L0_ffn_shared_gate_w"])
    cprojT = np.ascontiguousarray(inputs["L0_attn_C_proj_w"].T.astype(np.float32, copy=False))
    dvec = np.ascontiguousarray(inputs["L0_attn_D"].reshape(DIN, 1).astype(np.float32, copy=False))
    wop = np.ascontiguousarray(inputs["L0_attn_out_proj_w"].T.astype(np.float32, copy=False))
    utri = np.triu(np.ones((128, 128), np.float32), 1)
    iota64 = np.tile(np.arange(NE, dtype=np.float32), (128, 1))
    xfull = np.ascontiguousarray(x.reshape(B * L, Dm))
    ident = np.eye(128, dtype=np.float32)
    egu = inputs["L0_ffn_expert_gate_up_w"].astype(np.float32, copy=False)
    edw = inputs["L0_ffn_expert_down_w"].astype(np.float32, copy=False)
    wq_h = np.ascontiguousarray(inputs["L1_attn_Wq"].reshape(Dm, 1024).astype(np.float32, copy=False))
    wkv_h = np.ascontiguousarray(inputs["L1_attn_Wkv_down"].astype(np.float32, copy=False))
    wku_h = np.ascontiguousarray(inputs["L1_attn_Wk_up"].reshape(256, 1024).astype(np.float32, copy=False))
    wvu_h = np.ascontiguousarray(inputs["L1_attn_Wv_up"].reshape(256, 1024).astype(np.float32, copy=False))
    wout_h = np.ascontiguousarray(inputs["L1_attn_Wout"].reshape(1024, Dm).astype(np.float32, copy=False))
    wffn_h = np.ascontiguousarray(inputs["L1_ffn_weight"].T.astype(np.float32, copy=False))
    qq = np.arange(512)[None, :]
    kk = np.arange(128)[:, None]
    cmask = np.ascontiguousarray(np.stack(
        [np.where(qq - kk >= 128 * j, 0.0, -1e30).astype(np.float32) for j in range(4)], axis=1))
    p128 = np.arange(128, dtype=np.int32)

    maps = []
    for c in range(NCORES):
        b, h = c // 2, c % 2
        s0 = h * TPC
        my0 = c * TPC
        xb = x[b]
        xTh = np.zeros((Dm, TPC + HALO), np.float32)
        lo = s0 - HALO
        src0 = max(lo, 0)
        xTh[:, HALO - (s0 - src0):] = xb[src0:s0 + TPC].T
        xTv = np.ascontiguousarray(xb[s0:s0 + TPC].T)
        prefv = np.zeros((8, 1), np.float32)
        prefv[:c] = 1.0
        ptid = (64 * np.arange(8, dtype=np.int32)[None, :] + 8 * c
                + np.arange(8, dtype=np.int32)[:, None]).astype(np.int32)
        xsix = (h * 1024 + 128 * np.arange(DCH, dtype=np.int32)[None, :]
                + p128[:, None]).astype(np.int32)
        maps.append(dict(
            xTh=np.ascontiguousarray(xTh), w_in=w_in, convw=convw, convb=convb,
            w_xp=w_xp, w_dt=w_dt, dt_b=dt_b, a_log=a_log, w_gate=w_gate,
            w_sgu=w_sgu, w_sd=w_sd, w_sg=w_sg,
            hlidx_in=(h * 128 + p128).reshape(128, 1).astype(np.int32),
            cprojT=cprojT, dvec=dvec, wop=wop, utri=utri, iota64=iota64,
            tokid=(np.arange(128, dtype=np.float32) + my0).reshape(128, 1),
            prefvec=prefv,
            oddcol=np.full((128, 1), float(h), np.float32),
            xfull=xfull,
            egu8=np.ascontiguousarray(egu[c * 8:(c + 1) * 8]),
            edw8=np.ascontiguousarray(edw[c * 8:(c + 1) * 8]),
            ident=ident, ptidx=ptid,
            xTv=xTv, xsidx=xsix,
            wq=wq_h, wkv=wkv_h, wku=wku_h, wvu=wvu_h, wout=wout_h, wffn=wffn_h,
            cmask=cmask,
            evenfl_in=np.full((128, 1), 1.0 - h, np.float32),
            oddfl_in=np.full((128, 1), float(h), np.float32),
            killc_in=np.full((128, 1), -1e30 if h == 0 else 0.0, np.float32),
        ))

    nc = _get_nc()
    res = run_bass_kernel_spmd(nc, maps, core_ids=list(range(NCORES))).results

    out = np.empty((B, L, Dm), np.float32)
    for c in range(NCORES):
        b, h = c // 2, c % 2
        s0 = h * TPC
        out[b, s0:s0 + TPC, :] = res[c]["fin_o"].reshape(Dm, TPC).T
    return out
